# version 05
# baseline (speedup 1.0000x reference)
import math
import numpy as np
import ml_dtypes
import concourse.bass as bass
import concourse.mybir as mybir
from concourse.bass_utils import run_bass_kernel_spmd

F32 = mybir.dt.float32
BF16 = mybir.dt.bfloat16
I16 = mybir.dt.int16
AF = mybir.ActivationFunctionType
OP = mybir.AluOpType
AX = mybir.AxisListType.X

B, S, D, H = 4, 8192, 1024, 4096
NSTATE = 64
EPS = 1e-5
NCORE = 8
NTOK = S // 2
FC = D // 128
HC = H // 128
TILE = 456
TILES = [456] * 8 + [448]
TWO_PI = 2.0 * math.pi
QG = 2
DBG = False
EARLY_STOP = False


def build_nc():
    assert sum(TILES) == NTOK
    NT = len(TILES)
    TO = [sum(TILES[:i]) for i in range(NT)]
    NCOL = NTOK + 2
    nc = bass.Bass("TRN2", target_bir_lowering=False,
                   detect_race_conditions=False)

    xmain = nc.dram_tensor("xmain", [D, NCOL], F32, kind="ExternalInput").ap()
    xpre = nc.dram_tensor("xpre", [D, NTOK], F32, kind="ExternalInput").ap()
    tpos = nc.dram_tensor("tpos", [NSTATE, NCOL], F32, kind="ExternalInput").ap()
    maskr = nc.dram_tensor("maskr", [128, 1], F32, kind="ExternalInput").ap()
    lar_d = nc.dram_tensor("lar", [NSTATE, 1], F32, kind="ExternalInput").ap()
    lai_d = nc.dram_tensor("lai", [NSTATE, 1], F32, kind="ExternalInput").ap()
    bwT = nc.dram_tensor("bwT", [D, 128], F32, kind="ExternalInput").ap()
    cwT = nc.dram_tensor("cwT", [128, D], F32, kind="ExternalInput").ap()
    dwT = nc.dram_tensor("dwT", [D, D], F32, kind="ExternalInput").ap()
    w1T = nc.dram_tensor("w1T", [D, H], F32, kind="ExternalInput").ap()
    w2T = nc.dram_tensor("w2T", [H, D], F32, kind="ExternalInput").ap()
    cvw = nc.dram_tensor("cvw", [128, 128], F32, kind="ExternalInput").ap()
    idn_d = nc.dram_tensor("idn", [128, 128], BF16, kind="ExternalInput").ap()
    idnf_d = nc.dram_tensor("idnf32", [128, 128], F32, kind="ExternalInput").ap()
    out_d = nc.dram_tensor("out", [D, NTOK], F32, kind="ExternalOutput").ap()
    dbg_kind = "ExternalOutput" if DBG else "Internal"
    c1_d = nc.dram_tensor("c1scr", [FC, 128, H], BF16, kind=dbg_kind).ap()
    c2_d = nc.dram_tensor("c2scr", [HC, 128, D], BF16, kind=dbg_kind).ap()

    A = nc.alloc_sbuf_tensor

    costab = A("costab", [64, 2 * NCOL], BF16)
    pq = A("pq", [64, 2 * NTOK + 8], BF16)
    rho1 = A("rho1", [64, 1], F32)
    bstre = A("bstre", [128, FC * 64], BF16)
    bstim = A("bstim", [128, FC * 64], BF16)
    cstre = A("cstre", [64, FC * 128], BF16)
    cstim = A("cstim", [64, FC * 128], BF16)
    convw = A("convw", [128, 128], F32)
    mall = A("mall", [1, NTOK], BF16)
    rball = A("rball", [128, NTOK], BF16)
    onesb = A("onesb", [128, 1], BF16)
    onesr = A("onesr", [1, 128], BF16)
    negbsre = A("negbsre", [1, 64], BF16)
    negbsim = A("negbsim", [1, 64], BF16)
    negdsum = A("negdsum", [1, D], BF16)
    c2col = A("c2col", [128, 8], F32)
    thin = A("thin", [1, 3 * TILE + 16], F32)
    thinb = A("thinb", [1, 2 * TILE], BF16)
    th64 = A("th64", [64, 16], F32)
    sc128 = A("sc128", [128, 8], F32)
    maskc = A("maskc", [128, 1], F32)
    epsc = A("epsc", [128, 1], F32)
    onesf = A("onesf", [128, 1], F32)
    onesrf = A("onesrf", [1, 128], F32)
    idnb = A("idnb", [128, 128], BF16)
    idnf = A("idnf", [128, 128], F32)
    rmax = A("rmax", [128, 512], BF16)
    mstr = A("mstr", [128, 4], BF16)
    nbuf = A("nbuf", [128, 512], BF16)
    c2mc = A("c2mc", [128, 1], F32)
    w1buf = A("w1buf", [128, 2 * 1024], BF16)
    w2buf = A("w2buf", [128, 2 * 1024], BF16)
    dwbuf = A("dwbuf", [128, 2 * 1024], BF16)
    xsb = A("xsb", [128, FC * TILE], F32)
    xb = A("xb", [128, FC * TILE], BF16)
    sq = A("sq", [128, FC * TILE], BF16)
    ybf = A("ybf", [128, FC * TILE], BF16)
    x1 = A("x1", [128, 2 * FC * TILE], F32)
    xqw = A("xqw", [128, 2 * FC * (TILE + 2)], BF16)
    hsb = A("hsb", [64, 2 * TILE], BF16)
    hsilu = A("hsilu", [128, HC * TILE], BF16)
    qscr = A("qscr", [128, TILE], F32)
    k16f = A("k16f", [128, TILE], I16)
    scrA = A("scrA", [128, TILE], F32)
    scrB = A("scrB", [128, TILE], F32)
    scrC = A("scrC", [64, TILE], F32)
    cvt1 = A("cvt1", [128, TILE + 2], BF16)
    cvt2 = A("cvt2", [128, TILE + 2], BF16)
    cvt3 = A("cvt3", [128, TILE + 2], BF16)
    bcb = A("bcb", [128, 6 * TILE], BF16)
    haloq = A("haloq", [128, 2 * FC], BF16)
    halos = A("halos", [128, 48], F32)
    vpad = A("vpad", [1, 64], F32)
    if DBG:
        snapA = A("snapA", [64, 16], F32)
        snapB = A("snapB", [64, 16], F32)
        snapU = A("snapU", [64, 64], F32)
        snapC = A("snapC", [64, 256], BF16)
        snapR = A("snapR", [64, 1], F32)
        snapP = A("snapP", [64, 160], F32)

    rem = nc.sbuf_bytes_remaining
    if DBG:
        print("sbuf bytes remaining:", rem)
    assert rem >= 0, f"SBUF over budget by {-rem}"

    k16d = sq.ap().bitcast(I16)
    outb = hsilu.ap().bitcast(F32)
    wqbuf = hsilu.ap().bitcast(F32)
    tposf = costab.ap().bitcast(F32)
    x1f = x1.ap()
    tk16 = pq.ap().bitcast(I16)
    tcor = pq.ap()

    P = nc.alloc_psum_tensor
    ps_h = [P("ps_h0", [128, TILE + 2], F32), P("ps_h1", [128, TILE + 2], F32)]
    ps_y = [P("ps_y0", [128, TILE], F32), P("ps_y1", [128, TILE], F32)]
    ps_D = P("ps_D", [128, TILE], F32)
    ps_C = P("ps_C", [128, TILE], F32)
    ps_S = P("ps_S", [128, TILE], F32)
    ps_X = P("ps_X", [128, TILE], F32)

    ctr = {k: 0 for k in ("t", "v", "s", "g", "dx", "dw", "do")}
    plan = {e: [] for e in ("tensor", "vector", "scalar", "gpsimd", "sync")}

    def op(engine, fn, waits=(), inc=None):
        flat = []
        for w_ in waits:
            if isinstance(w_, list):
                flat.extend(w_)
            elif w_ is not None:
                flat.append(w_)
        plan[engine].append((flat, fn, inc))
        if inc is not None:
            ctr[inc] += 16 if inc.startswith("d") else 1

    def ev(name):
        return (name, ctr[name])

    def do_emit():
        with (
            nc.semaphore("sem_t") as s_t, nc.semaphore("sem_v") as s_v,
            nc.semaphore("sem_s") as s_s, nc.semaphore("sem_g") as s_g,
            nc.semaphore("sem_dx") as s_dx, nc.semaphore("sem_dw") as s_dw,
            nc.semaphore("sem_do") as s_do,
            nc.Block() as block,
        ):
            sem_handles = dict(t=s_t, v=s_v, s=s_s, g=s_g, dx=s_dx, dw=s_dw,
                               do=s_do)

            def run_engine(engname):
                def body(eng):
                    for waits, fn, inc in plan[engname]:
                        seen = {}
                        for semn, val in waits:
                            if val > 0 and seen.get(semn, -1) < val:
                                seen[semn] = val
                        for semn, val in seen.items():
                            eng.wait_ge(sem_handles[semn], val)
                        inst = fn(eng)
                        if inc is not None:
                            step = 16 if inc.startswith("d") else 1
                            assert inst is not None
                            inst.then_inc(sem_handles[inc], step)
                return body

            block.tensor(run_engine("tensor"))
            block.vector(run_engine("vector"))
            block.scalar(run_engine("scalar"))
            block.gpsimd(run_engine("gpsimd"))
            block.sync(run_engine("sync"))

    class VP:

        def __init__(self, v):
            self._v = v

        def __getattr__(self, name):
            real = getattr(self._v, name)

            def wrap(*a, **k):
                inst = real(*a, **k)
                self._v.tensor_single_scalar(vpad[0:1, :], vpad[0:1, :],
                                             0.0, OP.add)
                return inst
            return wrap

    W = {}
    mini_k = [0]

    def mini_scan(*deps):
        if not DBG:
            return
        k = mini_k[0]
        mini_k[0] += 1

        def f_ms_(v, k_=k):
            vv = VP(v)
            vv.tensor_copy(snapP[:, k_ * 16:k_ * 16 + 8],
                           rho1[:, :].broadcast_to([64, 8]))
            return vv.tensor_tensor_scan(
                snapP[:, k_ * 16 + 8:k_ * 16 + 16],
                snapP[:, k_ * 16:k_ * 16 + 8],
                onesf[0:64, :].broadcast_to([64, 8]), 0.0, OP.mult, OP.add)
        op("vector", f_ms_, waits=list(deps), inc="v")

    lastS = [None]
    lastX = [None]

    def slot(i, w=TILE):
        return thin[:, i * TILE: i * TILE + w]

    def bc(i, w=TILE):
        return bcb[:, i * TILE: i * TILE + w]

    def fcs(buf, fc, w, grid=TILE):
        return buf[:, fc * grid: fc * grid + w]

    def emit_stats(w, mov, movsq, tag, mov_ev, sq_ev):
        def f_mm1(t):
            for fc in range(FC):
                last = t.matmul(ps_S[0:1, 0:w], onesb[:, :], mov(fc),
                                start=(fc == 0), stop=(fc == FC - 1))
            return last
        op("tensor", f_mm1, waits=[mov_ev, lastS[0]], inc="t")
        W[f"{tag}_sx"] = ev("t")
        op("scalar", (lambda w_: lambda s: s.copy(slot(0, w_), ps_S[0:1, 0:w_]))(w),
           waits=[W[f"{tag}_sx"]], inc="s")
        W[f"{tag}_sxe"] = ev("s")

        def f_mm2(t):
            for fc in range(FC):
                last = t.matmul(ps_S[0:1, 0:w], onesb[:, :], movsq(fc),
                                start=(fc == 0), stop=(fc == FC - 1))
            return last
        op("tensor", f_mm2, waits=[W[f"{tag}_sxe"], sq_ev], inc="t")
        W[f"{tag}_sq"] = ev("t")
        op("scalar", (lambda w_: lambda s: s.copy(slot(1, w_), ps_S[0:1, 0:w_]))(w),
           waits=[W[f"{tag}_sq"]], inc="s")
        W[f"{tag}_sqe"] = ev("s")
        lastS[0] = W[f"{tag}_sqe"]

        def f_thin(v, w_=w):
            m, q, r = slot(0, w_), slot(1, w_), slot(2, w_)
            v.tensor_single_scalar(m, m, 1.0 / D, OP.mult)
            v.tensor_single_scalar(q, q, 1.0 / D, OP.mult)
            v.tensor_tensor(r, m, m, OP.mult)
            return v.tensor_tensor(q, q, r, OP.subtract)
        op("vector", f_thin, waits=[W[f"{tag}_sqe"]], inc="v")
        W[f"{tag}_var"] = ev("v")

        def f_r(s, w_=w):
            q = slot(1, w_)
            s.activation(q, q, AF.Ln, bias=epsc[0:1, :])
            return s.activation(slot(2, w_), q, AF.Exp, scale=-0.5)
        op("scalar", f_r, waits=[W[f"{tag}_var"]], inc="s")
        W[f"{tag}_r"] = ev("s")

    def emit_bcast(w, sl, dstf, tag, deps=()):
        def f_cp(v, w_=w, sl_=sl):
            return v.tensor_copy(thinb[0:1, TILE:TILE + w_], slot(sl_, w_))
        op("vector", f_cp, waits=list(deps) + [lastX[0]], inc="v")
        W[f"{tag}_bm"] = ev("v")
        op("tensor", (lambda w_: lambda t: t.matmul(
            ps_X[:, 0:w_], onesr[:, :], thinb[0:1, TILE:TILE + w_],
            start=True, stop=True))(w),
            waits=[W[f"{tag}_bm"], lastX[0]], inc="t")
        W[f"{tag}_bo"] = ev("t")
        op("scalar", (lambda w_: lambda s: s.copy(dstf(), ps_X[:, 0:w_]))(w),
           waits=[W[f"{tag}_bo"]], inc="s")
        W[f"{tag}_bc"] = ev("s")
        lastX[0] = W[f"{tag}_bc"]

    def emit_pmax_bcast(w, tag, deps):
        nblk = (w + 127) // 128

        psb = ps_X[:, 0:64].bitcast(BF16)

        def f_fwd(t, b):
            return t.transpose(psb, rmax[:, b * 128:(b + 1) * 128],
                               idnb[:, :])
        for b in range(nblk):
            op("tensor", (lambda b_: lambda t: f_fwd(t, b_))(b),
               waits=list(deps) + [lastX[0], W["p0ld"]], inc="t")
            W[f"{tag}_f{b}"] = ev("t")
            op("vector", (lambda b_: lambda v: v.tensor_reduce(
                mstr[:, b_:b_ + 1], psb, AX, OP.max))(b),
                waits=[W[f"{tag}_f{b}"]], inc="v")
            W[f"{tag}_r{b}"] = ev("v")
            lastX[0] = W[f"{tag}_r{b}"]

        def f_nexp(v):
            return v.tensor_copy(
                nbuf[:, 0:nblk * 128].rearrange("p (b r) -> p b r", b=nblk),
                mstr[:, 0:nblk].unsqueeze(2).broadcast_to([128, nblk, 128]))
        op("vector", f_nexp, waits=[W[f"{tag}_r{nblk-1}"]], inc="v")
        W[f"{tag}_n"] = ev("v")

        for b in range(nblk):
            op("tensor", (lambda b_: lambda t: t.transpose(
                psb, nbuf[:, b_ * 128:(b_ + 1) * 128],
                idnb[:, :]))(b), waits=[W[f"{tag}_n"], lastX[0]], inc="t")
            W[f"{tag}_g{b}"] = ev("t")
            hi = min(128, w - b * 128)
            op("scalar", (lambda b_, h_: lambda s: s.copy(
                rmax[:, b_ * 128:b_ * 128 + h_], psb[:, 0:h_]))(b, hi),
                waits=[W[f"{tag}_g{b}"]], inc="s")
            W[f"{tag}_e{b}"] = ev("s")
            lastX[0] = W[f"{tag}_e{b}"]
        W[f"{tag}_done"] = W[f"{tag}_e{nblk-1}"]

    def f_ms(e):
        for t_ in (rmax, nbuf, mstr, hsilu, halos, scrA, scrB, scrC, cvt1,
                   cvt2, cvt3, bcb, ybf, sq, xb, qscr, k16f, x1, xqw, pq,
                   thin, thinb, sc128, c2col, haloq, th64, xsb, hsb, costab, vpad,
                   mall, rball, negdsum, w1buf, w2buf, dwbuf):
            last = e.memset(t_.ap() if hasattr(t_, "ap") else t_, 0)
        return last
    op("gpsimd", f_ms, inc="g")
    W["msall"] = ev("g")

    op("sync", lambda e: e.dma_start(tposf[:, 0:NCOL], tpos),
       waits=[W["msall"]], inc="dx")
    W["tposld"] = ev("dx")
    op("sync", lambda e: e.dma_start(th64[:, 0:1], lar_d),
       waits=[W["msall"]], inc="dx")
    op("sync", lambda e: e.dma_start(th64[:, 1:2], lai_d), inc="dx")
    op("sync", lambda e: e.dma_start(convw[:, :], cvw), inc="dx")
    op("sync", lambda e: e.dma_start(maskc[:, :], maskr), inc="dx")
    op("sync", lambda e: e.dma_start(idnb[:, :], idn_d), inc="dx")
    op("sync", lambda e: e.dma_start(idnf[:, :], idnf_d), inc="dx")
    W["p0ld"] = ev("dx")

    bre = bwT.rearrange("(kc p) n -> p kc n", p=128)
    for kc in range(FC):
        op("gpsimd", (lambda k_: lambda e: e.dma_start(
            bstre[:, k_ * 64:(k_ + 1) * 64], bre[:, k_, 0:64]))(kc),
            inc="dw")
        op("gpsimd", (lambda k_: lambda e: e.dma_start(
            bstim[:, k_ * 64:(k_ + 1) * 64], bre[:, k_, 64:128]))(kc),
            inc="dw")
    op("gpsimd", lambda e: e.dma_start(cstre[:, :], cwT[0:64, :]), inc="dw")
    op("gpsimd", lambda e: e.dma_start(cstim[:, :], cwT[64:128, :]), inc="dw")
    W["bcld"] = ev("dw")
    op("gpsimd", lambda e: e.memset(onesb[:, :], 1.0), inc="g")
    op("gpsimd", lambda e: e.memset(onesr[:, :], 1.0), inc="g")
    op("gpsimd", lambda e: e.memset(epsc[:, :], EPS), inc="g")
    op("gpsimd", lambda e: e.memset(onesf[:, :], 1.0), inc="g")
    op("gpsimd", lambda e: e.memset(onesrf[:, :], 1.0), inc="g")
    W["ones"] = ev("g")

    c = th64
    op("scalar", lambda s: s.activation(c[:, 2:3], c[:, 0:1], AF.Exp),
       waits=[W["p0ld"]], inc="s")
    W["p0exp"] = ev("s")

    def f_zden(v):
        v = VP(v)
        v.tensor_single_scalar(c[:, 2:3], c[:, 2:3], -0.5, OP.mult)
        v.tensor_single_scalar(c[:, 3:4], c[:, 1:2], 0.5, OP.mult)
        v.tensor_scalar(out=c[:, 4:5], in0=c[:, 2:3], scalar1=-1.0,
                        scalar2=1.0, op0=OP.mult, op1=OP.add)
        v.tensor_tensor(c[:, 4:5], c[:, 4:5], c[:, 4:5], OP.mult)
        v.tensor_tensor(c[:, 5:6], c[:, 3:4], c[:, 3:4], OP.mult)
        v.tensor_tensor(c[:, 4:5], c[:, 4:5], c[:, 5:6], OP.add)
        v.tensor_tensor(c[:, 6:7], c[:, 2:3], c[:, 2:3], OP.mult)
        v.tensor_tensor(c[:, 6:7], c[:, 6:7], c[:, 5:6], OP.add)
        return v.tensor_scalar(out=c[:, 6:7], in0=c[:, 6:7], scalar1=-1.0,
                               scalar2=1.0, op0=OP.mult, op1=OP.add)
    op("vector", f_zden, waits=[W["p0exp"]], inc="v")
    W["p0den"] = ev("v")

    def f_invden(s):
        s.activation(c[:, 4:5], c[:, 4:5], AF.Ln)
        return s.activation(c[:, 4:5], c[:, 4:5], AF.Exp, scale=-1.0)
    op("scalar", f_invden, waits=[W["p0den"]], inc="s")
    W["p0invden"] = ev("s")

    def f_A(v):
        v = VP(v)
        v.tensor_tensor(c[:, 6:7], c[:, 6:7], c[:, 4:5], OP.mult)
        v.tensor_single_scalar(c[:, 7:8], c[:, 3:4], 2.0, OP.mult)
        v.tensor_tensor(c[:, 7:8], c[:, 7:8], c[:, 4:5], OP.mult)
        v.tensor_tensor(c[:, 2:3], c[:, 6:7], c[:, 6:7], OP.mult)
        v.tensor_tensor(c[:, 5:6], c[:, 7:8], c[:, 7:8], OP.mult)
        return v.tensor_tensor(c[:, 5:6], c[:, 5:6], c[:, 2:3], OP.add)
    op("vector", f_A, waits=[W["p0invden"]], inc="v")
    W["p0A"] = ev("v")

    def f_Aclamp(v):
        v = VP(v)
        v.tensor_single_scalar(c[:, 5:6], c[:, 5:6], 1e-38, OP.max)
        v.tensor_single_scalar(c[:, 8:9], c[:, 6:7], -1.0, OP.mult)
        v.tensor_tensor(c[:, 8:9], c[:, 8:9], c[:, 6:7], OP.max)
        return v.tensor_single_scalar(c[:, 8:9], c[:, 8:9], 1e-38, OP.max)
    op("vector", f_Aclamp, waits=[W["p0A"]], inc="v")
    W["p0Ac"] = ev("v")

    def f_rho(s):
        s.activation(c[:, 2:3], c[:, 5:6], AF.Ln)
        s.activation(c[:, 5:6], c[:, 2:3], AF.Exp, scale=0.5)
        s.activation(c[:, 2:3], c[:, 8:9], AF.Ln)
        return s.activation(c[:, 2:3], c[:, 2:3], AF.Exp, scale=-1.0)
    op("scalar", f_rho, waits=[W["p0Ac"]], inc="s")
    W["p0rho"] = ev("s")

    def f_atanarg(v):
        v.tensor_copy(rho1[:, :], c[:, 5:6])
        return v.tensor_tensor(c[:, 2:3], c[:, 7:8], c[:, 2:3], OP.mult)
    op("vector", f_atanarg, waits=[W["p0rho"]], inc="v")
    W["p0atanarg"] = ev("v")

    def f_atan(s):
        s.activation(c[:, 3:4], c[:, 7:8], AF.Sign)
        return s.activation(c[:, 2:3], c[:, 2:3], AF.Arctan)
    op("scalar", f_atan, waits=[W["p0atanarg"]], inc="s")
    W["p0atan"] = ev("s")

    def f_theta(v):
        v = VP(v)
        v.tensor_single_scalar(c[:, 4:5], c[:, 6:7], 0.0, OP.is_lt)
        v.tensor_scalar(out=c[:, 6:7], in0=c[:, 4:5], scalar1=-2.0,
                        scalar2=1.0, op0=OP.mult, op1=OP.add)
        v.tensor_tensor(c[:, 6:7], c[:, 6:7], c[:, 2:3], OP.mult)
        v.tensor_tensor(c[:, 4:5], c[:, 4:5], c[:, 3:4], OP.mult)
        v.tensor_single_scalar(c[:, 4:5], c[:, 4:5], math.pi, OP.mult)
        return v.tensor_tensor(c[:, 6:7], c[:, 6:7], c[:, 4:5], OP.add)
    op("vector", f_theta, waits=[W["p0atan"]], inc="v")
    W["p0theta"] = ev("v")
    if DBG:
        op("vector", lambda v: v.tensor_copy(snapA[:, :], c[:, :]),
           waits=[W["p0theta"]], inc="v")
        op("vector", lambda v: v.tensor_copy(snapR[:, :], rho1[:, :]),
           waits=[W["p0theta"]], inc="v")

    def f_tab_u(v):
        tp = tposf[0:64, 0:NCOL]
        return v.tensor_scalar(out=tp, in0=tp, scalar1=c[:, 6:7],
                               scalar2=None, op0=OP.mult)
    op("vector", f_tab_u, waits=[W["p0theta"], W["tposld"]], inc="v")
    W["tab_u"] = ev("v")

    def f_args(v, phase, w_):
        u = tposf[0:64, 0:NCOL]
        kk = tk16[0:64, 4102:4102 + NCOL]
        cr = tcor[0:64, 0:NCOL]
        v.tensor_scalar(out=w_, in0=u, scalar1=1.0 / TWO_PI,
                        scalar2=2048.5 + phase, op0=OP.mult, op1=OP.add)
        v.tensor_single_scalar(kk, w_, 0.0, OP.add)
        v.scalar_tensor_tensor(out=cr, in0=kk, scalar=0.0, in1=w_,
                               op0=OP.add, op1=OP.is_gt)
        v.tensor_tensor(w_, w_, kk, OP.subtract)
        v.tensor_tensor(w_, w_, cr, OP.add)
        return v.tensor_scalar(out=w_, in0=w_, scalar1=TWO_PI,
                               scalar2=-math.pi, op0=OP.mult, op1=OP.add)

    wcos = x1f[0:64, 0:NCOL]
    wsin = tposf[0:64, 0:NCOL]
    op("vector", lambda v: f_args(v, 0.25, wcos), waits=[W["tab_u"]], inc="v")
    W["tabcos_arg"] = ev("v")
    op("vector", lambda v: f_args(v, 0.0, wsin), waits=[W["tabcos_arg"]],
       inc="v")
    W["tabsin_arg"] = ev("v")
    op("scalar", lambda s: s.activation(tcor[0:64, 0:NCOL], wsin, AF.Sin),
       waits=[W["tabsin_arg"]], inc="s")
    W["tabsin_b"] = ev("s")
    op("scalar", lambda s: s.activation(costab[0:64, 0:NCOL], wcos, AF.Sin),
       waits=[W["tabsin_b"]], inc="s")
    W["tabcos_done"] = ev("s")
    op("vector", lambda v: v.tensor_copy(costab[0:64, NCOL:2 * NCOL],
                                         tcor[0:64, 0:NCOL]),
       waits=[W["tabcos_done"]], inc="v")
    W["tables"] = ev("v")

    mini_scan(W["tables"])

    def f_negb_mm(t):
        for kc in range(FC):
            t.matmul(ps_X[0:1, 0:64], onesb[:, :],
                     bstre[:, kc * 64:(kc + 1) * 64],
                     start=(kc == 0), stop=(kc == FC - 1))
        for kc in range(FC):
            last = t.matmul(ps_X[0:1, 64:128], onesb[:, :],
                            bstim[:, kc * 64:(kc + 1) * 64],
                            start=(kc == 0), stop=(kc == FC - 1))
        return last
    op("tensor", f_negb_mm, waits=[W["bcld"], W["ones"]], inc="t")
    W["negb_mm"] = ev("t")

    def f_negb_ev(v):
        v.tensor_single_scalar(negbsre[:, :], ps_X[0:1, 0:64], -1.0, OP.mult)
        return v.tensor_single_scalar(negbsim[:, :], ps_X[0:1, 64:128],
                                      -1.0, OP.mult)
    op("vector", f_negb_ev, waits=[W["negb_mm"]], inc="v")
    W["negb"] = ev("v")
    lastX[0] = W["negb"]

    dre = dwT.rearrange("(kc p) e -> p kc e", p=128)

    def emit_dpiece(oc, bsel, waits):
        for kc in range(FC):
            op("gpsimd", (lambda o_, b_, k_: lambda e: e.dma_start(
                dwbuf[:, b_ * 1024 + k_ * 128:b_ * 1024 + k_ * 128 + 128],
                dre[:, k_, o_ * 128:(o_ + 1) * 128]))(oc, bsel, kc),
                waits=waits if kc == 0 else [], inc="dw")

    for g in range(FC):
        bsel = g % 2
        wt = [W[f"negd_mm{g - 2}"]] if g >= 2 else [W["bcld"]]
        emit_dpiece(g, bsel, wt)
        W[f"negd_ld{g}"] = ev("dw")

        def f_mm(t, b_=bsel):
            for kc in range(FC):
                last = t.matmul(ps_X[0:1, 0:128], onesb[:, :],
                                dwbuf[:, b_ * 1024 + kc * 128:
                                      b_ * 1024 + kc * 128 + 128],
                                start=(kc == 0), stop=(kc == FC - 1))
            return last
        op("tensor", f_mm, waits=[W[f"negd_ld{g}"], W["ones"], lastX[0]],
           inc="t")
        W[f"negd_mm{g}"] = ev("t")
        op("vector", (lambda g_: lambda v: v.tensor_single_scalar(
            negdsum[:, g_ * 128:(g_ + 1) * 128], ps_X[0:1, 0:128], -1.0,
            OP.mult))(g), waits=[W[f"negd_mm{g}"]], inc="v")
        W[f"negd_ev{g}"] = ev("v")
        lastX[0] = W[f"negd_ev{g}"]
    W["negd"] = W[f"negd_ev{FC-1}"]
    mini_scan(W["negd"])

    if DBG and EARLY_STOP:
        for nm, t_ in [("snapA", snapA), ("snapU", snapU), ("snapC", snapC),
                       ("snapR", snapR), ("th64", th64), ("rho1", rho1),
                       ("costab", costab), ("snapP", snapP)]:
            shp = list(t_.ap().shape)
            dt_ = t_.ap().dtype
            dd = nc.dram_tensor("dump_" + nm, shp, dt_,
                                kind="ExternalOutput").ap()
            op("gpsimd", (lambda d_, a_: lambda e: e.dma_start(d_, a_))(
                dd, t_.ap()), waits=[W["tables"], W["negd"], W["negb"]],
                inc="do")
        final_do = ("do", ctr["do"])
        op("gpsimd", lambda e: e.nop(), waits=[final_do])
        op("sync", lambda e: e.nop(), waits=[final_do])
        do_emit()
        return nc

    bg = []
    bg_marks = {}

    def bgop(engine, fn, waits=(), inc=None):
        bg.append((engine, fn, list(waits), inc))

    def drain(k):
        n = 0
        while bg and (k < 0 or n < k):
            engine, fn, waits, inc = bg.pop(0)
            if engine == "_mark":
                bg_marks[fn] = dict(ctr)
                continue
            rw = []
            for w_ in waits:
                if isinstance(w_, tuple) and w_[0] in ctr:
                    rw.append(w_)
                else:
                    snap = bg_marks.get(w_)
                    assert snap is not None, f"unresolved bg mark {w_}"
                    rw.extend((sn, val) for sn, val in snap.items() if val > 0)
            op(engine, fn, waits=rw, inc=inc)
            n += 1

    def bg_event(key):
        snap = bg_marks.get(key)
        assert snap is not None, f"bg mark {key} not drained"
        return [(sn, val) for sn, val in snap.items() if val > 0]

    WQP = 2048

    def emit_mean(src, nrow, ncol, tag, slot_i):
        WQ = min(WQP, ncol)
        nrb, ncb = nrow // 128, ncol // WQ
        npieces = nrb * ncb
        assert npieces + 101 <= TILE
        for rb in range(nrb):
            for cb in range(ncb):
                buf = rb * ncb + cb
                dst = wqbuf[:, (buf % 2) * WQ:(buf % 2) * WQ + WQ]
                piece = src[rb * 128:(rb + 1) * 128, cb * WQ:(cb + 1) * WQ]
                wt = [(f"{tag}_rd", buf - 2)] if buf >= 2 else [W["msall"]]
                bgop("gpsimd", (lambda d, p: lambda e: e.dma_start(d, p))(dst, piece),
                     waits=wt, inc="dw")
                bgop("_mark", (f"{tag}_ld", buf), [], None)

                def f_red(v, d=dst, b=buf):
                    return v.tensor_reduce(qscr[:, b:b + 1], d, AX, OP.add,
                                           apply_absolute_value=True)
                rw = [(f"{tag}_ld", buf)]
                if tag == "m2" and buf < 2:
                    rw.append(("m1_pm", 0))
                bgop("vector", f_red, waits=rw, inc="v")
                bgop("_mark", (f"{tag}_rd", buf), [], None)

        bgop("tensor", (lambda np_: lambda t: t.matmul(
            ps_y[0][0:1, 0:np_], onesf[:, :], qscr[:, 0:np_],
            start=True, stop=True))(npieces),
             waits=[W["ones"], (f"{tag}_rd", nrb * ncb - 1)], inc="t")
        bgop("_mark", (f"{tag}_pm", 0), [], None)
        bgop("scalar", (lambda np_: lambda s: s.copy(
            qscr[0:1, 101:101 + np_], ps_y[0][0:1, 0:np_]))(npieces),
             waits=[(f"{tag}_pm", 0)], inc="s")
        bgop("_mark", (f"{tag}_pe", 0), [], None)

        def f_tree(v, n=nrow * ncol, si=slot_i, np_=npieces):
            vv = VP(v)
            vv.tensor_reduce(qscr[0:1, 100:101], qscr[0:1, 101:101 + np_],
                             AX, OP.add)
            return vv.tensor_scalar(
                out=thin[0:1, 3 * TILE + si:3 * TILE + si + 1],
                in0=qscr[0:1, 100:101], scalar1=1.0 / n, scalar2=1e-5,
                op0=OP.mult, op1=OP.max)
        bgop("vector", f_tree, waits=[(f"{tag}_pe", 0)], inc="v")
        bgop("_mark", (f"{tag}_mean", 0), [], None)

    emit_mean(w1T, D, H, "m1", 0)
    emit_mean(w2T, H, D, "m2", 1)

    def f_thr(v):
        t0 = thin[0:1, 3 * TILE + 0:3 * TILE + 1]
        t1 = thin[0:1, 3 * TILE + 1:3 * TILE + 2]
        v.tensor_single_scalar(thin[0:1, 3 * TILE + 2:3 * TILE + 3], t0, 0.5, OP.mult)
        v.tensor_single_scalar(thin[0:1, 3 * TILE + 3:3 * TILE + 4], t0, -0.5, OP.mult)
        v.tensor_single_scalar(thin[0:1, 3 * TILE + 4:3 * TILE + 5], t1, 0.5, OP.mult)
        return v.tensor_single_scalar(thin[0:1, 3 * TILE + 5:3 * TILE + 6], t1, -0.5, OP.mult)
    bgop("vector", f_thr, waits=[(f"m2_mean", 0)], inc="v")
    bgop("_mark", ("thrrow", 0), [], None)
    bgop("tensor", lambda t: t.matmul(ps_D[:, 0:6], onesrf[:, :],
                                      thin[0:1, 3 * TILE:3 * TILE + 6],
                                      start=True, stop=True),
         waits=[("thrrow", 0)], inc="t")
    bgop("_mark", ("thrmm", 0), [], None)
    bgop("scalar", lambda s: s.copy(sc128[:, 0:6], ps_D[:, 0:6]),
         waits=[("thrmm", 0)], inc="s")
    bgop("_mark", ("thrcol", 0), [], None)
    bgop("vector", lambda v: v.tensor_scalar(
        out=convw[:, 0:96], in0=convw[:, 0:96], scalar1=sc128[:, 0:1],
        scalar2=None, op0=OP.mult), waits=[("thrcol", 0), W["p0ld"]], inc="v")
    bgop("_mark", ("convfold", 0), [], None)

    def emit_quant(src, dstf, nrow, ncol, thrP, thrN, tag, rowsum, ncw):
        nrb, ncb = nrow // 128, ncol // ncw
        for rb in range(nrb):
            for cb in range(ncb):
                buf = rb * ncb + cb
                dst = wqbuf[:, (buf % 2) * ncw:(buf % 2) * ncw + ncw]
                piece = src[rb * 128:(rb + 1) * 128, cb * ncw:(cb + 1) * ncw]
                wt = [(f"{tag}_q", buf - 2)] if buf >= 2 else [("thrcol", 0)]
                bgop("gpsimd", (lambda d, p: lambda e: e.dma_start(d, p))(dst, piece),
                     waits=wt, inc="dw")
                bgop("_mark", (f"{tag}_l", buf), [], None)

                cps = hsilu[:, 8192:8192 + ncw]
                cms = hsilu[:, 8192 + 2048:8192 + 2048 + ncw]

                def f_q(v, d=dst, cp=cps, cm=cms):
                    v.tensor_scalar(out=cp, in0=d, scalar1=sc128[:, thrP:thrP + 1],
                                    scalar2=None, op0=OP.is_gt)
                    v.tensor_scalar(out=cm, in0=d, scalar1=sc128[:, thrN:thrN + 1],
                                    scalar2=None, op0=OP.is_lt)
                    return v.tensor_tensor(cp, cp, cm, OP.subtract)
                stw = [(f"{tag}_l", buf)] + ([(f"{tag}_st", buf - 1)] if buf >= 1 else [])
                bgop("vector", f_q, waits=stw, inc="v")
                bgop("_mark", (f"{tag}_q", buf), [], None)

                if rowsum:
                    def f_rs(t, cp=cps):
                        for oc in range(FC):
                            last = t.matmul(ps_C[:, oc:oc + 1],
                                            cp[:, oc * 128:(oc + 1) * 128],
                                            onesb[:, :],
                                            start=True, stop=True)
                        return last
                    rw = [(f"{tag}_q", buf)]
                    if buf > 0:
                        rw.append((f"{tag}_ra", buf - 1))
                    bgop("tensor", f_rs, waits=rw, inc="t")
                    bgop("_mark", (f"{tag}_rs", buf), [], None)

                    def f_ra(v, fst=(buf == 0)):
                        if fst:
                            return v.tensor_copy(c2col[:, 0:8], ps_C[:, 0:8])
                        return v.tensor_tensor(c2col[:, 0:8], c2col[:, 0:8],
                                               ps_C[:, 0:8], OP.add)
                    bgop("vector", f_ra, waits=[(f"{tag}_rs", buf)], inc="v")
                    bgop("_mark", (f"{tag}_ra", buf), [], None)
                    stdep = (f"{tag}_rs", buf)
                else:
                    stdep = (f"{tag}_q", buf)
                bgop("gpsimd", (lambda o, cp=cps: lambda e: e.dma_start(
                    o, cp))(dstf(rb, cb)), waits=[stdep], inc="dw")
                bgop("_mark", (f"{tag}_st", buf), [], None)
        bgop("_mark", (f"{tag}_codes", 0), [], None)

    emit_quant(w1T, lambda rb, cb: c1_d[rb, :, cb * WQP:(cb + 1) * WQP],
               D, H, 2, 3, "q1", False, WQP)
    emit_quant(w2T, lambda rb, cb: c2_d[rb, :, :],
               H, D, 4, 5, "q2", True, 1024)
    bgop("vector", lambda v: v.tensor_single_scalar(
        c2col[:, 0:8], c2col[:, 0:8], 128.0, OP.mult),
        waits=[("q2_ra", HC - 1)], inc="v")
    bgop("_mark", ("c2e", 0), [], None)
    bgop("vector", lambda v: v.tensor_scalar(
        out=thinb[0:1, 2 * TILE - 8:2 * TILE - 7],
        in0=thin[0:1, 3 * TILE + 1:3 * TILE + 2],
        scalar1=-1.0 / 127.0, scalar2=None, op0=OP.mult), inc="v")
    bgop("_mark", ("c2mb", 0), [], None)
    bgop("tensor", lambda t: t.matmul(ps_y[1][:, 0:1], onesr[:, :],
                                      thinb[0:1, 2 * TILE - 8:2 * TILE - 7],
                                      start=True, stop=True),
         waits=[("c2mb", 0)], inc="t")
    bgop("_mark", ("c2mm", 0), [], None)
    bgop("scalar", lambda s: s.copy(c2mc[:, :], ps_y[1][:, 0:1]),
         waits=[("c2mm", 0)], inc="s")
    bgop("_mark", ("c2col", 0), [], None)

    def emit_passA(i, w, src, is_main, tag, first_dep):
        j0 = TO[i]
        dep = [W[f"{tag}A{i-1}_xb"]] if i > 0 else list(first_dep)
        for fc in range(FC):
            op("sync", (lambda s_, w_, fc_: lambda e: e.dma_start(
                xsb[:, fc_ * TILE:fc_ * TILE + w_], s_[:, fc_, :]))(
                src, w, fc), waits=dep if fc == 0 else [], inc="dx")
        W[f"{tag}A{i}_x"] = ev("dx")

        def f_cast(v):
            v.tensor_copy(xb[:, :], xsb[:, :])
            return v.tensor_tensor(sq[:, :], xb[:, :], xb[:, :], OP.mult)
        cdep = [W[f"{tag}A{i}_x"]]
        if i > 0:
            cdep.append(W[f"{tag}A{i-1}_bu"])
        elif is_main:
            cdep.append(W[f"PA{NT-1}_bu"])
        op("vector", f_cast, waits=cdep, inc="v")
        W[f"{tag}A{i}_xb"] = ev("v")

        emit_stats(w, lambda fc: fcs(xb, fc, w), lambda fc: fcs(sq, fc, w),
                   f"{tag}st{i}", W[f"{tag}A{i}_xb"], W[f"{tag}A{i}_xb"])

        def f_mrow(v):
            if is_main:
                return v.tensor_copy(mall[:, j0:j0 + w], slot(0, w))
            return v.tensor_copy(thinb[0:1, 0:w], slot(0, w))
        op("vector", f_mrow, waits=[W[f"{tag}st{i}_sxe"]], inc="v")
        W[f"{tag}A{i}_mrow"] = ev("v")

        rdst = (lambda: rball[:, j0:j0 + w]) if is_main else (lambda: bc(5, w))
        emit_bcast(w, 2, rdst, f"{tag}A{i}r", deps=[W[f"{tag}st{i}_r"]])

        mrow = (lambda: mall[:, j0:j0 + w]) if is_main else (lambda: thinb[0:1, 0:w])

        def f_bmm(t):
            for kc in range(FC):
                t.matmul(ps_h[0][0:64, 0:w], bstre[:, kc * 64:(kc + 1) * 64],
                         fcs(xb, kc, w), start=(kc == 0), stop=False)
            t.matmul(ps_h[0][0:64, 0:w], negbsre[:, :], mrow(),
                     start=False, stop=True)
            for kc in range(FC):
                t.matmul(ps_h[1][0:64, 0:w], bstim[:, kc * 64:(kc + 1) * 64],
                         fcs(xb, kc, w), start=(kc == 0), stop=False)
            return t.matmul(ps_h[1][0:64, 0:w], negbsim[:, :], mrow(),
                            start=False, stop=True)
        bdeps = [W[f"{tag}A{i}_xb"], W[f"{tag}A{i}_mrow"], W["negb"]]
        if i > 0:
            bdeps.append(W[f"{tag}A{i-1}_pq"])
        elif is_main:
            bdeps.append(W[f"PA{NT-1}_pq"])
        op("tensor", f_bmm, waits=bdeps, inc="t")
        W[f"{tag}A{i}_bu"] = ev("t")

        def f_rot(v, w_=w, j0_=j0):
            rb_ = rdst()[0:64, 0:w_]
            cs = costab[0:64, j0_:j0_ + w_]
            sn = costab[0:64, NCOL + j0_:NCOL + j0_ + w_]
            v.tensor_tensor(scrA[0:64, 0:w_], cs, rb_, OP.mult)
            v.tensor_tensor(scrB[0:64, 0:w_], sn, rb_, OP.mult)
            v.tensor_tensor(scrC[:, 0:w_], scrA[0:64, 0:w_],
                            ps_h[0][0:64, 0:w_], OP.mult)
            v.tensor_tensor(pq[:, j0_:j0_ + w_], scrB[0:64, 0:w_],
                            ps_h[1][0:64, 0:w_], OP.mult)
            v.tensor_tensor(pq[:, j0_:j0_ + w_], scrC[:, 0:w_],
                            pq[:, j0_:j0_ + w_], OP.add)
            v.tensor_tensor(scrC[:, 0:w_], scrB[0:64, 0:w_],
                            ps_h[0][0:64, 0:w_], OP.mult)
            v.tensor_tensor(scrB[0:64, 0:w_], scrA[0:64, 0:w_],
                            ps_h[1][0:64, 0:w_], OP.mult)
            return v.tensor_tensor(pq[:, NTOK + j0_:NTOK + j0_ + w_],
                                   scrC[:, 0:w_], scrB[0:64, 0:w_],
                                   OP.subtract)
        op("vector", f_rot, waits=[W[f"{tag}A{i}_bu"], W["tables"],
                                   W[f"{tag}A{i}r_bc"]], inc="v")
        W[f"{tag}A{i}_pq"] = ev("v")
        if DBG and not is_main:
            def f_snapt(v, i_=i, w_=w, j0_=j0):
                v.tensor_reduce(snapP[:, 144 + i_:145 + i_],
                                pq[:, j0_:j0_ + w_], AX, OP.max,
                                apply_absolute_value=True)
                return v.tensor_reduce(snapU[:, i_:i_ + 1],
                                       pq[:, NTOK + j0_:NTOK + j0_ + w_],
                                       AX, OP.max, apply_absolute_value=True)
            op("vector", f_snapt, waits=[W[f"{tag}A{i}_pq"]], inc="v")


    xpre_r = xpre.rearrange("(fc p) t -> p fc t", p=128)
    xmain_r = xmain.rearrange("(fc p) t -> p fc t", p=128)

    for i in range(NT):
        emit_passA(i, TILES[i], xpre_r[:, :, TO[i]:TO[i] + TILES[i]],
                   False, "P", [])
        drain(10)
        if i in (0, 3, 6, 8):
            mini_scan(W[f"PA{i}_pq"])

    def emit_scan_fn(init_p, init_q):
        def f_scan(v):
            vv = VP(v)
            vv.tensor_copy(scrA[0:64, 0:TILE],
                           rho1[:, :].broadcast_to([64, TILE]))
            last = None
            for half, init0 in ((0, init_p), (NTOK, init_q)):
                for ck in range(NT):
                    lo, w_ = TO[ck], TILES[ck]
                    ini = (init0 if ck == 0
                           else pq[:, half + lo - 1:half + lo])
                    last = vv.tensor_tensor_scan(
                        pq[:, half + lo:half + lo + w_], scrA[0:64, 0:w_],
                        pq[:, half + lo:half + lo + w_], ini,
                        OP.mult, OP.add)
            return last
        return f_scan

    f_scan_pre = emit_scan_fn(0.0, 0.0)

    def sacrificial_scan(*deps):
        def f_sac(v):
            vv = VP(v)
            vv.tensor_copy(scrC[0:64, 8:16],
                           rho1[:, :].broadcast_to([64, 8]))
            return vv.tensor_tensor_scan(
                scrC[0:64, 0:8], scrC[0:64, 8:16],
                onesf[0:64, :].broadcast_to([64, 8]), 0.0, OP.mult, OP.add)
        op("vector", f_sac, waits=list(deps), inc="v")

    def scan_barrier(evname):
        allw = [(k, ctr[k]) for k in ("t", "s", "g", "dx", "dw", "do")
                if ctr[k] > 0]
        return allw

    def scan_post_barrier(ev_):
        for eng in ("tensor", "scalar", "gpsimd", "sync"):
            op(eng, lambda e: e.nop(), waits=[ev_])
    if DBG:
        def f_snap_pre(v):
            for i_ in range(NT):
                j0_, w_ = TO[i_], TILES[i_]
                v.tensor_reduce(snapU[:, 16 + i_:17 + i_],
                                pq[:, j0_:j0_ + w_], AX, OP.max,
                                apply_absolute_value=True)
                v.tensor_reduce(snapU[:, 32 + i_:33 + i_],
                                pq[:, NTOK + j0_:NTOK + j0_ + w_], AX, OP.max,
                                apply_absolute_value=True)
            return v.tensor_single_scalar(vpad[0:1, 0:1], vpad[0:1, 0:1],
                                          0.0, OP.add)
        op("vector", f_snap_pre, waits=[W[f"PA{NT-1}_pq"]], inc="v")
    sacrificial_scan(W[f"PA{NT-1}_pq"])
    op("vector", f_scan_pre,
       waits=[W[f"PA{NT-1}_pq"]] + scan_barrier("pre"), inc="v")
    W["prescan"] = ev("v")
    scan_post_barrier(W["prescan"])
    if DBG:
        def f_snap_post(v):
            for i_ in range(NT):
                j0_, w_ = TO[i_], TILES[i_]
                v.tensor_reduce(snapU[:, 48 + i_:49 + i_],
                                pq[:, j0_:j0_ + w_], AX, OP.max,
                                apply_absolute_value=True)
            return v.tensor_single_scalar(vpad[0:1, 0:1], vpad[0:1, 0:1],
                                          0.0, OP.add)
        op("vector", f_snap_post, waits=[W["prescan"]], inc="v")


    def f_carry(v):
        v = VP(v)
        cs = costab[0:64, NTOK - 1:NTOK]
        sn = costab[0:64, NCOL + NTOK - 1:NCOL + NTOK]
        sp = pq[:, NTOK - 1:NTOK]
        sqq = pq[:, 2 * NTOK - 1:2 * NTOK]
        v.tensor_tensor(c[:, 8:9], cs, sp, OP.mult)
        v.tensor_tensor(c[:, 9:10], sn, sqq, OP.mult)
        v.tensor_tensor(c[:, 8:9], c[:, 8:9], c[:, 9:10], OP.add)
        v.tensor_tensor(c[:, 9:10], sn, sp, OP.mult)
        v.tensor_tensor(c[:, 10:11], cs, sqq, OP.mult)
        v.tensor_tensor(c[:, 9:10], c[:, 9:10], c[:, 10:11], OP.subtract)
        cs1 = costab[0:64, 1:2]
        sn1 = costab[0:64, NCOL + 1:NCOL + 2]
        v.tensor_tensor(c[:, 10:11], cs1, c[:, 8:9], OP.mult)
        v.tensor_tensor(c[:, 11:12], sn1, c[:, 9:10], OP.mult)
        v.tensor_tensor(c[:, 10:11], c[:, 10:11], c[:, 11:12], OP.subtract)
        v.tensor_tensor(c[:, 11:12], sn1, c[:, 8:9], OP.mult)
        v.tensor_tensor(c[:, 12:13], cs1, c[:, 9:10], OP.mult)
        v.tensor_tensor(c[:, 11:12], c[:, 11:12], c[:, 12:13], OP.add)
        return v.tensor_single_scalar(c[:, 11:12], c[:, 11:12], -1.0, OP.mult)
    op("vector", f_carry, waits=[W["prescan"], W["tables"]], inc="v")
    W["carry"] = ev("v")
    if DBG:
        op("vector", lambda v: v.tensor_copy(snapB[:, :], c[:, :]),
           waits=[W["carry"]], inc="v")

    for i in range(NT):
        emit_passA(i, TILES[i], xmain_r[:, :, 1 + TO[i]:1 + TO[i] + TILES[i]],
                   True, "M", [W[f"PA{NT-1}_xb"]])
        drain(10)
    drain(-1)

    f_scan_main = emit_scan_fn(c[:, 10:11], c[:, 11:12])
    sacrificial_scan(W[f"MA{NT-1}_pq"], W["carry"])
    op("vector", f_scan_main,
       waits=[W[f"MA{NT-1}_pq"], W["carry"]] + scan_barrier("main"), inc="v")
    W["mainscan"] = ev("v")
    scan_post_barrier(W["mainscan"])

    hs_ = halos
    xhb = cvt1.ap()

    for fc in range(FC):
        op("sync", (lambda fc_: lambda e: e.dma_start(
            hs_[:, fc_:fc_ + 1], xmain_r[:, fc_, 0:1],
            allow_slow_non_contiguous=True))(fc),
            waits=[W[f"MA{NT-1}_xb"]] if fc == 0 else [], inc="dx")
        op("sync", (lambda fc_: lambda e: e.dma_start(
            hs_[:, FC + fc_:FC + fc_ + 1], xmain_r[:, fc_, NCOL - 1:NCOL],
            allow_slow_non_contiguous=True))(fc), inc="dx")
    W["hx"] = ev("dx")

    def f_hcast(v):
        return v.tensor_copy(xhb[:, 0:2 * FC], hs_[:, 0:2 * FC])
    op("vector", f_hcast, waits=[W["hx"]], inc="v")
    W["hxb"] = ev("v")

    def f_hsq(v):
        return v.tensor_tensor(xhb[:, 32:32 + 2 * FC], xhb[:, 0:2 * FC],
                               xhb[:, 0:2 * FC], OP.mult)
    op("vector", f_hsq, waits=[W["hxb"]], inc="v")
    W["hsq"] = ev("v")

    TH = 3 * TILE + 8

    def h_stats(movoff, tag):
        def f_mm(t, mo=movoff):
            for fc in range(FC):
                t.matmul(ps_S[0:1, 0:1], onesb[:, :],
                         xhb[:, mo + fc:mo + fc + 1],
                         start=(fc == 0), stop=(fc == FC - 1))
            for fc in range(FC):
                last = t.matmul(ps_S[0:1, 1:2], onesb[:, :],
                                xhb[:, mo + FC + fc:mo + FC + fc + 1],
                                start=(fc == 0), stop=(fc == FC - 1))
            return last
        op("tensor", f_mm, waits=[W[f"{tag}_movr"], lastS[0]], inc="t")
        W[f"{tag}_mm"] = ev("t")
        op("scalar", lambda s: s.copy(thin[0:1, TH:TH + 2], ps_S[0:1, 0:2]),
           waits=[W[f"{tag}_mm"]], inc="s")
        W[f"{tag}_me"] = ev("s")

        def f_mm2(t, mo=movoff):
            for fc in range(FC):
                t.matmul(ps_S[0:1, 0:1], onesb[:, :],
                         xhb[:, mo + 32 + fc:mo + 32 + fc + 1],
                         start=(fc == 0), stop=(fc == FC - 1))
            for fc in range(FC):
                last = t.matmul(ps_S[0:1, 1:2], onesb[:, :],
                                xhb[:, mo + 32 + FC + fc:mo + 32 + FC + fc + 1],
                                start=(fc == 0), stop=(fc == FC - 1))
            return last
        op("tensor", f_mm2, waits=[W[f"{tag}_me"], W[f"{tag}_sqr"]], inc="t")
        W[f"{tag}_mm2"] = ev("t")
        op("scalar", lambda s: s.copy(thin[0:1, TH + 2:TH + 4], ps_S[0:1, 0:2]),
           waits=[W[f"{tag}_mm2"]], inc="s")
        W[f"{tag}_qe"] = ev("s")
        lastS[0] = W[f"{tag}_qe"]

        def f_thin(v):
            v = VP(v)
            m = thin[0:1, TH:TH + 2]
            q = thin[0:1, TH + 2:TH + 4]
            r = thin[0:1, TH + 4:TH + 6]
            v.tensor_single_scalar(m, m, 1.0 / D, OP.mult)
            v.tensor_single_scalar(q, q, 1.0 / D, OP.mult)
            v.tensor_tensor(r, m, m, OP.mult)
            return v.tensor_tensor(q, q, r, OP.subtract)
        op("vector", f_thin, waits=[W[f"{tag}_qe"]], inc="v")
        W[f"{tag}_var"] = ev("v")

        def f_r(s):
            q = thin[0:1, TH + 2:TH + 4]
            s.activation(q, q, AF.Ln, bias=epsc[0:1, :])
            return s.activation(thin[0:1, TH + 4:TH + 6], q, AF.Exp, scale=-0.5)
        op("scalar", f_r, waits=[W[f"{tag}_var"]], inc="s")
        W[f"{tag}_r"] = ev("s")

    def h_bcast(srcoff, n, dstoff, tag, deps=()):
        def f_cp(v):
            return v.tensor_copy(thinb[0:1, 8:8 + n], thin[0:1, srcoff:srcoff + n])
        op("vector", f_cp, waits=list(deps) + [lastX[0]], inc="v")
        W[f"{tag}_bm"] = ev("v")
        op("tensor", lambda t: t.matmul(ps_X[:, 0:n], onesr[:, :],
                                        thinb[0:1, 8:8 + n], start=True, stop=True),
           waits=[W[f"{tag}_bm"], lastX[0]], inc="t")
        W[f"{tag}_bo"] = ev("t")
        op("scalar", lambda s: s.copy(hs_[:, dstoff:dstoff + n], ps_X[:, 0:n]),
           waits=[W[f"{tag}_bo"]], inc="s")
        W[f"{tag}_bc"] = ev("s")
        lastX[0] = W[f"{tag}_bc"]

    W["hln1_movr"] = W["hxb"]
    W["hln1_sqr"] = W["hsq"]
    h_stats(0, "hln1")
    h_bcast(TH, 2, 40, "hb1m", deps=[W["hln1_r"]])
    h_bcast(TH + 4, 2, 42, "hb1r", deps=[W["hln1_r"]])

    def f_hmb(v):
        return v.tensor_copy(thinb[0:1, 12:14], thin[0:1, TH:TH + 2])
    op("vector", f_hmb, waits=[W["hln1_r"]], inc="v")
    W["hmb"] = ev("v")

    def f_hbmm(t):
        for kc in range(FC):
            t.matmul(ps_h[0][0:64, 0:1], bstre[:, kc * 64:(kc + 1) * 64],
                     xhb[:, FC + kc:FC + kc + 1], start=(kc == 0), stop=False)
        t.matmul(ps_h[0][0:64, 0:1], negbsre[:, :], thinb[0:1, 13:14],
                 start=False, stop=True)
        for kc in range(FC):
            t.matmul(ps_h[1][0:64, 0:1], bstim[:, kc * 64:(kc + 1) * 64],
                     xhb[:, FC + kc:FC + kc + 1], start=(kc == 0), stop=False)
        return t.matmul(ps_h[1][0:64, 0:1], negbsim[:, :], thinb[0:1, 13:14],
                        start=False, stop=True)
    op("tensor", f_hbmm, waits=[W["hmb"], W["hxb"], W["mainscan"], W["negb"],
                                W[f"MA{NT-1}_pq"]], inc="t")
    W["hbu"] = ev("t")

    def f_hh(v):
        v = VP(v)
        v.tensor_copy(hs_[0:64, 44:45], c[:, 8:9])
        v.tensor_copy(hs_[0:64, 46:47], c[:, 9:10])
        cs = costab[0:64, NTOK:NTOK + 1]
        sn = costab[0:64, NCOL + NTOK:NCOL + NTOK + 1]
        rcol = hs_[0:64, 43:44]
        v.tensor_tensor(c[:, 12:13], cs, rcol, OP.mult)
        v.tensor_tensor(c[:, 13:14], sn, rcol, OP.mult)
        v.tensor_tensor(c[:, 14:15], c[:, 12:13], ps_h[0][0:64, 0:1], OP.mult)
        v.tensor_tensor(scrC[:, 0:1], c[:, 13:14], ps_h[1][0:64, 0:1], OP.mult)
        v.tensor_tensor(c[:, 14:15], c[:, 14:15], scrC[:, 0:1], OP.add)
        v.tensor_tensor(c[:, 15:16], c[:, 13:14], ps_h[0][0:64, 0:1], OP.mult)
        v.tensor_tensor(scrC[:, 0:1], c[:, 12:13], ps_h[1][0:64, 0:1], OP.mult)
        v.tensor_tensor(c[:, 15:16], c[:, 15:16], scrC[:, 0:1], OP.subtract)
        v.tensor_tensor(scrC[:, 0:1], rho1[:, :], pq[:, NTOK - 1:NTOK], OP.mult)
        v.tensor_tensor(c[:, 14:15], c[:, 14:15], scrC[:, 0:1], OP.add)
        v.tensor_tensor(scrC[:, 0:1], rho1[:, :], pq[:, 2 * NTOK - 1:2 * NTOK],
                        OP.mult)
        v.tensor_tensor(c[:, 15:16], c[:, 15:16], scrC[:, 0:1], OP.add)
        v.tensor_tensor(scrC[:, 0:1], cs, c[:, 14:15], OP.mult)
        v.tensor_tensor(scrC[:, 1:2], sn, c[:, 15:16], OP.mult)
        v.tensor_tensor(hs_[0:64, 45:46], scrC[:, 0:1], scrC[:, 1:2], OP.add)
        v.tensor_tensor(scrC[:, 0:1], sn, c[:, 14:15], OP.mult)
        v.tensor_tensor(scrC[:, 1:2], cs, c[:, 15:16], OP.mult)
        v.tensor_tensor(hs_[0:64, 47:48], scrC[:, 0:1], scrC[:, 1:2],
                        OP.subtract)
        v.tensor_copy(hsb[0:64, 0:1], hs_[0:64, 44:45])
        v.tensor_copy(hsb[0:64, 1:2], hs_[0:64, 45:46])
        v.tensor_copy(hsb[0:64, 2:3], hs_[0:64, 46:47])
        return v.tensor_copy(hsb[0:64, 3:4], hs_[0:64, 47:48])
    op("vector", f_hh, waits=[W["hbu"], W["carry"], W["hb1r_bc"]], inc="v")
    W["hh"] = ev("v")

    for g in range(FC):
        bsel = g % 2
        wt = [W[f"hd_mm{g - 2}"]] if g >= 2 else [W[f"negd_mm{FC - 2 + g}"]]
        emit_dpiece(g, bsel, wt + [W["negd"]])
        W[f"hd_ld{g}"] = ev("dw")

        def f_dc(t, oc=g, b_=bsel):
            for kc in range(FC):
                t.matmul(ps_D[:, 2 * oc:2 * oc + 2],
                         dwbuf[:, b_ * 1024 + kc * 128:
                               b_ * 1024 + kc * 128 + 128],
                         xhb[:, kc:2 * FC:FC],
                         start=(kc == 0), stop=False)
            t.matmul(ps_D[:, 2 * oc:2 * oc + 2],
                     negdsum[:, oc * 128:(oc + 1) * 128],
                     thinb[0:1, 12:14], start=False, stop=True)
            t.matmul(ps_C[:, 2 * oc:2 * oc + 2],
                     cstre[:, oc * 128:(oc + 1) * 128],
                     hsb[0:64, 0:2], start=True, stop=False)
            return t.matmul(ps_C[:, 2 * oc:2 * oc + 2],
                            cstim[:, oc * 128:(oc + 1) * 128],
                            hsb[0:64, 2:4], start=False, stop=True)
        hdw = [W[f"hd_ld{g}"], W["hh"], W["hmb"], W["negd"]]
        if g == 0:
            hdw += bg_event(("c2col", 0))
        op("tensor", f_dc, waits=hdw, inc="t")
        W[f"hd_mm{g}"] = ev("t")

    def f_hev(v):
        v = VP(v)
        rb2 = hs_[:, 42:44]
        for oc in range(FC):
            v.tensor_tensor(hs_[:, 2 * oc:2 * oc + 2],
                            ps_D[:, 2 * oc:2 * oc + 2], rb2, OP.mult)
            v.tensor_tensor(hs_[:, 2 * oc:2 * oc + 2],
                            hs_[:, 2 * oc:2 * oc + 2],
                            ps_C[:, 2 * oc:2 * oc + 2], OP.add)
        return v.tensor_single_scalar(vpad[0:1, 0:1], vpad[0:1, 0:1],
                                          0.0, OP.add)
    op("vector", f_hev, waits=[W[f"hd_mm{FC-1}"], W["hb1r_bc"]], inc="v")
    W["hy"] = ev("v")

    for fc in range(FC):
        op("sync", (lambda fc_: lambda e: e.dma_start(
            hs_[:, 16 + fc_:16 + fc_ + 1], xmain_r[:, fc_, 0:1],
            allow_slow_non_contiguous=True))(fc),
            waits=[W["hy"]] if fc == 0 else [], inc="dx")
        op("sync", (lambda fc_: lambda e: e.dma_start(
            hs_[:, 16 + FC + fc_:16 + FC + fc_ + 1],
            xmain_r[:, fc_, NCOL - 1:NCOL],
            allow_slow_non_contiguous=True))(fc), inc="dx")
    W["hx2"] = ev("dx")

    def f_hyb(v):
        v = VP(v)
        v.tensor_copy(xhb[:, 0:16], hs_[:, 0:16])
        return v.tensor_tensor(xhb[:, 32:48], xhb[:, 0:16], xhb[:, 0:16],
                               OP.mult)
    op("vector", f_hyb, waits=[W["hy"]], inc="v")
    W["hln2_movr"] = ev("v")
    W["hln2_sqr"] = W["hln2_movr"]

    def h_stats2(tag):
        def f_mm(t):
            for fc in range(FC):
                t.matmul(ps_S[0:1, 0:2], onesb[:, :],
                         xhb[:, 2 * fc:2 * fc + 2],
                         start=(fc == 0), stop=(fc == FC - 1))
            for fc in range(FC):
                last = t.matmul(ps_S[0:1, 2:4], onesb[:, :],
                                xhb[:, 32 + 2 * fc:32 + 2 * fc + 2],
                                start=(fc == 0), stop=(fc == FC - 1))
            return last
        op("tensor", f_mm, waits=[W[f"{tag}_movr"], lastS[0]], inc="t")
        W[f"{tag}_mm"] = ev("t")
        op("scalar", lambda s: s.copy(thin[0:1, TH:TH + 4], ps_S[0:1, 0:4]),
           waits=[W[f"{tag}_mm"]], inc="s")
        W[f"{tag}_qe"] = ev("s")
        lastS[0] = W[f"{tag}_qe"]

        def f_thin(v):
            v = VP(v)
            m = thin[0:1, TH:TH + 2]
            q = thin[0:1, TH + 2:TH + 4]
            r = thin[0:1, TH + 4:TH + 6]
            v.tensor_single_scalar(m, m, 1.0 / D, OP.mult)
            v.tensor_single_scalar(q, q, 1.0 / D, OP.mult)
            v.tensor_tensor(r, m, m, OP.mult)
            return v.tensor_tensor(q, q, r, OP.subtract)
        op("vector", f_thin, waits=[W[f"{tag}_qe"]], inc="v")
        W[f"{tag}_var"] = ev("v")

        def f_r(s):
            q = thin[0:1, TH + 2:TH + 4]
            s.activation(q, q, AF.Ln, bias=epsc[0:1, :])
            return s.activation(thin[0:1, TH + 4:TH + 6], q, AF.Exp, scale=-0.5)
        op("scalar", f_r, waits=[W[f"{tag}_var"]], inc="s")
        W[f"{tag}_r"] = ev("s")

    h_stats2("hln2")
    h_bcast(TH, 2, 40, "hb2m", deps=[W["hln2_r"]])
    h_bcast(TH + 4, 2, 42, "hb2r", deps=[W["hln2_r"]])

    def f_hx1b(v):
        v = VP(v)
        mb2 = hs_[:, 40:42]
        rb2 = hs_[:, 42:44]
        for oc in range(FC):
            yv = hs_[:, 2 * oc:2 * oc + 2]
            v.tensor_tensor(yv, yv, mb2, OP.subtract)
            v.tensor_tensor(yv, yv, rb2, OP.mult)
        for oc in range(FC):
            yv = hs_[:, 2 * oc:2 * oc + 2]
            v.tensor_tensor(yv[:, 0:1], yv[:, 0:1],
                            hs_[:, 16 + oc:16 + oc + 1], OP.add)
            v.tensor_tensor(yv[:, 1:2], yv[:, 1:2],
                            hs_[:, 16 + FC + oc:16 + FC + oc + 1], OP.add)
        return v.tensor_single_scalar(vpad[0:1, 0:1], vpad[0:1, 0:1],
                                          0.0, OP.add)
    op("vector", f_hx1b, waits=[W["hb2m_bc"], W["hb2r_bc"], W["hx2"]], inc="v")
    W["hx1"] = ev("v")

    def f_hx1sq(v):
        v = VP(v)
        v.tensor_copy(xhb[:, 0:16], hs_[:, 0:16])
        return v.tensor_tensor(xhb[:, 32:48], xhb[:, 0:16], xhb[:, 0:16],
                               OP.mult)
    op("vector", f_hx1sq, waits=[W["hx1"]], inc="v")
    W["hln3_movr"] = ev("v")
    W["hln3_sqr"] = W["hln3_movr"]
    h_stats2("hln3")
    h_bcast(TH, 2, 40, "hb3m", deps=[W["hln3_r"]])

    def f_ht3(v):
        v = VP(v)
        mb3 = hs_[:, 40:42]
        for oc in range(FC):
            yv = hs_[:, 2 * oc:2 * oc + 2]
            v.tensor_tensor(yv, yv, mb3, OP.subtract)
        return v.tensor_reduce(hs_[:, 16:18],
                               hs_[:, 0:16].rearrange("p (fc t) -> p t fc", fc=FC),
                               AX, OP.max, apply_absolute_value=True)
    op("vector", f_ht3, waits=[W["hb3m_bc"]], inc="v")
    W["ht3"] = ev("v")

    op("tensor", lambda t: t.transpose(ps_X[0:2, 0:128], hs_[:, 16:18],
                                       idnf[:, :]),
       waits=[W["ht3"], lastX[0], W["p0ld"]], inc="t")
    W["htf"] = ev("t")
    op("vector", lambda v: v.tensor_reduce(hs_[0:2, 20:21], ps_X[0:2, 0:128],
                                           AX, OP.max),
       waits=[W["htf"]], inc="v")
    W["htr"] = ev("v")
    lastX[0] = W["htr"]
    op("tensor", lambda t: t.transpose(ps_X[0:1, 0:128], hs_[:, 20:21],
                                       idnf[:, :]),
       waits=[W["htr"], lastX[0]], inc="t")
    W["htg"] = ev("t")
    op("scalar", lambda s: s.copy(thin[0:1, TH:TH + 2], ps_X[0:1, 0:2]),
       waits=[W["htg"]], inc="s")
    W["hte"] = ev("s")
    lastX[0] = W["hte"]

    def f_huc(v):
        v = VP(v)
        v.tensor_tensor(thin[0:1, TH:TH + 2], thin[0:1, TH:TH + 2],
                        thin[0:1, TH + 4:TH + 6], OP.mult)
        return v.tensor_single_scalar(thin[0:1, TH:TH + 2],
                                      thin[0:1, TH:TH + 2], 1e-5, OP.max)
    op("vector", f_huc, waits=[W["hte"]], inc="v")
    W["huc"] = ev("v")

    def f_hs3(s):
        uc = thin[0:1, TH:TH + 2]
        s3 = thin[0:1, TH + 2:TH + 4]
        s.activation(s3, uc, AF.Ln)
        return s.activation(s3, s3, AF.Exp, scale=-1.0)
    op("scalar", f_hs3, waits=[W["huc"]], inc="s")
    W["hs3"] = ev("s")

    def f_hrs(v):
        v = VP(v)
        s3 = thin[0:1, TH + 2:TH + 4]
        uc = thin[0:1, TH:TH + 2]
        v.tensor_single_scalar(s3, s3, 127.0, OP.mult)
        v.tensor_tensor(s3, s3, thin[0:1, TH + 4:TH + 6], OP.mult)
        v.tensor_single_scalar(uc, uc, 1.0 / 127.0, OP.mult)
        return v.tensor_single_scalar(vpad[0:1, 0:1], vpad[0:1, 0:1],
                                          0.0, OP.add)
    op("vector", f_hrs, waits=[W["hs3"]], inc="v")
    W["hrs"] = ev("v")
    h_bcast(TH + 2, 2, 40, "hbrs", deps=[W["hrs"]])
    h_bcast(TH, 2, 42, "hbdq", deps=[W["hrs"]])

    def f_hq(v):
        v = VP(v)
        rsb = hs_[:, 40:42]
        dqb = hs_[:, 42:44]
        for oc in range(FC):
            tcol = hs_[:, 2 * oc:2 * oc + 2]
            v.tensor_tensor(tcol, tcol, rsb, OP.mult)
            v.tensor_single_scalar(k16f[:, 0:2], tcol, 128.5, OP.add)
            v.scalar_tensor_tensor(out=hs_[:, 18:20], in0=k16f[:, 0:2],
                                   scalar=-128.5, in1=tcol,
                                   op0=OP.add, op1=OP.is_gt)
            v.tensor_tensor(hs_[:, 16:18], k16f[:, 0:2], hs_[:, 18:20],
                            OP.subtract)
            v.tensor_single_scalar(hs_[:, 16:18], hs_[:, 16:18], -128.0, OP.add)
            v.tensor_tensor(hs_[:, 16:18], hs_[:, 16:18], dqb, OP.mult)
            v.tensor_scalar(out=hs_[:, 17:18], in0=hs_[:, 17:18],
                            scalar1=maskc[:, 0:1], scalar2=None, op0=OP.mult)
            v.tensor_copy(haloq[:, 2 * oc:2 * oc + 2], hs_[:, 16:18])
        return v.tensor_single_scalar(vpad[0:1, 0:1], vpad[0:1, 0:1],
                                          0.0, OP.add)
    op("vector", f_hq, waits=[W["hbdq_bc"], W["hbrs_bc"]], inc="v")
    W["haloq"] = ev("v")

    def emit_attn(i):
        w = TILES[i]
        j0 = TO[i]
        x1t = x1[:, (i % 2) * FC * TILE:(i % 2) * FC * TILE + FC * TILE]
        winb = xqw[:, (i % 2) * FC * 458:((i % 2) + 1) * FC * 458]

        dep = [W[f"Bq3_{i-1}"]] if i > 0 else [W["hln3_movr"]]
        for fc in range(FC):
            op("sync", (lambda w_, fc_, j0_: lambda e: e.dma_start(
                xsb[:, fc_ * TILE:fc_ * TILE + w_],
                xmain_r[:, fc_, 1 + j0_:1 + j0_ + w_]))(w, fc, j0),
                waits=dep if fc == 0 else [], inc="dx")
        W[f"Bx{i}"] = ev("dx")

        cdep = [W[f"Bx{i}"]]
        if i > 0:
            cdep += [W[f"B_ln3_{i-1}_sq"], W[f"BDC{i-1}_7"]]
        op("vector", lambda v: v.tensor_copy(xb[:, :], xsb[:, :]),
           waits=cdep, inc="v")
        W[f"Bxb{i}"] = ev("v")

        pdep = [W["mainscan"]]
        pdep.append(W[f"BDC{i-1}_7"] if i > 0 else W[f"hd_mm{FC-1}"])
        def f_post(v, w_=w, j0_=j0):
            cs = costab[0:64, j0_:j0_ + w_]
            sn = costab[0:64, NCOL + j0_:NCOL + j0_ + w_]
            sp = pq[:, j0_:j0_ + w_]
            sqq = pq[:, NTOK + j0_:NTOK + j0_ + w_]
            v.tensor_tensor(scrA[0:64, 0:w_], cs, sp, OP.mult)
            v.tensor_tensor(scrB[0:64, 0:w_], sn, sqq, OP.mult)
            v.tensor_tensor(hsb[0:64, 0:w_], scrA[0:64, 0:w_],
                            scrB[0:64, 0:w_], OP.add)
            v.tensor_tensor(scrA[0:64, 0:w_], sn, sp, OP.mult)
            v.tensor_tensor(scrB[0:64, 0:w_], cs, sqq, OP.mult)
            return v.tensor_tensor(hsb[0:64, TILE:TILE + w_],
                                   scrA[0:64, 0:w_], scrB[0:64, 0:w_],
                                   OP.subtract)
        op("vector", f_post, waits=pdep, inc="v")
        W[f"Bhs{i}"] = ev("v")

        for oc in range(FC):
            bsel = oc % 2
            if oc >= 2:
                wt = [W[f"BDC{i}_{oc-2}"]]
            elif i == 0:
                wt = [W[f"hd_mm{FC - 2 + oc}"]]
            else:
                wt = [W[f"BDC{i-1}_{FC - 2 + oc}"]]
            emit_dpiece(oc, bsel, wt)
            W[f"BDld{i}_{oc}"] = ev("dw")

            def f_dc1(t, oc_=oc, b_=bsel, w_=w, j0_=j0):
                for kc in range(FC):
                    t.matmul(ps_D[:, 0:w_],
                             dwbuf[:, b_ * 1024 + kc * 128:
                                   b_ * 1024 + kc * 128 + 128],
                             fcs(xb, kc, w_), start=(kc == 0), stop=False)
                t.matmul(ps_D[:, 0:w_], negdsum[:, oc_ * 128:(oc_ + 1) * 128],
                         mall[:, j0_:j0_ + w_], start=False, stop=True)
                t.matmul(ps_C[:, 0:w_], cstre[:, oc_ * 128:(oc_ + 1) * 128],
                         hsb[0:64, 0:w_], start=True, stop=False)
                return t.matmul(ps_C[:, 0:w_],
                                cstim[:, oc_ * 128:(oc_ + 1) * 128],
                                hsb[0:64, TILE:TILE + w_], start=False,
                                stop=True)
            deps = [W[f"BDld{i}_{oc}"], W[f"Bxb{i}"], W[f"Bhs{i}"], W["negd"]]
            if oc > 0:
                deps.append(W[f"Byev{i}_{oc-1}"])
            elif i == 0:
                deps.append(W["hy"])
            elif i >= 2:
                deps.append(W[f"F_ln4_{i-2}_sq"])
            op("tensor", f_dc1, waits=deps, inc="t")
            W[f"BDC{i}_{oc}"] = ev("t")

            def f_yev(v, oc_=oc, w_=w, j0_=j0):
                v.tensor_tensor(scrA[:, 0:w_], ps_D[:, 0:w_],
                                rball[:, j0_:j0_ + w_], OP.mult)
                return v.tensor_tensor(fcs(ybf, oc_, w_), scrA[:, 0:w_],
                                       ps_C[:, 0:w_], OP.add)
            op("vector", f_yev, waits=[W[f"BDC{i}_{oc}"]], inc="v")
            W[f"Byev{i}_{oc}"] = ev("v")

        sqdep = [W[f"Byev{i}_7"]]
        if i >= 2:
            sqdep.append(W[f"F_ln4_{i-2}_sq"])
        op("vector", lambda v: v.tensor_tensor(sq[:, :], ybf[:, :], ybf[:, :],
                                               OP.mult), waits=sqdep, inc="v")
        W[f"Bysq{i}"] = ev("v")
        emit_stats(w, lambda fc: fcs(ybf, fc, w), lambda fc: fcs(sq, fc, w),
                   f"B_ln2_{i}", W[f"Byev{i}_7"], W[f"Bysq{i}"])
        emit_bcast(w, 0, lambda: bc(0, w), f"Bm2_{i}", deps=[W[f"B_ln2_{i}_r"]])
        emit_bcast(w, 2, lambda: bc(1, w), f"Br2_{i}", deps=[W[f"B_ln2_{i}_r"]])

        def f_x1(v, w_=w, x1t_=x1t):
            for fc in range(FC):
                y = fcs(ybf, fc, w_)
                v.tensor_tensor(y, y, bc(0, w_), OP.subtract)
                v.tensor_tensor(y, y, bc(1, w_), OP.mult)
                v.tensor_tensor(fcs(x1t_, fc, w_), fcs(xsb, fc, w_), y, OP.add)
            return v.tensor_single_scalar(vpad[0:1, 0:1], vpad[0:1, 0:1],
                                          0.0, OP.add)
        x1dep = [W[f"Bm2_{i}_bc"], W[f"Br2_{i}_bc"], W[f"Bx{i}"]]
        if i >= 2:
            x1dep.append(W[f"Fout{i-2}"])
        if DBG and i == 1:
            op("vector", lambda v: v.nop(), waits=x1dep, inc="v")
            dxs = nc.dram_tensor("dump_xsb1", [128, FC * TILE], F32,
                                 kind="ExternalOutput").ap()
            dyv = nc.dram_tensor("dump_ybv1", [128, FC * TILE], BF16,
                                 kind="ExternalOutput").ap()
            op("sync", lambda e: e.dma_start(dxs, xsb[:, :]),
               waits=[ev("v")], inc="dx")
            op("sync", lambda e: e.dma_start(dyv, ybf[:, :]), inc="dx")
            x1dep = x1dep + [ev("dx")]
        op("vector", f_x1, waits=x1dep, inc="v")
        W[f"Bx1_{i}"] = ev("v")

        def f_x1b(v, x1t_=x1t):
            v.tensor_copy(xb[:, :], x1t_)
            return v.tensor_tensor(sq[:, :], xb[:, :], xb[:, :], OP.mult)
        op("vector", f_x1b, waits=[W[f"Bx1_{i}"], W[f"BDC{i}_7"],
                                   W[f"B_ln2_{i}_sq"]], inc="v")
        W[f"Bx1b{i}"] = ev("v")
        emit_stats(w, lambda fc: fcs(xb, fc, w), lambda fc: fcs(sq, fc, w),
                   f"B_ln3_{i}", W[f"Bx1b{i}"], W[f"Bx1b{i}"])
        emit_bcast(w, 0, lambda: bc(2, w), f"Bm3_{i}", deps=[W[f"B_ln3_{i}_r"]])

        def f_t3(v, w_=w, x1t_=x1t):
            for fc in range(FC):
                v.tensor_tensor(fcs(xsb, fc, w_), fcs(x1t_, fc, w_),
                                bc(2, w_), OP.subtract)
            return v.tensor_single_scalar(vpad[0:1, 0:1], vpad[0:1, 0:1],
                                          0.0, OP.add)
        op("vector", f_t3, waits=[W[f"Bm3_{i}_bc"]], inc="v")
        W[f"Bt3_{i}"] = ev("v")

        emit_bcast(w, 2, lambda: bc(3, w), f"Br3b_{i}",
                   deps=[W[f"B_ln3_{i}_r"]])

        def f_mx3(v, w_=w):
            return v.tensor_reduce(rmax[:, 0:w_],
                                   xsb.ap().rearrange("p (fc t) -> p t fc", fc=FC)
                                   [:, 0:w_, :], AX, OP.max,
                                   apply_absolute_value=True)
        op("vector", f_mx3, waits=[W[f"Bt3_{i}"], W[f"Fpm{i-2}_done"]
                                   if i >= 2 else W["haloq"]], inc="v")
        W[f"Bmx3_{i}"] = ev("v")
        emit_pmax_bcast(w, f"Bpm{i}", [W[f"Bmx3_{i}"]])

        def f_scale3(v, w_=w):
            v.tensor_scalar(out=scrA[:, 0:w_], in0=rmax[:, 0:w_], scalar1=1e-5,
                            scalar2=None, op0=OP.max)
            return v.tensor_tensor(scrA[:, 0:w_], scrA[:, 0:w_], bc(3, w_),
                                   OP.mult)
        op("vector", f_scale3, waits=[W[f"Bpm{i}_done"], W[f"Br3b_{i}_bc"]],
           inc="v")
        W[f"Buc3_{i}"] = ev("v")

        def f_s3(s, w_=w):
            s.activation(scrB[:, 0:w_], scrA[:, 0:w_], AF.Ln)
            return s.activation(scrB[:, 0:w_], scrB[:, 0:w_], AF.Exp,
                                scale=-1.0)
        op("scalar", f_s3, waits=[W[f"Buc3_{i}"]], inc="s")
        W[f"Bs3_{i}"] = ev("s")

        def f_rs3(v, w_=w):
            v.tensor_single_scalar(scrB[:, 0:w_], scrB[:, 0:w_], 127.0, OP.mult)
            v.tensor_tensor(bc(3, w_), bc(3, w_), scrB[:, 0:w_], OP.mult)
            return v.tensor_scalar(out=bc(4, w_), in0=scrA[:, 0:w_],
                                   scalar1=1.0 / 127.0, scalar2=None,
                                   op0=OP.mult)
        op("vector", f_rs3, waits=[W[f"Bs3_{i}"]], inc="v")
        W[f"Brs3b_{i}_bc"] = ev("v")
        W[f"Bdq3b_{i}_bc"] = W[f"Brs3b_{i}_bc"]

        def f_q3(v, w_=w, winb_=winb):
            for fc in range(FC):
                t = fcs(xsb, fc, w_)
                v.tensor_tensor(t, t, bc(3, w_), OP.mult)
            v.tensor_single_scalar(k16d[:, 0:FC * TILE], xsb[:, :], 128.5,
                                   OP.add)
            v.scalar_tensor_tensor(out=ybf[:, :], in0=k16d[:, 0:FC * TILE],
                                   scalar=-128.5, in1=xsb[:, :],
                                   op0=OP.add, op1=OP.is_gt)
            v.tensor_tensor(xsb[:, :], k16d[:, 0:FC * TILE], ybf[:, :],
                            OP.subtract)
            for fc in range(FC):
                v.scalar_tensor_tensor(
                    out=winb_[:, fc * 458 + 1:fc * 458 + 1 + w_],
                    in0=fcs(xsb, fc, w_), scalar=-128.0, in1=bc(4, w_),
                    op0=OP.add, op1=OP.mult)
            return v.tensor_single_scalar(vpad[0:1, 0:1], vpad[0:1, 0:1],
                                          0.0, OP.add)
        q3dep = [W[f"Brs3b_{i}_bc"], W[f"Bdq3b_{i}_bc"], W[f"B_ln3_{i}_sq"]]
        if i >= 2:
            q3dep.append(W[f"Fmm1_{i-2}_{HC-1}"])
        op("vector", f_q3, waits=q3dep, inc="v")
        W[f"Bq3_{i}"] = ev("v")

        def f_edges(v, w_=w, winb_=winb, i_=i):
            if i_ == 0:
                for fc in range(FC):
                    v.tensor_copy(winb_[:, fc * 458:fc * 458 + 1],
                                  haloq[:, 2 * fc:2 * fc + 1])
            else:
                pw = TILES[i_ - 1]
                pwin = xqw[:, ((i_ - 1) % 2) * FC * 458:
                           ((i_ - 1) % 2 + 1) * FC * 458]
                for fc in range(FC):
                    v.tensor_copy(winb_[:, fc * 458:fc * 458 + 1],
                                  pwin[:, fc * 458 + pw:fc * 458 + pw + 1])
                    v.tensor_copy(pwin[:, fc * 458 + pw + 1:fc * 458 + pw + 2],
                                  winb_[:, fc * 458 + 1:fc * 458 + 2])
            if i_ == NT - 1:
                for fc in range(FC):
                    v.tensor_copy(winb_[:, fc * 458 + w_ + 1:fc * 458 + w_ + 2],
                                  haloq[:, 2 * fc + 1:2 * fc + 2])
            return v.tensor_single_scalar(vpad[0:1, 0:1], vpad[0:1, 0:1],
                                          0.0, OP.add)
        edeps = [W[f"Bq3_{i}"], W["haloq"]]
        op("vector", f_edges, waits=edeps, inc="v")
        W[f"Bwin{i}"] = ev("v")

    def emit_ffn(i):
        w = TILES[i]
        ww = w + 2
        winb = xqw[:, (i % 2) * FC * 458:((i % 2) + 1) * FC * 458]
        x1t = x1[:, (i % 2) * FC * TILE:(i % 2) * FC * TILE + FC * TILE]
        windep = W[f"Bwin{i+1}"] if i + 1 < NT else W[f"Bwin{i}"]

        for hc in range(HC):
            bsel = hc % 2
            bank = hc % 2
            if hc >= 2:
                wt = [W[f"Fmm1_{i}_{hc-2}"]]
            elif i == 0:
                wt = bg_event(("q1_codes", 0))
            else:
                wt = [W[f"Fmm1_{i-1}_{HC - 2 + hc}"]]
            for kc in range(FC):
                op("sync", (lambda h_, b_, k_: lambda e: e.dma_start(
                    w1buf[:, b_ * 1024 + k_ * 128:b_ * 1024 + k_ * 128 + 128],
                    c1_d[k_, :, h_ * 128:(h_ + 1) * 128]))(hc, bsel, kc),
                    waits=wt if kc == 0 else [], inc="dx")
            W[f"Fw1ld{i}_{hc}"] = ev("dx")

            if True:
                def f_mm1(t, b_=bsel, bank_=bank, ww_=ww):
                    for kc in range(FC):
                        last = t.matmul(
                            ps_h[bank_][:, 0:ww_],
                            w1buf[:, b_ * 1024 + kc * 128:
                                  b_ * 1024 + kc * 128 + 128],
                            winb[:, kc * 458:kc * 458 + ww_],
                            start=(kc == 0), stop=(kc == FC - 1))
                    return last
                deps = [W[f"Fw1ld{i}_{hc}"], windep]
                if hc >= 2:
                    deps.append(W[f"Fconv{i}_{hc-2}"])
                elif i == 0:
                    deps.append(W["hh"])
                else:
                    deps.append(W[f"Fconv{i-1}_{HC-1}"])
                op("tensor", f_mm1, waits=deps, inc="t")
                W[f"Fmm1_{i}_{hc}"] = ev("t")

                def f_taps(s, hc_=hc, bank_=bank, w_=w):
                    pk = ps_h[bank_]
                    s.activation(cvt1[:, 0:w_], pk[:, 0:w_], AF.Copy,
                                 scale=convw[:, hc_:hc_ + 1])
                    s.activation(cvt2[:, 0:w_], pk[:, 1:1 + w_], AF.Copy,
                                 scale=convw[:, 32 + hc_:32 + hc_ + 1])
                    return s.activation(cvt3[:, 0:w_], pk[:, 2:2 + w_], AF.Copy,
                                        scale=convw[:, 64 + hc_:64 + hc_ + 1])
                tdep = [W[f"Fmm1_{i}_{hc}"]]
                if i == 0 and hc == 0:
                    tdep += bg_event(("convfold", 0))
                    tdep.append(W["haloq"])
                if hc >= 1:
                    tdep.append(W[f"Fconv{i}_{hc-1}"])
                elif i > 0:
                    tdep.append(W[f"Fq{i-1}_{HC - 1}"])
                op("scalar", f_taps, waits=tdep, inc="s")
                W[f"Ftap{i}_{hc}"] = ev("s")

                def f_adds(v, hc_=hc, w_=w):
                    v.scalar_tensor_tensor(out=cvt1[:, 0:w_], in0=cvt1[:, 0:w_],
                                           scalar=convw[:, 96 + hc_:96 + hc_ + 1],
                                           in1=cvt2[:, 0:w_],
                                           op0=OP.add, op1=OP.add)
                    return v.tensor_tensor(hsilu[:, hc_ * TILE:hc_ * TILE + w_],
                                           cvt1[:, 0:w_], cvt3[:, 0:w_], OP.add)
                adep = [W[f"Ftap{i}_{hc}"]]
                if hc == 0:
                    if i > 0:
                        adep.append(W[f"Fo{i-1}"])
                    else:
                        adep += bg_event(("q2_codes", 0))
                op("vector", f_adds, waits=adep, inc="v")
                W[f"Fconv{i}_{hc}"] = ev("v")

        hsv = hsilu.ap().rearrange("p (hc t) -> p hc t", hc=HC)[:, :, 0:w]
        op("scalar", (lambda a: lambda s: s.activation(a, a, AF.Silu))(hsv),
           waits=[W[f"Fconv{i}_{HC-1}"]], inc="s")
        W[f"Fsilu{i}"] = ev("s")

        def f_mxh(v, w_=w):
            return v.tensor_reduce(rmax[:, 0:w_],
                                   hsilu.ap().rearrange("p (hc t) -> p t hc", hc=HC)
                                   [:, 0:w_, :], AX, OP.max,
                                   apply_absolute_value=True)
        op("vector", f_mxh, waits=[W[f"Fsilu{i}"], W[f"Bpm{i+1}_done"]
                                   if i + 1 < NT else W[f"Bpm{i}_done"]],
           inc="v")
        W[f"Fmxh{i}"] = ev("v")
        emit_pmax_bcast(w, f"Fpm{i}", [W[f"Fmxh{i}"]])

        def f_scaleh(v, w_=w):
            return v.tensor_scalar(out=scrA[:, 0:w_], in0=rmax[:, 0:w_],
                                   scalar1=1e-5, scalar2=None, op0=OP.max)
        op("vector", f_scaleh, waits=[W[f"Fpm{i}_done"]], inc="v")
        W[f"Fuch{i}"] = ev("v")

        def f_sh(s, w_=w):
            s.activation(scrB[:, 0:w_], scrA[:, 0:w_], AF.Ln)
            return s.activation(scrB[:, 0:w_], scrB[:, 0:w_], AF.Exp,
                                scale=-1.0)
        op("scalar", f_sh, waits=[W[f"Fuch{i}"]], inc="s")
        W[f"Fsh{i}"] = ev("s")

        def f_al(v, w_=w):
            v.tensor_scalar(out=bc(5, w_), in0=scrB[:, 0:w_], scalar1=127.0,
                            scalar2=None, op0=OP.mult)
            return v.tensor_scalar(out=bc(2, w_), in0=scrA[:, 0:w_],
                                   scalar1=c2mc[:, 0:1], scalar2=None,
                                   op0=OP.mult)
        op("vector", f_al, waits=[W[f"Fsh{i}"]], inc="v")
        W[f"Fshb_{i}_bc"] = ev("v")
        W[f"Falb_{i}_bc"] = W[f"Fshb_{i}_bc"]

        NGRP = HC
        for gq in range(NGRP):
            def f_q(v, hc=gq, w_=w):
                hv = hsilu[:, hc * TILE:hc * TILE + w_]
                qv = qscr[:, 0:w_]
                kk = k16f[:, 0:w_]
                v.tensor_tensor(qv, hv, bc(5, w_), OP.mult)
                v.tensor_single_scalar(kk, qv, 128.5, OP.add)
                v.scalar_tensor_tensor(out=cvt1[:, 0:w_], in0=kk,
                                       scalar=-128.5, in1=qv,
                                       op0=OP.add, op1=OP.is_gt)
                return v.scalar_tensor_tensor(out=hv, in0=kk, scalar=-1.0,
                                              in1=cvt1[:, 0:w_], op0=OP.mult,
                                              op1=OP.add)
            op("vector", f_q, waits=[W[f"Fshb_{i}_bc"]], inc="v")
            W[f"Fq{i}_{gq}"] = ev("v")

        for oc in range(FC):
            for q8 in range(4):
                pidx = 4 * oc + q8
                bsel = pidx % 2
                if pidx >= 2:
                    wt = [W[f"Fmm2p{i}_{pidx - 2}"]]
                elif i == 0:
                    wt = bg_event(("q2_codes", 0))
                else:
                    wt = [W[f"Fmm2_{i-1}_7"]]
                for k8 in range(FC):
                    op("sync", (lambda oc_, q_, b_, k_: lambda e: e.dma_start(
                        w2buf[:, b_ * 1024 + k_ * 128:
                              b_ * 1024 + k_ * 128 + 128],
                        c2_d[q_ * 8 + k_, :, oc_ * 128:(oc_ + 1) * 128]))(
                        oc, q8, bsel, k8),
                        waits=wt if k8 == 0 else [], inc="dx")
                W[f"Fw2ld{i}_{pidx}"] = ev("dx")

                def f_mm2p(t, oc_=oc, q_=q8, b_=bsel, w_=w):
                    for k8 in range(FC):
                        kc = q_ * 8 + k8
                        last = t.matmul(ps_y[oc_ % 2][:, 0:w_],
                                        w2buf[:, b_ * 1024 + k8 * 128:
                                              b_ * 1024 + k8 * 128 + 128],
                                        hsilu[:, kc * TILE:kc * TILE + w_],
                                        start=(kc == 0), stop=(kc == HC - 1))
                    return last
                deps = [W[f"Fw2ld{i}_{pidx}"]]
                if q8 == 0:
                    deps.append(W[f"Fq{i}_{NGRP-1}"])
                    if oc >= 2:
                        deps.append(W[f"Fy2ev{i}_{oc-2}"])
                    elif i == 0:
                        deps += bg_event(("c2col", 0))
                op("tensor", f_mm2p, waits=deps, inc="t")
                W[f"Fmm2p{i}_{pidx}"] = ev("t")
            W[f"Fmm2_{i}_{oc}"] = W[f"Fmm2p{i}_{4*oc+3}"]

            def f_y2ev(v, oc_=oc, w_=w):
                return v.scalar_tensor_tensor(
                    out=fcs(ybf, oc_, w_), in0=ps_y[oc_ % 2][:, 0:w_],
                    scalar=c2col[:, oc_:oc_ + 1], in1=bc(2, w_),
                    op0=OP.add, op1=OP.mult)
            ydep = [W[f"Fmm2_{i}_{oc}"], W[f"Falb_{i}_bc"]]
            if oc == 0:
                ydep.append(W[f"Bq3_{i+1}"] if i + 1 < NT else W[f"Bq3_{i}"])
            op("vector", f_y2ev, waits=ydep, inc="v")
            W[f"Fy2ev{i}_{oc}"] = ev("v")

        sqdep = [W[f"Fy2ev{i}_7"],
                 W[f"B_ln3_{i+1}_sq"] if i + 1 < NT else W[f"B_ln3_{i}_sq"]]
        op("vector", lambda v: v.tensor_tensor(sq[:, :], ybf[:, :], ybf[:, :],
                                               OP.mult), waits=sqdep, inc="v")
        W[f"Fy2sq{i}"] = ev("v")
        emit_stats(w, lambda fc: fcs(ybf, fc, w), lambda fc: fcs(sq, fc, w),
                   f"F_ln4_{i}", W[f"Fy2ev{i}_7"], W[f"Fy2sq{i}"])
        emit_bcast(w, 0, lambda: bc(0, w), f"Fm4_{i}", deps=[W[f"F_ln4_{i}_r"]])
        emit_bcast(w, 2, lambda: bc(1, w), f"Fr4_{i}", deps=[W[f"F_ln4_{i}_r"]])

        def f_out(v, w_=w, x1t_=x1t):
            for fc in range(FC):
                y = fcs(ybf, fc, w_)
                v.tensor_tensor(y, y, bc(0, w_), OP.subtract)
                v.tensor_tensor(y, y, bc(1, w_), OP.mult)
                v.tensor_tensor(outb[:, fc * TILE:fc * TILE + w_],
                                fcs(x1t_, fc, w_), y, OP.add)
            return v.tensor_single_scalar(vpad[0:1, 0:1], vpad[0:1, 0:1],
                                          0.0, OP.add)
        odeps = [W[f"Fm4_{i}_bc"], W[f"Fr4_{i}_bc"], W[f"Bx1_{i}"]]
        if i > 0:
            odeps.append(W[f"Fo{i-1}"])
        if DBG and i == 1:
            op("vector", lambda v: v.nop(), waits=odeps, inc="v")
            W["dbg_pre_out1"] = ev("v")
            dyb = nc.dram_tensor("dump_yb1", [128, FC * TILE], BF16,
                                 kind="ExternalOutput").ap()
            dbc = nc.dram_tensor("dump_bcb1", [128, 6 * TILE], BF16,
                                 kind="ExternalOutput").ap()
            dx1 = nc.dram_tensor("dump_x1s1", [128, FC * TILE], F32,
                                 kind="ExternalOutput").ap()
            op("sync", (lambda x1t_: lambda e: e.dma_start(
                dyb, ybf[:, :]))(x1t), waits=[W["dbg_pre_out1"]], inc="dx")
            op("sync", lambda e: e.dma_start(dbc, bcb[:, :]), inc="dx")
            op("sync", (lambda x1t_: lambda e: e.dma_start(dx1, x1t_))(x1t),
               inc="dx")
            odeps = odeps + [ev("dx")]
        op("vector", f_out, waits=odeps, inc="v")
        W[f"Fout{i}"] = ev("v")

        for fc in range(FC):
            op("gpsimd", (lambda i_, w_, fc_: lambda e: e.dma_start(
                out_d.rearrange("(fc p) t -> p fc t", p=128)
                [:, fc_, TO[i_]:TO[i_] + w_],
                outb[:, fc_ * TILE:fc_ * TILE + w_]))(i, w, fc),
                waits=[W[f"Fout{i}"]], inc="do")
        W[f"Fo{i}"] = ev("do")

    emit_attn(0)
    emit_attn(1)
    for i in range(NT - 1):
        emit_ffn(i)
        if i + 2 < NT:
            emit_attn(i + 2)
    emit_ffn(NT - 1)

    if DBG:
        dump_list = [("pq", pq), ("costab", costab), ("mall", mall),
                     ("rball", rball), ("th64", th64), ("sc128", sc128),
                     ("c2col", c2col), ("convw", convw), ("x1", x1),
                     ("xqw", xqw), ("hsilu", hsilu), ("ybf", ybf),
                     ("rho1", rho1), ("negdsum", negdsum), ("thin", thin),
                     ("c2mc", c2mc), ("rmax", rmax), ("bcb", bcb),
                     ("halos", halos), ("haloq", haloq), ("negbsre", negbsre),
                     ("negbsim", negbsim), ("hsb", hsb), ("xsb", xsb),
                     ("snapA", snapA), ("snapB", snapB), ("snapU", snapU),
                     ("snapC", snapC), ("snapR", snapR), ("snapP", snapP)]
        pre_do = ("do", ctr["do"])
        for nm, t_ in dump_list:
            shp = list(t_.ap().shape) if hasattr(t_, "ap") else list(t_.shape)
            dt_ = t_.ap().dtype if hasattr(t_, "ap") else t_.dtype
            dd = nc.dram_tensor("dump_" + nm, shp, dt_, kind="ExternalOutput").ap()
            op("gpsimd", (lambda d_, a_: lambda e: e.dma_start(d_, a_))(
                dd, t_.ap() if hasattr(t_, "ap") else t_), waits=[pre_do],
                inc="do")

    final_do = ("do", ctr["do"])
    op("gpsimd", lambda e: e.nop(), waits=[final_do])
    op("sync", lambda e: e.nop(), waits=[final_do])

    do_emit()
    return nc


def build_inmaps(L0_attn_u, L0_attn_log_A_real, L0_attn_log_A_imag, L0_attn_B_w,
                 L0_attn_C_w, L0_attn_D_w, L0_ffn_w1, L0_ffn_conv_w,
                 L0_ffn_conv_b, L0_ffn_w2):
    x = np.ascontiguousarray(np.asarray(L0_attn_u, dtype=np.float32))
    bw = np.asarray(L0_attn_B_w, np.float32)
    cw = np.asarray(L0_attn_C_w, np.float32)
    dw = np.asarray(L0_attn_D_w, np.float32)
    w1 = np.asarray(L0_ffn_w1, np.float32)
    w2 = np.asarray(L0_ffn_w2, np.float32)
    cvw_ = np.asarray(L0_ffn_conv_w, np.float32).reshape(H, 3)
    cvb_ = np.asarray(L0_ffn_conv_b, np.float32)

    cvpack = np.zeros((128, 128), np.float32)
    for tap in range(3):
        cvpack[:, tap * 32:(tap + 1) * 32] = cvw_[:, tap].reshape(HC, 128).T
    cvpack[:, 96:128] = cvb_.reshape(HC, 128).T

    tpos_h = np.ascontiguousarray(
        np.broadcast_to(np.arange(NTOK + 2, dtype=np.float32),
                        (NSTATE, NTOK + 2)))
    bwT_h = np.ascontiguousarray(bw.T)
    cwT_h = np.ascontiguousarray(cw.T)
    dwT_h = np.ascontiguousarray(dw.T)
    w1T_h = np.ascontiguousarray(w1.T)
    w2T_h = np.ascontiguousarray(w2.T)
    lar_h = np.asarray(L0_attn_log_A_real, np.float32).reshape(NSTATE, 1)
    lai_h = np.asarray(L0_attn_log_A_imag, np.float32).reshape(NSTATE, 1)
    eye_f = np.eye(128, dtype=np.float32)
    eye_b = np.eye(128).astype(ml_dtypes.bfloat16)

    in_maps = []
    for core in range(NCORE):
        b = core // 2
        half = core % 2
        s0 = half * NTOK
        xm = np.zeros((D, NTOK + 2), np.float32)
        lo, hi = s0 - 1, s0 + NTOK + 1
        src_lo, src_hi = max(lo, 0), min(hi, S)
        xm[:, src_lo - lo: src_hi - lo] = x[b, src_lo:src_hi, :].T
        if half == 1:
            xp = np.ascontiguousarray(x[b, 0:NTOK, :].T)
        else:
            xp = np.zeros((D, NTOK), np.float32)
        mr = np.full((128, 1), 1.0 if half == 0 else 0.0, np.float32)
        in_maps.append(dict(
            xmain=xm, xpre=xp, tpos=tpos_h, maskr=mr, lar=lar_h, lai=lai_h,
            bwT=bwT_h, cwT=cwT_h, dwT=dwT_h, w1T=w1T_h, w2T=w2T_h, cvw=cvpack,
            idn=eye_b, idnf32=eye_f,
        ))
    return in_maps


def kernel(**inputs):
    in_maps = build_inmaps(**inputs)
    nc = build_nc()
    res = run_bass_kernel_spmd(nc, in_maps, core_ids=list(range(NCORE)))
    out = np.empty((B, S, D), np.float32)
    for core in range(NCORE):
        b = core // 2
        half = core % 2
        s0 = half * NTOK
        out[b, s0:s0 + NTOK, :] = np.asarray(res.results[core]["out"]).T
    return out


# version 06
# speedup vs baseline: 1.1480x; 1.1480x over previous
import math
import numpy as np
import ml_dtypes
import concourse.bass as bass
import concourse.mybir as mybir
from concourse.bass_utils import run_bass_kernel_spmd

F32 = mybir.dt.float32
BF16 = mybir.dt.bfloat16
I16 = mybir.dt.int16
AF = mybir.ActivationFunctionType
OP = mybir.AluOpType
AX = mybir.AxisListType.X

B, S, D, H = 4, 8192, 1024, 4096
NSTATE = 64
EPS = 1e-5
NCORE = 8
NTOK = S // 2
FC = D // 128
HC = H // 128
TILE = 456
TILES = [456] * 8 + [448]
TWO_PI = 2.0 * math.pi
QG = 2
DBG = False
EARLY_STOP = False


def build_nc():
    assert sum(TILES) == NTOK
    NT = len(TILES)
    TO = [sum(TILES[:i]) for i in range(NT)]
    NCOL = NTOK + 2
    nc = bass.Bass("TRN2", target_bir_lowering=False,
                   detect_race_conditions=False)

    xmain = nc.dram_tensor("xmain", [D, NCOL], F32, kind="ExternalInput").ap()
    xpre = nc.dram_tensor("xpre", [D, NTOK], F32, kind="ExternalInput").ap()
    tpos = nc.dram_tensor("tpos", [NSTATE, NCOL], F32, kind="ExternalInput").ap()
    maskr = nc.dram_tensor("maskr", [128, 1], F32, kind="ExternalInput").ap()
    lar_d = nc.dram_tensor("lar", [NSTATE, 1], F32, kind="ExternalInput").ap()
    lai_d = nc.dram_tensor("lai", [NSTATE, 1], F32, kind="ExternalInput").ap()
    bwT = nc.dram_tensor("bwT", [D, 128], F32, kind="ExternalInput").ap()
    cwT = nc.dram_tensor("cwT", [128, D], F32, kind="ExternalInput").ap()
    dwT = nc.dram_tensor("dwT", [D, D], F32, kind="ExternalInput").ap()
    w1T = nc.dram_tensor("w1T", [D, H], F32, kind="ExternalInput").ap()
    w2T = nc.dram_tensor("w2T", [H, D], F32, kind="ExternalInput").ap()
    cvw = nc.dram_tensor("cvw", [128, 128], F32, kind="ExternalInput").ap()
    idn_d = nc.dram_tensor("idn", [128, 128], BF16, kind="ExternalInput").ap()
    idnf_d = nc.dram_tensor("idnf32", [128, 128], F32, kind="ExternalInput").ap()
    out_d = nc.dram_tensor("out", [D, NTOK], F32, kind="ExternalOutput").ap()
    dbg_kind = "ExternalOutput" if DBG else "Internal"
    c1_f = nc.dram_tensor("c1scr", [HC, 128, FC * 128], BF16,
                          kind=dbg_kind).ap()
    c2_f = nc.dram_tensor("c2scr", [4, FC, 128, FC * 128], BF16,
                          kind=dbg_kind).ap()

    A = nc.alloc_sbuf_tensor

    costab = A("costab", [64, 2 * NCOL], BF16)
    pq = A("pq", [64, 2 * NTOK + 8], BF16)
    rho1 = A("rho1", [64, 1], F32)
    bstre = A("bstre", [128, FC * 64], BF16)
    bstim = A("bstim", [128, FC * 64], BF16)
    cstre = A("cstre", [64, FC * 128], BF16)
    cstim = A("cstim", [64, FC * 128], BF16)
    convw = A("convw", [128, 128], F32)
    mall = A("mall", [1, NTOK], BF16)
    rball = A("rball", [128, NTOK], BF16)
    onesb = A("onesb", [128, 1], BF16)
    onesr = A("onesr", [1, 128], BF16)
    negbsre = A("negbsre", [1, 64], BF16)
    negbsim = A("negbsim", [1, 64], BF16)
    negdsum = A("negdsum", [1, D], BF16)
    c2col = A("c2col", [128, 8], F32)
    thin = A("thin", [1, 3 * TILE + 16], F32)
    thinb = A("thinb", [1, 2 * TILE], BF16)
    th64 = A("th64", [64, 16], F32)
    sc128 = A("sc128", [128, 8], F32)
    maskc = A("maskc", [128, 1], F32)
    epsc = A("epsc", [128, 1], F32)
    onesf = A("onesf", [128, 1], F32)
    onesrf = A("onesrf", [1, 128], F32)
    idnb = A("idnb", [128, 128], BF16)
    idnf = A("idnf", [128, 128], F32)
    rmax = A("rmax", [128, 512], BF16)
    mstr = A("mstr", [128, 4], BF16)
    nbuf = A("nbuf", [128, 512], BF16)
    c2mc = A("c2mc", [128, 1], F32)
    w1buf = A("w1buf", [128, 2 * 1024], BF16)
    w2buf = A("w2buf", [128, 2 * 1024], BF16)
    dwbuf = A("dwbuf", [128, 2 * 1024], BF16)
    xsb = A("xsb", [128, FC * TILE], F32)
    xb = A("xb", [128, FC * TILE], BF16)
    sq = A("sq", [128, FC * TILE], BF16)
    ybf = A("ybf", [128, FC * TILE], BF16)
    x1 = A("x1", [128, 2 * FC * TILE], F32)
    xqw = A("xqw", [128, 2 * FC * (TILE + 2)], BF16)
    hsb = A("hsb", [64, 2 * TILE], BF16)
    hsilu = A("hsilu", [128, HC * TILE], BF16)
    qscr = A("qscr", [128, TILE], F32)
    k16f = A("k16f", [128, TILE], I16)
    scrA = A("scrA", [128, TILE], F32)
    scrB = A("scrB", [128, TILE], F32)
    scrC = A("scrC", [64, TILE], F32)
    cvt1 = A("cvt1", [128, TILE + 2], BF16)
    cvt2 = A("cvt2", [128, TILE + 2], BF16)
    cvt3 = A("cvt3", [128, TILE + 2], BF16)
    bcb = A("bcb", [128, 6 * TILE], BF16)
    haloq = A("haloq", [128, 2 * FC], BF16)
    halos = A("halos", [128, 48], F32)
    vpad = A("vpad", [1, 64], F32)
    if DBG:
        snapA = A("snapA", [64, 16], F32)
        snapB = A("snapB", [64, 16], F32)
        snapU = A("snapU", [64, 64], F32)
        snapC = A("snapC", [64, 256], BF16)
        snapR = A("snapR", [64, 1], F32)
        snapP = A("snapP", [64, 160], F32)

    rem = nc.sbuf_bytes_remaining
    if DBG:
        print("sbuf bytes remaining:", rem)
    assert rem >= 0, f"SBUF over budget by {-rem}"

    k16d = sq.ap().bitcast(I16)
    outb = hsilu.ap().bitcast(F32)
    wqbuf = hsilu.ap().bitcast(F32)
    tposf = costab.ap().bitcast(F32)
    x1f = x1.ap()
    tk16 = pq.ap().bitcast(I16)
    tcor = pq.ap()

    P = nc.alloc_psum_tensor
    ps_h = [P("ps_h0", [128, TILE + 2], F32), P("ps_h1", [128, TILE + 2], F32)]
    ps_y = [P("ps_y0", [128, TILE], F32), P("ps_y1", [128, TILE], F32)]
    ps_D = P("ps_D", [128, TILE], F32)
    ps_C = P("ps_C", [128, TILE], F32)
    ps_S = P("ps_S", [128, TILE], F32)
    ps_X = P("ps_X", [128, TILE], F32)

    ctr = {k: 0 for k in ("t", "v", "s", "g", "dx", "dw", "do")}
    plan = {e: [] for e in ("tensor", "vector", "scalar", "gpsimd", "sync")}

    def op(engine, fn, waits=(), inc=None):
        flat = []
        for w_ in waits:
            if isinstance(w_, list):
                flat.extend(w_)
            elif w_ is not None:
                flat.append(w_)
        plan[engine].append((flat, fn, inc))
        if inc is not None:
            ctr[inc] += 16 if inc.startswith("d") else 1

    def ev(name):
        return (name, ctr[name])

    def do_emit():
        with (
            nc.semaphore("sem_t") as s_t, nc.semaphore("sem_v") as s_v,
            nc.semaphore("sem_s") as s_s, nc.semaphore("sem_g") as s_g,
            nc.semaphore("sem_dx") as s_dx, nc.semaphore("sem_dw") as s_dw,
            nc.semaphore("sem_do") as s_do,
            nc.Block() as block,
        ):
            sem_handles = dict(t=s_t, v=s_v, s=s_s, g=s_g, dx=s_dx, dw=s_dw,
                               do=s_do)

            def run_engine(engname):
                def body(eng):
                    for waits, fn, inc in plan[engname]:
                        seen = {}
                        for semn, val in waits:
                            if val > 0 and seen.get(semn, -1) < val:
                                seen[semn] = val
                        for semn, val in seen.items():
                            eng.wait_ge(sem_handles[semn], val)
                        inst = fn(eng)
                        if inc is not None:
                            step = 16 if inc.startswith("d") else 1
                            assert inst is not None
                            inst.then_inc(sem_handles[inc], step)
                return body

            block.tensor(run_engine("tensor"))
            block.vector(run_engine("vector"))
            block.scalar(run_engine("scalar"))
            block.gpsimd(run_engine("gpsimd"))
            block.sync(run_engine("sync"))

    class VP:

        def __init__(self, v):
            self._v = v

        def __getattr__(self, name):
            real = getattr(self._v, name)

            def wrap(*a, **k):
                inst = real(*a, **k)
                self._v.tensor_single_scalar(vpad[0:1, :], vpad[0:1, :],
                                             0.0, OP.add)
                return inst
            return wrap

    W = {}
    mini_k = [0]

    def mini_scan(*deps):
        if not DBG:
            return
        k = mini_k[0]
        mini_k[0] += 1

        def f_ms_(v, k_=k):
            vv = VP(v)
            vv.tensor_copy(snapP[:, k_ * 16:k_ * 16 + 8],
                           rho1[:, :].broadcast_to([64, 8]))
            return vv.tensor_tensor_scan(
                snapP[:, k_ * 16 + 8:k_ * 16 + 16],
                snapP[:, k_ * 16:k_ * 16 + 8],
                onesf[0:64, :].broadcast_to([64, 8]), 0.0, OP.mult, OP.add)
        op("vector", f_ms_, waits=list(deps), inc="v")

    lastS = [None]
    lastX = [None]

    def slot(i, w=TILE):
        return thin[:, i * TILE: i * TILE + w]

    def bc(i, w=TILE):
        return bcb[:, i * TILE: i * TILE + w]

    def fcs(buf, fc, w, grid=TILE):
        return buf[:, fc * grid: fc * grid + w]

    def emit_stats(w, mov, movsq, tag, mov_ev, sq_ev):
        def f_mm1(t):
            for fc in range(FC):
                last = t.matmul(ps_S[0:1, 0:w], onesb[:, :], mov(fc),
                                start=(fc == 0), stop=(fc == FC - 1))
            return last
        op("tensor", f_mm1, waits=[mov_ev, lastS[0]], inc="t")
        W[f"{tag}_sx"] = ev("t")
        op("scalar", (lambda w_: lambda s: s.copy(slot(0, w_), ps_S[0:1, 0:w_]))(w),
           waits=[W[f"{tag}_sx"]], inc="s")
        W[f"{tag}_sxe"] = ev("s")

        def f_mm2(t):
            for fc in range(FC):
                last = t.matmul(ps_S[0:1, 0:w], onesb[:, :], movsq(fc),
                                start=(fc == 0), stop=(fc == FC - 1))
            return last
        op("tensor", f_mm2, waits=[W[f"{tag}_sxe"], sq_ev], inc="t")
        W[f"{tag}_sq"] = ev("t")
        op("scalar", (lambda w_: lambda s: s.copy(slot(1, w_), ps_S[0:1, 0:w_]))(w),
           waits=[W[f"{tag}_sq"]], inc="s")
        W[f"{tag}_sqe"] = ev("s")
        lastS[0] = W[f"{tag}_sqe"]

        def f_thin(v, w_=w):
            m, q, r = slot(0, w_), slot(1, w_), slot(2, w_)
            v.tensor_single_scalar(m, m, 1.0 / D, OP.mult)
            v.tensor_single_scalar(q, q, 1.0 / D, OP.mult)
            v.tensor_tensor(r, m, m, OP.mult)
            return v.tensor_tensor(q, q, r, OP.subtract)
        op("vector", f_thin, waits=[W[f"{tag}_sqe"]], inc="v")
        W[f"{tag}_var"] = ev("v")

        def f_r(s, w_=w):
            q = slot(1, w_)
            s.activation(q, q, AF.Ln, bias=epsc[0:1, :])
            return s.activation(slot(2, w_), q, AF.Exp, scale=-0.5)
        op("scalar", f_r, waits=[W[f"{tag}_var"]], inc="s")
        W[f"{tag}_r"] = ev("s")

    def emit_bcast(w, sl, dstf, tag, deps=()):
        def f_cp(v, w_=w, sl_=sl):
            return v.tensor_copy(thinb[0:1, TILE:TILE + w_], slot(sl_, w_))
        op("vector", f_cp, waits=list(deps) + [lastX[0]], inc="v")
        W[f"{tag}_bm"] = ev("v")
        op("tensor", (lambda w_: lambda t: t.matmul(
            ps_X[:, 0:w_], onesr[:, :], thinb[0:1, TILE:TILE + w_],
            start=True, stop=True))(w),
            waits=[W[f"{tag}_bm"], lastX[0]], inc="t")
        W[f"{tag}_bo"] = ev("t")
        op("scalar", (lambda w_: lambda s: s.copy(dstf(), ps_X[:, 0:w_]))(w),
           waits=[W[f"{tag}_bo"]], inc="s")
        W[f"{tag}_bc"] = ev("s")
        lastX[0] = W[f"{tag}_bc"]

    def emit_pmax_bcast(w, tag, deps):
        nblk = (w + 127) // 128

        psb = ps_X[:, 0:64].bitcast(BF16)

        def f_fwd(t, b):
            return t.transpose(psb, rmax[:, b * 128:(b + 1) * 128],
                               idnb[:, :])
        for b in range(nblk):
            op("tensor", (lambda b_: lambda t: f_fwd(t, b_))(b),
               waits=list(deps) + [lastX[0], W["p0ld"]], inc="t")
            W[f"{tag}_f{b}"] = ev("t")
            op("vector", (lambda b_: lambda v: v.tensor_reduce(
                mstr[:, b_:b_ + 1], psb, AX, OP.max))(b),
                waits=[W[f"{tag}_f{b}"]], inc="v")
            W[f"{tag}_r{b}"] = ev("v")
            lastX[0] = W[f"{tag}_r{b}"]

        def f_nexp(v):
            return v.tensor_copy(
                nbuf[:, 0:nblk * 128].rearrange("p (b r) -> p b r", b=nblk),
                mstr[:, 0:nblk].unsqueeze(2).broadcast_to([128, nblk, 128]))
        op("vector", f_nexp, waits=[W[f"{tag}_r{nblk-1}"]], inc="v")
        W[f"{tag}_n"] = ev("v")

        for b in range(nblk):
            op("tensor", (lambda b_: lambda t: t.transpose(
                psb, nbuf[:, b_ * 128:(b_ + 1) * 128],
                idnb[:, :]))(b), waits=[W[f"{tag}_n"], lastX[0]], inc="t")
            W[f"{tag}_g{b}"] = ev("t")
            hi = min(128, w - b * 128)
            op("scalar", (lambda b_, h_: lambda s: s.copy(
                rmax[:, b_ * 128:b_ * 128 + h_], psb[:, 0:h_]))(b, hi),
                waits=[W[f"{tag}_g{b}"]], inc="s")
            W[f"{tag}_e{b}"] = ev("s")
            lastX[0] = W[f"{tag}_e{b}"]
        W[f"{tag}_done"] = W[f"{tag}_e{nblk-1}"]

    def f_ms(e):
        for t_ in (rmax, nbuf, mstr, hsilu, halos, scrA, scrB, scrC, cvt1,
                   cvt2, cvt3, bcb, ybf, sq, xb, qscr, k16f, x1, xqw, pq,
                   thin, thinb, sc128, c2col, haloq, th64, xsb, hsb, costab, vpad,
                   mall, rball, negdsum, w1buf, w2buf, dwbuf):
            last = e.memset(t_.ap() if hasattr(t_, "ap") else t_, 0)
        return last
    op("gpsimd", f_ms, inc="g")
    W["msall"] = ev("g")

    op("sync", lambda e: e.dma_start(tposf[:, 0:NCOL], tpos),
       waits=[W["msall"]], inc="dx")
    W["tposld"] = ev("dx")
    op("sync", lambda e: e.dma_start(th64[:, 0:1], lar_d),
       waits=[W["msall"]], inc="dx")
    op("sync", lambda e: e.dma_start(th64[:, 1:2], lai_d), inc="dx")
    op("sync", lambda e: e.dma_start(convw[:, :], cvw), inc="dx")
    op("sync", lambda e: e.dma_start(maskc[:, :], maskr), inc="dx")
    op("sync", lambda e: e.dma_start(idnb[:, :], idn_d), inc="dx")
    op("sync", lambda e: e.dma_start(idnf[:, :], idnf_d), inc="dx")
    W["p0ld"] = ev("dx")

    bre = bwT.rearrange("(kc p) n -> p kc n", p=128)
    for kc in range(FC):
        op("gpsimd", (lambda k_: lambda e: e.dma_start(
            bstre[:, k_ * 64:(k_ + 1) * 64], bre[:, k_, 0:64]))(kc),
            inc="dw")
        op("gpsimd", (lambda k_: lambda e: e.dma_start(
            bstim[:, k_ * 64:(k_ + 1) * 64], bre[:, k_, 64:128]))(kc),
            inc="dw")
    op("gpsimd", lambda e: e.dma_start(cstre[:, :], cwT[0:64, :]), inc="dw")
    op("gpsimd", lambda e: e.dma_start(cstim[:, :], cwT[64:128, :]), inc="dw")
    W["bcld"] = ev("dw")
    op("gpsimd", lambda e: e.memset(onesb[:, :], 1.0), inc="g")
    op("gpsimd", lambda e: e.memset(onesr[:, :], 1.0), inc="g")
    op("gpsimd", lambda e: e.memset(epsc[:, :], EPS), inc="g")
    op("gpsimd", lambda e: e.memset(onesf[:, :], 1.0), inc="g")
    op("gpsimd", lambda e: e.memset(onesrf[:, :], 1.0), inc="g")
    W["ones"] = ev("g")

    c = th64
    op("scalar", lambda s: s.activation(c[:, 2:3], c[:, 0:1], AF.Exp),
       waits=[W["p0ld"]], inc="s")
    W["p0exp"] = ev("s")

    def f_zden(v):
        v = VP(v)
        v.tensor_single_scalar(c[:, 2:3], c[:, 2:3], -0.5, OP.mult)
        v.tensor_single_scalar(c[:, 3:4], c[:, 1:2], 0.5, OP.mult)
        v.tensor_scalar(out=c[:, 4:5], in0=c[:, 2:3], scalar1=-1.0,
                        scalar2=1.0, op0=OP.mult, op1=OP.add)
        v.tensor_tensor(c[:, 4:5], c[:, 4:5], c[:, 4:5], OP.mult)
        v.tensor_tensor(c[:, 5:6], c[:, 3:4], c[:, 3:4], OP.mult)
        v.tensor_tensor(c[:, 4:5], c[:, 4:5], c[:, 5:6], OP.add)
        v.tensor_tensor(c[:, 6:7], c[:, 2:3], c[:, 2:3], OP.mult)
        v.tensor_tensor(c[:, 6:7], c[:, 6:7], c[:, 5:6], OP.add)
        return v.tensor_scalar(out=c[:, 6:7], in0=c[:, 6:7], scalar1=-1.0,
                               scalar2=1.0, op0=OP.mult, op1=OP.add)
    op("vector", f_zden, waits=[W["p0exp"]], inc="v")
    W["p0den"] = ev("v")

    def f_invden(s):
        s.activation(c[:, 4:5], c[:, 4:5], AF.Ln)
        return s.activation(c[:, 4:5], c[:, 4:5], AF.Exp, scale=-1.0)
    op("scalar", f_invden, waits=[W["p0den"]], inc="s")
    W["p0invden"] = ev("s")

    def f_A(v):
        v = VP(v)
        v.tensor_tensor(c[:, 6:7], c[:, 6:7], c[:, 4:5], OP.mult)
        v.tensor_single_scalar(c[:, 7:8], c[:, 3:4], 2.0, OP.mult)
        v.tensor_tensor(c[:, 7:8], c[:, 7:8], c[:, 4:5], OP.mult)
        v.tensor_tensor(c[:, 2:3], c[:, 6:7], c[:, 6:7], OP.mult)
        v.tensor_tensor(c[:, 5:6], c[:, 7:8], c[:, 7:8], OP.mult)
        return v.tensor_tensor(c[:, 5:6], c[:, 5:6], c[:, 2:3], OP.add)
    op("vector", f_A, waits=[W["p0invden"]], inc="v")
    W["p0A"] = ev("v")

    def f_Aclamp(v):
        v = VP(v)
        v.tensor_single_scalar(c[:, 5:6], c[:, 5:6], 1e-38, OP.max)
        v.tensor_single_scalar(c[:, 8:9], c[:, 6:7], -1.0, OP.mult)
        v.tensor_tensor(c[:, 8:9], c[:, 8:9], c[:, 6:7], OP.max)
        return v.tensor_single_scalar(c[:, 8:9], c[:, 8:9], 1e-38, OP.max)
    op("vector", f_Aclamp, waits=[W["p0A"]], inc="v")
    W["p0Ac"] = ev("v")

    def f_rho(s):
        s.activation(c[:, 2:3], c[:, 5:6], AF.Ln)
        s.activation(c[:, 5:6], c[:, 2:3], AF.Exp, scale=0.5)
        s.activation(c[:, 2:3], c[:, 8:9], AF.Ln)
        return s.activation(c[:, 2:3], c[:, 2:3], AF.Exp, scale=-1.0)
    op("scalar", f_rho, waits=[W["p0Ac"]], inc="s")
    W["p0rho"] = ev("s")

    def f_atanarg(v):
        v.tensor_copy(rho1[:, :], c[:, 5:6])
        return v.tensor_tensor(c[:, 2:3], c[:, 7:8], c[:, 2:3], OP.mult)
    op("vector", f_atanarg, waits=[W["p0rho"]], inc="v")
    W["p0atanarg"] = ev("v")

    def f_atan(s):
        s.activation(c[:, 3:4], c[:, 7:8], AF.Sign)
        return s.activation(c[:, 2:3], c[:, 2:3], AF.Arctan)
    op("scalar", f_atan, waits=[W["p0atanarg"]], inc="s")
    W["p0atan"] = ev("s")

    def f_theta(v):
        v = VP(v)
        v.tensor_single_scalar(c[:, 4:5], c[:, 6:7], 0.0, OP.is_lt)
        v.tensor_scalar(out=c[:, 6:7], in0=c[:, 4:5], scalar1=-2.0,
                        scalar2=1.0, op0=OP.mult, op1=OP.add)
        v.tensor_tensor(c[:, 6:7], c[:, 6:7], c[:, 2:3], OP.mult)
        v.tensor_tensor(c[:, 4:5], c[:, 4:5], c[:, 3:4], OP.mult)
        v.tensor_single_scalar(c[:, 4:5], c[:, 4:5], math.pi, OP.mult)
        return v.tensor_tensor(c[:, 6:7], c[:, 6:7], c[:, 4:5], OP.add)
    op("vector", f_theta, waits=[W["p0atan"]], inc="v")
    W["p0theta"] = ev("v")
    if DBG:
        op("vector", lambda v: v.tensor_copy(snapA[:, :], c[:, :]),
           waits=[W["p0theta"]], inc="v")
        op("vector", lambda v: v.tensor_copy(snapR[:, :], rho1[:, :]),
           waits=[W["p0theta"]], inc="v")

    def f_tab_u(v):
        tp = tposf[0:64, 0:NCOL]
        return v.tensor_scalar(out=tp, in0=tp, scalar1=c[:, 6:7],
                               scalar2=None, op0=OP.mult)
    op("vector", f_tab_u, waits=[W["p0theta"], W["tposld"]], inc="v")
    W["tab_u"] = ev("v")

    def f_args(v, phase, w_):
        u = tposf[0:64, 0:NCOL]
        kk = tk16[0:64, 4102:4102 + NCOL]
        cr = tcor[0:64, 0:NCOL]
        v.tensor_scalar(out=w_, in0=u, scalar1=1.0 / TWO_PI,
                        scalar2=2048.5 + phase, op0=OP.mult, op1=OP.add)
        v.tensor_single_scalar(kk, w_, 0.0, OP.add)
        v.scalar_tensor_tensor(out=cr, in0=kk, scalar=0.0, in1=w_,
                               op0=OP.add, op1=OP.is_gt)
        v.tensor_tensor(w_, w_, kk, OP.subtract)
        v.tensor_tensor(w_, w_, cr, OP.add)
        return v.tensor_scalar(out=w_, in0=w_, scalar1=TWO_PI,
                               scalar2=-math.pi, op0=OP.mult, op1=OP.add)

    wcos = x1f[0:64, 0:NCOL]
    wsin = tposf[0:64, 0:NCOL]
    op("vector", lambda v: f_args(v, 0.25, wcos), waits=[W["tab_u"]], inc="v")
    W["tabcos_arg"] = ev("v")
    op("vector", lambda v: f_args(v, 0.0, wsin), waits=[W["tabcos_arg"]],
       inc="v")
    W["tabsin_arg"] = ev("v")
    op("scalar", lambda s: s.activation(tcor[0:64, 0:NCOL], wsin, AF.Sin),
       waits=[W["tabsin_arg"]], inc="s")
    W["tabsin_b"] = ev("s")
    op("scalar", lambda s: s.activation(costab[0:64, 0:NCOL], wcos, AF.Sin),
       waits=[W["tabsin_b"]], inc="s")
    W["tabcos_done"] = ev("s")
    op("vector", lambda v: v.tensor_copy(costab[0:64, NCOL:2 * NCOL],
                                         tcor[0:64, 0:NCOL]),
       waits=[W["tabcos_done"]], inc="v")
    W["tables"] = ev("v")

    mini_scan(W["tables"])

    def f_negb_mm(t):
        for kc in range(FC):
            t.matmul(ps_X[0:1, 0:64], onesb[:, :],
                     bstre[:, kc * 64:(kc + 1) * 64],
                     start=(kc == 0), stop=(kc == FC - 1))
        for kc in range(FC):
            last = t.matmul(ps_X[0:1, 64:128], onesb[:, :],
                            bstim[:, kc * 64:(kc + 1) * 64],
                            start=(kc == 0), stop=(kc == FC - 1))
        return last
    op("tensor", f_negb_mm, waits=[W["bcld"], W["ones"]], inc="t")
    W["negb_mm"] = ev("t")

    def f_negb_ev(v):
        v.tensor_single_scalar(negbsre[:, :], ps_X[0:1, 0:64], -1.0, OP.mult)
        return v.tensor_single_scalar(negbsim[:, :], ps_X[0:1, 64:128],
                                      -1.0, OP.mult)
    op("vector", f_negb_ev, waits=[W["negb_mm"]], inc="v")
    W["negb"] = ev("v")
    lastX[0] = W["negb"]

    dre = dwT.rearrange("(kc p) e -> p kc e", p=128)

    def emit_dpiece(oc, bsel, waits):
        for kc in range(FC):
            op("gpsimd", (lambda o_, b_, k_: lambda e: e.dma_start(
                dwbuf[:, b_ * 1024 + k_ * 128:b_ * 1024 + k_ * 128 + 128],
                dre[:, k_, o_ * 128:(o_ + 1) * 128]))(oc, bsel, kc),
                waits=waits if kc == 0 else [], inc="dw")

    for g in range(FC):
        bsel = g % 2
        wt = [W[f"negd_mm{g - 2}"]] if g >= 2 else [W["bcld"]]
        emit_dpiece(g, bsel, wt)
        W[f"negd_ld{g}"] = ev("dw")

        def f_mm(t, b_=bsel):
            for kc in range(FC):
                last = t.matmul(ps_X[0:1, 0:128], onesb[:, :],
                                dwbuf[:, b_ * 1024 + kc * 128:
                                      b_ * 1024 + kc * 128 + 128],
                                start=(kc == 0), stop=(kc == FC - 1))
            return last
        op("tensor", f_mm, waits=[W[f"negd_ld{g}"], W["ones"], lastX[0]],
           inc="t")
        W[f"negd_mm{g}"] = ev("t")
        op("vector", (lambda g_: lambda v: v.tensor_single_scalar(
            negdsum[:, g_ * 128:(g_ + 1) * 128], ps_X[0:1, 0:128], -1.0,
            OP.mult))(g), waits=[W[f"negd_mm{g}"]], inc="v")
        W[f"negd_ev{g}"] = ev("v")
        lastX[0] = W[f"negd_ev{g}"]
    W["negd"] = W[f"negd_ev{FC-1}"]
    mini_scan(W["negd"])

    if DBG and EARLY_STOP:
        for nm, t_ in [("snapA", snapA), ("snapU", snapU), ("snapC", snapC),
                       ("snapR", snapR), ("th64", th64), ("rho1", rho1),
                       ("costab", costab), ("snapP", snapP)]:
            shp = list(t_.ap().shape)
            dt_ = t_.ap().dtype
            dd = nc.dram_tensor("dump_" + nm, shp, dt_,
                                kind="ExternalOutput").ap()
            op("gpsimd", (lambda d_, a_: lambda e: e.dma_start(d_, a_))(
                dd, t_.ap()), waits=[W["tables"], W["negd"], W["negb"]],
                inc="do")
        final_do = ("do", ctr["do"])
        op("gpsimd", lambda e: e.nop(), waits=[final_do])
        op("sync", lambda e: e.nop(), waits=[final_do])
        do_emit()
        return nc

    bg = []
    bg_marks = {}

    def bgop(engine, fn, waits=(), inc=None):
        bg.append((engine, fn, list(waits), inc))

    def drain(k):
        n = 0
        while bg and (k < 0 or n < k):
            engine, fn, waits, inc = bg.pop(0)
            if engine == "_mark":
                bg_marks[fn] = dict(ctr)
                continue
            rw = []
            for w_ in waits:
                if isinstance(w_, tuple) and w_[0] in ctr:
                    rw.append(w_)
                else:
                    snap = bg_marks.get(w_)
                    assert snap is not None, f"unresolved bg mark {w_}"
                    rw.extend((sn, val) for sn, val in snap.items() if val > 0)
            op(engine, fn, waits=rw, inc=inc)
            n += 1

    def bg_event(key):
        snap = bg_marks.get(key)
        assert snap is not None, f"bg mark {key} not drained"
        return [(sn, val) for sn, val in snap.items() if val > 0]

    WQP = 2048

    def emit_mean(src, nrow, ncol, tag, slot_i):
        WQ = min(WQP, ncol)
        nrb, ncb = nrow // 128, ncol // WQ
        npieces = nrb * ncb
        assert npieces + 101 <= TILE
        for rb in range(nrb):
            for cb in range(ncb):
                buf = rb * ncb + cb
                dst = wqbuf[:, (buf % 2) * WQ:(buf % 2) * WQ + WQ]
                piece = src[rb * 128:(rb + 1) * 128, cb * WQ:(cb + 1) * WQ]
                wt = [(f"{tag}_rd", buf - 2)] if buf >= 2 else [W["msall"]]
                bgop("gpsimd", (lambda d, p: lambda e: e.dma_start(d, p))(dst, piece),
                     waits=wt, inc="dw")
                bgop("_mark", (f"{tag}_ld", buf), [], None)

                def f_red(v, d=dst, b=buf):
                    return v.tensor_reduce(qscr[:, b:b + 1], d, AX, OP.add,
                                           apply_absolute_value=True)
                rw = [(f"{tag}_ld", buf)]
                if tag == "m2" and buf < 2:
                    rw.append(("m1_pm", 0))
                bgop("vector", f_red, waits=rw, inc="v")
                bgop("_mark", (f"{tag}_rd", buf), [], None)

        bgop("tensor", (lambda np_: lambda t: t.matmul(
            ps_y[0][0:1, 0:np_], onesf[:, :], qscr[:, 0:np_],
            start=True, stop=True))(npieces),
             waits=[W["ones"], (f"{tag}_rd", nrb * ncb - 1)], inc="t")
        bgop("_mark", (f"{tag}_pm", 0), [], None)
        bgop("scalar", (lambda np_: lambda s: s.copy(
            qscr[0:1, 101:101 + np_], ps_y[0][0:1, 0:np_]))(npieces),
             waits=[(f"{tag}_pm", 0)], inc="s")
        bgop("_mark", (f"{tag}_pe", 0), [], None)

        def f_tree(v, n=nrow * ncol, si=slot_i, np_=npieces):
            vv = VP(v)
            vv.tensor_reduce(qscr[0:1, 100:101], qscr[0:1, 101:101 + np_],
                             AX, OP.add)
            return vv.tensor_scalar(
                out=thin[0:1, 3 * TILE + si:3 * TILE + si + 1],
                in0=qscr[0:1, 100:101], scalar1=1.0 / n, scalar2=1e-5,
                op0=OP.mult, op1=OP.max)
        bgop("vector", f_tree, waits=[(f"{tag}_pe", 0)], inc="v")
        bgop("_mark", (f"{tag}_mean", 0), [], None)

    emit_mean(w1T, D, H, "m1", 0)
    emit_mean(w2T, H, D, "m2", 1)

    def f_thr(v):
        t0 = thin[0:1, 3 * TILE + 0:3 * TILE + 1]
        t1 = thin[0:1, 3 * TILE + 1:3 * TILE + 2]
        v.tensor_single_scalar(thin[0:1, 3 * TILE + 2:3 * TILE + 3], t0, 0.5, OP.mult)
        v.tensor_single_scalar(thin[0:1, 3 * TILE + 3:3 * TILE + 4], t0, -0.5, OP.mult)
        v.tensor_single_scalar(thin[0:1, 3 * TILE + 4:3 * TILE + 5], t1, 0.5, OP.mult)
        return v.tensor_single_scalar(thin[0:1, 3 * TILE + 5:3 * TILE + 6], t1, -0.5, OP.mult)
    bgop("vector", f_thr, waits=[(f"m2_mean", 0)], inc="v")
    bgop("_mark", ("thrrow", 0), [], None)
    bgop("tensor", lambda t: t.matmul(ps_D[:, 0:6], onesrf[:, :],
                                      thin[0:1, 3 * TILE:3 * TILE + 6],
                                      start=True, stop=True),
         waits=[("thrrow", 0)], inc="t")
    bgop("_mark", ("thrmm", 0), [], None)
    bgop("scalar", lambda s: s.copy(sc128[:, 0:6], ps_D[:, 0:6]),
         waits=[("thrmm", 0)], inc="s")
    bgop("_mark", ("thrcol", 0), [], None)
    bgop("vector", lambda v: v.tensor_scalar(
        out=convw[:, 0:96], in0=convw[:, 0:96], scalar1=sc128[:, 0:1],
        scalar2=None, op0=OP.mult), waits=[("thrcol", 0), W["p0ld"]], inc="v")
    bgop("_mark", ("convfold", 0), [], None)

    def emit_quant(src, dstf, nrow, ncol, thrP, thrN, tag, rowsum, ncw):
        nrb, ncb = nrow // 128, ncol // ncw
        for rb in range(nrb):
            for cb in range(ncb):
                buf = rb * ncb + cb
                dst = wqbuf[:, (buf % 2) * ncw:(buf % 2) * ncw + ncw]
                piece = src[rb * 128:(rb + 1) * 128, cb * ncw:(cb + 1) * ncw]
                wt = [(f"{tag}_q", buf - 2)] if buf >= 2 else [("thrcol", 0)]
                bgop("gpsimd", (lambda d, p: lambda e: e.dma_start(d, p))(dst, piece),
                     waits=wt, inc="dw")
                bgop("_mark", (f"{tag}_l", buf), [], None)

                cps = hsilu[:, 8192:8192 + ncw]
                cms = hsilu[:, 8192 + 2048:8192 + 2048 + ncw]

                def f_q(v, d=dst, cp=cps, cm=cms):
                    v.tensor_scalar(out=cp, in0=d, scalar1=sc128[:, thrP:thrP + 1],
                                    scalar2=None, op0=OP.is_gt)
                    v.tensor_scalar(out=cm, in0=d, scalar1=sc128[:, thrN:thrN + 1],
                                    scalar2=None, op0=OP.is_lt)
                    return v.tensor_tensor(cp, cp, cm, OP.subtract)
                stw = [(f"{tag}_l", buf)] + ([(f"{tag}_st", buf - 1)] if buf >= 1 else [])
                bgop("vector", f_q, waits=stw, inc="v")
                bgop("_mark", (f"{tag}_q", buf), [], None)

                if rowsum:
                    def f_rs(t, cp=cps):
                        for oc in range(FC):
                            last = t.matmul(ps_C[:, oc:oc + 1],
                                            cp[:, oc * 128:(oc + 1) * 128],
                                            onesb[:, :],
                                            start=True, stop=True)
                        return last
                    rw = [(f"{tag}_q", buf)]
                    if buf > 0:
                        rw.append((f"{tag}_ra", buf - 1))
                    bgop("tensor", f_rs, waits=rw, inc="t")
                    bgop("_mark", (f"{tag}_rs", buf), [], None)

                    def f_ra(v, fst=(buf == 0)):
                        if fst:
                            return v.tensor_copy(c2col[:, 0:8], ps_C[:, 0:8])
                        return v.tensor_tensor(c2col[:, 0:8], c2col[:, 0:8],
                                               ps_C[:, 0:8], OP.add)
                    bgop("vector", f_ra, waits=[(f"{tag}_rs", buf)], inc="v")
                    bgop("_mark", (f"{tag}_ra", buf), [], None)
                    stdep = (f"{tag}_rs", buf)
                else:
                    stdep = (f"{tag}_q", buf)
                for dsto, csl in dstf(rb, cb):
                    bgop("gpsimd", (lambda o, s_: lambda e: e.dma_start(
                        o, s_))(dsto, cps[:, csl]), waits=[stdep], inc="dw")
                bgop("_mark", (f"{tag}_st", buf), [], None)
        bgop("_mark", (f"{tag}_codes", 0), [], None)

    emit_quant(w1T,
               lambda rb, cb: [(c1_f[cb * 16 + j, :, rb * 128:(rb + 1) * 128],
                                slice(j * 128, (j + 1) * 128))
                               for j in range(16)],
               D, H, 2, 3, "q1", False, WQP)
    emit_quant(w2T,
               lambda rb, cb: [(c2_f[rb // 8, oc, :,
                                     (rb % 8) * 128:(rb % 8) * 128 + 128],
                                slice(oc * 128, (oc + 1) * 128))
                               for oc in range(FC)],
               H, D, 4, 5, "q2", True, 1024)
    bgop("vector", lambda v: v.tensor_single_scalar(
        c2col[:, 0:8], c2col[:, 0:8], 128.0, OP.mult),
        waits=[("q2_ra", HC - 1)], inc="v")
    bgop("_mark", ("c2e", 0), [], None)
    bgop("vector", lambda v: v.tensor_scalar(
        out=thinb[0:1, 2 * TILE - 8:2 * TILE - 7],
        in0=thin[0:1, 3 * TILE + 1:3 * TILE + 2],
        scalar1=-1.0 / 127.0, scalar2=None, op0=OP.mult), inc="v")
    bgop("_mark", ("c2mb", 0), [], None)
    bgop("tensor", lambda t: t.matmul(ps_y[1][:, 0:1], onesr[:, :],
                                      thinb[0:1, 2 * TILE - 8:2 * TILE - 7],
                                      start=True, stop=True),
         waits=[("c2mb", 0)], inc="t")
    bgop("_mark", ("c2mm", 0), [], None)
    bgop("scalar", lambda s: s.copy(c2mc[:, :], ps_y[1][:, 0:1]),
         waits=[("c2mm", 0)], inc="s")
    bgop("_mark", ("c2col", 0), [], None)

    def emit_passA(i, w, src, is_main, tag, first_dep):
        j0 = TO[i]
        dep = [W[f"{tag}A{i-1}_xb"]] if i > 0 else list(first_dep)
        for fc in range(FC):
            op("sync", (lambda s_, w_, fc_: lambda e: e.dma_start(
                xsb[:, fc_ * TILE:fc_ * TILE + w_], s_[:, fc_, :]))(
                src, w, fc), waits=dep if fc == 0 else [], inc="dx")
        W[f"{tag}A{i}_x"] = ev("dx")

        def f_cast(v):
            v.tensor_copy(xb[:, :], xsb[:, :])
            return v.tensor_tensor(sq[:, :], xb[:, :], xb[:, :], OP.mult)
        cdep = [W[f"{tag}A{i}_x"]]
        if i > 0:
            cdep.append(W[f"{tag}A{i-1}_bu"])
        elif is_main:
            cdep.append(W[f"PA{NT-1}_bu"])
        op("vector", f_cast, waits=cdep, inc="v")
        W[f"{tag}A{i}_xb"] = ev("v")

        emit_stats(w, lambda fc: fcs(xb, fc, w), lambda fc: fcs(sq, fc, w),
                   f"{tag}st{i}", W[f"{tag}A{i}_xb"], W[f"{tag}A{i}_xb"])

        def f_mrow(v):
            if is_main:
                return v.tensor_copy(mall[:, j0:j0 + w], slot(0, w))
            return v.tensor_copy(thinb[0:1, 0:w], slot(0, w))
        op("vector", f_mrow, waits=[W[f"{tag}st{i}_sxe"]], inc="v")
        W[f"{tag}A{i}_mrow"] = ev("v")

        rdst = (lambda: rball[:, j0:j0 + w]) if is_main else (lambda: bc(5, w))
        emit_bcast(w, 2, rdst, f"{tag}A{i}r", deps=[W[f"{tag}st{i}_r"]])

        mrow = (lambda: mall[:, j0:j0 + w]) if is_main else (lambda: thinb[0:1, 0:w])

        def f_bmm(t):
            for kc in range(FC):
                t.matmul(ps_h[0][0:64, 0:w], bstre[:, kc * 64:(kc + 1) * 64],
                         fcs(xb, kc, w), start=(kc == 0), stop=False)
            t.matmul(ps_h[0][0:64, 0:w], negbsre[:, :], mrow(),
                     start=False, stop=True)
            for kc in range(FC):
                t.matmul(ps_h[1][0:64, 0:w], bstim[:, kc * 64:(kc + 1) * 64],
                         fcs(xb, kc, w), start=(kc == 0), stop=False)
            return t.matmul(ps_h[1][0:64, 0:w], negbsim[:, :], mrow(),
                            start=False, stop=True)
        bdeps = [W[f"{tag}A{i}_xb"], W[f"{tag}A{i}_mrow"], W["negb"]]
        if i > 0:
            bdeps.append(W[f"{tag}A{i-1}_pq"])
        elif is_main:
            bdeps.append(W[f"PA{NT-1}_pq"])
        op("tensor", f_bmm, waits=bdeps, inc="t")
        W[f"{tag}A{i}_bu"] = ev("t")

        def f_rot(v, w_=w, j0_=j0):
            rb_ = rdst()[0:64, 0:w_]
            cs = costab[0:64, j0_:j0_ + w_]
            sn = costab[0:64, NCOL + j0_:NCOL + j0_ + w_]
            v.tensor_tensor(scrA[0:64, 0:w_], cs, rb_, OP.mult)
            v.tensor_tensor(scrB[0:64, 0:w_], sn, rb_, OP.mult)
            v.tensor_tensor(scrC[:, 0:w_], scrA[0:64, 0:w_],
                            ps_h[0][0:64, 0:w_], OP.mult)
            v.tensor_tensor(pq[:, j0_:j0_ + w_], scrB[0:64, 0:w_],
                            ps_h[1][0:64, 0:w_], OP.mult)
            v.tensor_tensor(pq[:, j0_:j0_ + w_], scrC[:, 0:w_],
                            pq[:, j0_:j0_ + w_], OP.add)
            v.tensor_tensor(scrC[:, 0:w_], scrB[0:64, 0:w_],
                            ps_h[0][0:64, 0:w_], OP.mult)
            v.tensor_tensor(scrB[0:64, 0:w_], scrA[0:64, 0:w_],
                            ps_h[1][0:64, 0:w_], OP.mult)
            return v.tensor_tensor(pq[:, NTOK + j0_:NTOK + j0_ + w_],
                                   scrC[:, 0:w_], scrB[0:64, 0:w_],
                                   OP.subtract)
        op("vector", f_rot, waits=[W[f"{tag}A{i}_bu"], W["tables"],
                                   W[f"{tag}A{i}r_bc"]], inc="v")
        W[f"{tag}A{i}_pq"] = ev("v")
        if DBG and not is_main:
            def f_snapt(v, i_=i, w_=w, j0_=j0):
                v.tensor_reduce(snapP[:, 144 + i_:145 + i_],
                                pq[:, j0_:j0_ + w_], AX, OP.max,
                                apply_absolute_value=True)
                return v.tensor_reduce(snapU[:, i_:i_ + 1],
                                       pq[:, NTOK + j0_:NTOK + j0_ + w_],
                                       AX, OP.max, apply_absolute_value=True)
            op("vector", f_snapt, waits=[W[f"{tag}A{i}_pq"]], inc="v")


    xpre_r = xpre.rearrange("(fc p) t -> p fc t", p=128)
    xmain_r = xmain.rearrange("(fc p) t -> p fc t", p=128)

    for i in range(NT):
        emit_passA(i, TILES[i], xpre_r[:, :, TO[i]:TO[i] + TILES[i]],
                   False, "P", [])
        drain(10)
        if i in (0, 3, 6, 8):
            mini_scan(W[f"PA{i}_pq"])

    def emit_scan_fn(init_p, init_q):
        def f_scan(v):
            vv = VP(v)
            vv.tensor_copy(scrA[0:64, 0:TILE],
                           rho1[:, :].broadcast_to([64, TILE]))
            last = None
            for half, init0 in ((0, init_p), (NTOK, init_q)):
                for ck in range(NT):
                    lo, w_ = TO[ck], TILES[ck]
                    ini = (init0 if ck == 0
                           else pq[:, half + lo - 1:half + lo])
                    last = vv.tensor_tensor_scan(
                        pq[:, half + lo:half + lo + w_], scrA[0:64, 0:w_],
                        pq[:, half + lo:half + lo + w_], ini,
                        OP.mult, OP.add)
            return last
        return f_scan

    f_scan_pre = emit_scan_fn(0.0, 0.0)

    def sacrificial_scan(*deps):
        def f_sac(v):
            vv = VP(v)
            vv.tensor_copy(scrC[0:64, 8:16],
                           rho1[:, :].broadcast_to([64, 8]))
            return vv.tensor_tensor_scan(
                scrC[0:64, 0:8], scrC[0:64, 8:16],
                onesf[0:64, :].broadcast_to([64, 8]), 0.0, OP.mult, OP.add)
        op("vector", f_sac, waits=list(deps), inc="v")

    def scan_barrier(evname):
        allw = [(k, ctr[k]) for k in ("t", "s", "g", "dx", "dw", "do")
                if ctr[k] > 0]
        return allw

    def scan_post_barrier(ev_):
        for eng in ("tensor", "scalar", "gpsimd", "sync"):
            op(eng, lambda e: e.nop(), waits=[ev_])
    if DBG:
        def f_snap_pre(v):
            for i_ in range(NT):
                j0_, w_ = TO[i_], TILES[i_]
                v.tensor_reduce(snapU[:, 16 + i_:17 + i_],
                                pq[:, j0_:j0_ + w_], AX, OP.max,
                                apply_absolute_value=True)
                v.tensor_reduce(snapU[:, 32 + i_:33 + i_],
                                pq[:, NTOK + j0_:NTOK + j0_ + w_], AX, OP.max,
                                apply_absolute_value=True)
            return v.tensor_single_scalar(vpad[0:1, 0:1], vpad[0:1, 0:1],
                                          0.0, OP.add)
        op("vector", f_snap_pre, waits=[W[f"PA{NT-1}_pq"]], inc="v")
    sacrificial_scan(W[f"PA{NT-1}_pq"])
    op("vector", f_scan_pre,
       waits=[W[f"PA{NT-1}_pq"]] + scan_barrier("pre"), inc="v")
    W["prescan"] = ev("v")
    scan_post_barrier(W["prescan"])
    if DBG:
        def f_snap_post(v):
            for i_ in range(NT):
                j0_, w_ = TO[i_], TILES[i_]
                v.tensor_reduce(snapU[:, 48 + i_:49 + i_],
                                pq[:, j0_:j0_ + w_], AX, OP.max,
                                apply_absolute_value=True)
            return v.tensor_single_scalar(vpad[0:1, 0:1], vpad[0:1, 0:1],
                                          0.0, OP.add)
        op("vector", f_snap_post, waits=[W["prescan"]], inc="v")


    def f_carry(v):
        v = VP(v)
        cs = costab[0:64, NTOK - 1:NTOK]
        sn = costab[0:64, NCOL + NTOK - 1:NCOL + NTOK]
        sp = pq[:, NTOK - 1:NTOK]
        sqq = pq[:, 2 * NTOK - 1:2 * NTOK]
        v.tensor_tensor(c[:, 8:9], cs, sp, OP.mult)
        v.tensor_tensor(c[:, 9:10], sn, sqq, OP.mult)
        v.tensor_tensor(c[:, 8:9], c[:, 8:9], c[:, 9:10], OP.add)
        v.tensor_tensor(c[:, 9:10], sn, sp, OP.mult)
        v.tensor_tensor(c[:, 10:11], cs, sqq, OP.mult)
        v.tensor_tensor(c[:, 9:10], c[:, 9:10], c[:, 10:11], OP.subtract)
        cs1 = costab[0:64, 1:2]
        sn1 = costab[0:64, NCOL + 1:NCOL + 2]
        v.tensor_tensor(c[:, 10:11], cs1, c[:, 8:9], OP.mult)
        v.tensor_tensor(c[:, 11:12], sn1, c[:, 9:10], OP.mult)
        v.tensor_tensor(c[:, 10:11], c[:, 10:11], c[:, 11:12], OP.subtract)
        v.tensor_tensor(c[:, 11:12], sn1, c[:, 8:9], OP.mult)
        v.tensor_tensor(c[:, 12:13], cs1, c[:, 9:10], OP.mult)
        v.tensor_tensor(c[:, 11:12], c[:, 11:12], c[:, 12:13], OP.add)
        return v.tensor_single_scalar(c[:, 11:12], c[:, 11:12], -1.0, OP.mult)
    op("vector", f_carry, waits=[W["prescan"], W["tables"]], inc="v")
    W["carry"] = ev("v")
    if DBG:
        op("vector", lambda v: v.tensor_copy(snapB[:, :], c[:, :]),
           waits=[W["carry"]], inc="v")

    for i in range(NT):
        emit_passA(i, TILES[i], xmain_r[:, :, 1 + TO[i]:1 + TO[i] + TILES[i]],
                   True, "M", [W[f"PA{NT-1}_xb"]])
        drain(10)
    drain(-1)

    f_scan_main = emit_scan_fn(c[:, 10:11], c[:, 11:12])
    sacrificial_scan(W[f"MA{NT-1}_pq"], W["carry"])
    op("vector", f_scan_main,
       waits=[W[f"MA{NT-1}_pq"], W["carry"]] + scan_barrier("main"), inc="v")
    W["mainscan"] = ev("v")
    scan_post_barrier(W["mainscan"])

    hs_ = halos
    xhb = cvt1.ap()

    for fc in range(FC):
        op("sync", (lambda fc_: lambda e: e.dma_start(
            hs_[:, fc_:fc_ + 1], xmain_r[:, fc_, 0:1],
            allow_slow_non_contiguous=True))(fc),
            waits=[W[f"MA{NT-1}_xb"]] if fc == 0 else [], inc="dx")
        op("sync", (lambda fc_: lambda e: e.dma_start(
            hs_[:, FC + fc_:FC + fc_ + 1], xmain_r[:, fc_, NCOL - 1:NCOL],
            allow_slow_non_contiguous=True))(fc), inc="dx")
    W["hx"] = ev("dx")

    def f_hcast(v):
        return v.tensor_copy(xhb[:, 0:2 * FC], hs_[:, 0:2 * FC])
    op("vector", f_hcast, waits=[W["hx"]], inc="v")
    W["hxb"] = ev("v")

    def f_hsq(v):
        return v.tensor_tensor(xhb[:, 32:32 + 2 * FC], xhb[:, 0:2 * FC],
                               xhb[:, 0:2 * FC], OP.mult)
    op("vector", f_hsq, waits=[W["hxb"]], inc="v")
    W["hsq"] = ev("v")

    TH = 3 * TILE + 8

    def h_stats(movoff, tag):
        def f_mm(t, mo=movoff):
            for fc in range(FC):
                t.matmul(ps_S[0:1, 0:1], onesb[:, :],
                         xhb[:, mo + fc:mo + fc + 1],
                         start=(fc == 0), stop=(fc == FC - 1))
            for fc in range(FC):
                last = t.matmul(ps_S[0:1, 1:2], onesb[:, :],
                                xhb[:, mo + FC + fc:mo + FC + fc + 1],
                                start=(fc == 0), stop=(fc == FC - 1))
            return last
        op("tensor", f_mm, waits=[W[f"{tag}_movr"], lastS[0]], inc="t")
        W[f"{tag}_mm"] = ev("t")
        op("scalar", lambda s: s.copy(thin[0:1, TH:TH + 2], ps_S[0:1, 0:2]),
           waits=[W[f"{tag}_mm"]], inc="s")
        W[f"{tag}_me"] = ev("s")

        def f_mm2(t, mo=movoff):
            for fc in range(FC):
                t.matmul(ps_S[0:1, 0:1], onesb[:, :],
                         xhb[:, mo + 32 + fc:mo + 32 + fc + 1],
                         start=(fc == 0), stop=(fc == FC - 1))
            for fc in range(FC):
                last = t.matmul(ps_S[0:1, 1:2], onesb[:, :],
                                xhb[:, mo + 32 + FC + fc:mo + 32 + FC + fc + 1],
                                start=(fc == 0), stop=(fc == FC - 1))
            return last
        op("tensor", f_mm2, waits=[W[f"{tag}_me"], W[f"{tag}_sqr"]], inc="t")
        W[f"{tag}_mm2"] = ev("t")
        op("scalar", lambda s: s.copy(thin[0:1, TH + 2:TH + 4], ps_S[0:1, 0:2]),
           waits=[W[f"{tag}_mm2"]], inc="s")
        W[f"{tag}_qe"] = ev("s")
        lastS[0] = W[f"{tag}_qe"]

        def f_thin(v):
            v = VP(v)
            m = thin[0:1, TH:TH + 2]
            q = thin[0:1, TH + 2:TH + 4]
            r = thin[0:1, TH + 4:TH + 6]
            v.tensor_single_scalar(m, m, 1.0 / D, OP.mult)
            v.tensor_single_scalar(q, q, 1.0 / D, OP.mult)
            v.tensor_tensor(r, m, m, OP.mult)
            return v.tensor_tensor(q, q, r, OP.subtract)
        op("vector", f_thin, waits=[W[f"{tag}_qe"]], inc="v")
        W[f"{tag}_var"] = ev("v")

        def f_r(s):
            q = thin[0:1, TH + 2:TH + 4]
            s.activation(q, q, AF.Ln, bias=epsc[0:1, :])
            return s.activation(thin[0:1, TH + 4:TH + 6], q, AF.Exp, scale=-0.5)
        op("scalar", f_r, waits=[W[f"{tag}_var"]], inc="s")
        W[f"{tag}_r"] = ev("s")

    def h_bcast(srcoff, n, dstoff, tag, deps=()):
        def f_cp(v):
            return v.tensor_copy(thinb[0:1, 8:8 + n], thin[0:1, srcoff:srcoff + n])
        op("vector", f_cp, waits=list(deps) + [lastX[0]], inc="v")
        W[f"{tag}_bm"] = ev("v")
        op("tensor", lambda t: t.matmul(ps_X[:, 0:n], onesr[:, :],
                                        thinb[0:1, 8:8 + n], start=True, stop=True),
           waits=[W[f"{tag}_bm"], lastX[0]], inc="t")
        W[f"{tag}_bo"] = ev("t")
        op("scalar", lambda s: s.copy(hs_[:, dstoff:dstoff + n], ps_X[:, 0:n]),
           waits=[W[f"{tag}_bo"]], inc="s")
        W[f"{tag}_bc"] = ev("s")
        lastX[0] = W[f"{tag}_bc"]

    W["hln1_movr"] = W["hxb"]
    W["hln1_sqr"] = W["hsq"]
    h_stats(0, "hln1")
    h_bcast(TH, 2, 40, "hb1m", deps=[W["hln1_r"]])
    h_bcast(TH + 4, 2, 42, "hb1r", deps=[W["hln1_r"]])

    def f_hmb(v):
        return v.tensor_copy(thinb[0:1, 12:14], thin[0:1, TH:TH + 2])
    op("vector", f_hmb, waits=[W["hln1_r"]], inc="v")
    W["hmb"] = ev("v")

    def f_hbmm(t):
        for kc in range(FC):
            t.matmul(ps_h[0][0:64, 0:1], bstre[:, kc * 64:(kc + 1) * 64],
                     xhb[:, FC + kc:FC + kc + 1], start=(kc == 0), stop=False)
        t.matmul(ps_h[0][0:64, 0:1], negbsre[:, :], thinb[0:1, 13:14],
                 start=False, stop=True)
        for kc in range(FC):
            t.matmul(ps_h[1][0:64, 0:1], bstim[:, kc * 64:(kc + 1) * 64],
                     xhb[:, FC + kc:FC + kc + 1], start=(kc == 0), stop=False)
        return t.matmul(ps_h[1][0:64, 0:1], negbsim[:, :], thinb[0:1, 13:14],
                        start=False, stop=True)
    op("tensor", f_hbmm, waits=[W["hmb"], W["hxb"], W["mainscan"], W["negb"],
                                W[f"MA{NT-1}_pq"]], inc="t")
    W["hbu"] = ev("t")

    def f_hh(v):
        v = VP(v)
        v.tensor_copy(hs_[0:64, 44:45], c[:, 8:9])
        v.tensor_copy(hs_[0:64, 46:47], c[:, 9:10])
        cs = costab[0:64, NTOK:NTOK + 1]
        sn = costab[0:64, NCOL + NTOK:NCOL + NTOK + 1]
        rcol = hs_[0:64, 43:44]
        v.tensor_tensor(c[:, 12:13], cs, rcol, OP.mult)
        v.tensor_tensor(c[:, 13:14], sn, rcol, OP.mult)
        v.tensor_tensor(c[:, 14:15], c[:, 12:13], ps_h[0][0:64, 0:1], OP.mult)
        v.tensor_tensor(scrC[:, 0:1], c[:, 13:14], ps_h[1][0:64, 0:1], OP.mult)
        v.tensor_tensor(c[:, 14:15], c[:, 14:15], scrC[:, 0:1], OP.add)
        v.tensor_tensor(c[:, 15:16], c[:, 13:14], ps_h[0][0:64, 0:1], OP.mult)
        v.tensor_tensor(scrC[:, 0:1], c[:, 12:13], ps_h[1][0:64, 0:1], OP.mult)
        v.tensor_tensor(c[:, 15:16], c[:, 15:16], scrC[:, 0:1], OP.subtract)
        v.tensor_tensor(scrC[:, 0:1], rho1[:, :], pq[:, NTOK - 1:NTOK], OP.mult)
        v.tensor_tensor(c[:, 14:15], c[:, 14:15], scrC[:, 0:1], OP.add)
        v.tensor_tensor(scrC[:, 0:1], rho1[:, :], pq[:, 2 * NTOK - 1:2 * NTOK],
                        OP.mult)
        v.tensor_tensor(c[:, 15:16], c[:, 15:16], scrC[:, 0:1], OP.add)
        v.tensor_tensor(scrC[:, 0:1], cs, c[:, 14:15], OP.mult)
        v.tensor_tensor(scrC[:, 1:2], sn, c[:, 15:16], OP.mult)
        v.tensor_tensor(hs_[0:64, 45:46], scrC[:, 0:1], scrC[:, 1:2], OP.add)
        v.tensor_tensor(scrC[:, 0:1], sn, c[:, 14:15], OP.mult)
        v.tensor_tensor(scrC[:, 1:2], cs, c[:, 15:16], OP.mult)
        v.tensor_tensor(hs_[0:64, 47:48], scrC[:, 0:1], scrC[:, 1:2],
                        OP.subtract)
        v.tensor_copy(hsb[0:64, 0:1], hs_[0:64, 44:45])
        v.tensor_copy(hsb[0:64, 1:2], hs_[0:64, 45:46])
        v.tensor_copy(hsb[0:64, 2:3], hs_[0:64, 46:47])
        return v.tensor_copy(hsb[0:64, 3:4], hs_[0:64, 47:48])
    op("vector", f_hh, waits=[W["hbu"], W["carry"], W["hb1r_bc"]], inc="v")
    W["hh"] = ev("v")

    for g in range(FC):
        bsel = g % 2
        wt = [W[f"hd_mm{g - 2}"]] if g >= 2 else [W[f"negd_mm{FC - 2 + g}"]]
        emit_dpiece(g, bsel, wt + [W["negd"]])
        W[f"hd_ld{g}"] = ev("dw")

        def f_dc(t, oc=g, b_=bsel):
            for kc in range(FC):
                t.matmul(ps_D[:, 2 * oc:2 * oc + 2],
                         dwbuf[:, b_ * 1024 + kc * 128:
                               b_ * 1024 + kc * 128 + 128],
                         xhb[:, kc:2 * FC:FC],
                         start=(kc == 0), stop=False)
            t.matmul(ps_D[:, 2 * oc:2 * oc + 2],
                     negdsum[:, oc * 128:(oc + 1) * 128],
                     thinb[0:1, 12:14], start=False, stop=True)
            t.matmul(ps_C[:, 2 * oc:2 * oc + 2],
                     cstre[:, oc * 128:(oc + 1) * 128],
                     hsb[0:64, 0:2], start=True, stop=False)
            return t.matmul(ps_C[:, 2 * oc:2 * oc + 2],
                            cstim[:, oc * 128:(oc + 1) * 128],
                            hsb[0:64, 2:4], start=False, stop=True)
        hdw = [W[f"hd_ld{g}"], W["hh"], W["hmb"], W["negd"]]
        if g == 0:
            hdw += bg_event(("c2col", 0))
        op("tensor", f_dc, waits=hdw, inc="t")
        W[f"hd_mm{g}"] = ev("t")

    def f_hev(v):
        v = VP(v)
        rb2 = hs_[:, 42:44]
        for oc in range(FC):
            v.tensor_tensor(hs_[:, 2 * oc:2 * oc + 2],
                            ps_D[:, 2 * oc:2 * oc + 2], rb2, OP.mult)
            v.tensor_tensor(hs_[:, 2 * oc:2 * oc + 2],
                            hs_[:, 2 * oc:2 * oc + 2],
                            ps_C[:, 2 * oc:2 * oc + 2], OP.add)
        return v.tensor_single_scalar(vpad[0:1, 0:1], vpad[0:1, 0:1],
                                          0.0, OP.add)
    op("vector", f_hev, waits=[W[f"hd_mm{FC-1}"], W["hb1r_bc"]], inc="v")
    W["hy"] = ev("v")

    for fc in range(FC):
        op("sync", (lambda fc_: lambda e: e.dma_start(
            hs_[:, 16 + fc_:16 + fc_ + 1], xmain_r[:, fc_, 0:1],
            allow_slow_non_contiguous=True))(fc),
            waits=[W["hy"]] if fc == 0 else [], inc="dx")
        op("sync", (lambda fc_: lambda e: e.dma_start(
            hs_[:, 16 + FC + fc_:16 + FC + fc_ + 1],
            xmain_r[:, fc_, NCOL - 1:NCOL],
            allow_slow_non_contiguous=True))(fc), inc="dx")
    W["hx2"] = ev("dx")

    def f_hyb(v):
        v = VP(v)
        v.tensor_copy(xhb[:, 0:16], hs_[:, 0:16])
        return v.tensor_tensor(xhb[:, 32:48], xhb[:, 0:16], xhb[:, 0:16],
                               OP.mult)
    op("vector", f_hyb, waits=[W["hy"]], inc="v")
    W["hln2_movr"] = ev("v")
    W["hln2_sqr"] = W["hln2_movr"]

    def h_stats2(tag):
        def f_mm(t):
            for fc in range(FC):
                t.matmul(ps_S[0:1, 0:2], onesb[:, :],
                         xhb[:, 2 * fc:2 * fc + 2],
                         start=(fc == 0), stop=(fc == FC - 1))
            for fc in range(FC):
                last = t.matmul(ps_S[0:1, 2:4], onesb[:, :],
                                xhb[:, 32 + 2 * fc:32 + 2 * fc + 2],
                                start=(fc == 0), stop=(fc == FC - 1))
            return last
        op("tensor", f_mm, waits=[W[f"{tag}_movr"], lastS[0]], inc="t")
        W[f"{tag}_mm"] = ev("t")
        op("scalar", lambda s: s.copy(thin[0:1, TH:TH + 4], ps_S[0:1, 0:4]),
           waits=[W[f"{tag}_mm"]], inc="s")
        W[f"{tag}_qe"] = ev("s")
        lastS[0] = W[f"{tag}_qe"]

        def f_thin(v):
            v = VP(v)
            m = thin[0:1, TH:TH + 2]
            q = thin[0:1, TH + 2:TH + 4]
            r = thin[0:1, TH + 4:TH + 6]
            v.tensor_single_scalar(m, m, 1.0 / D, OP.mult)
            v.tensor_single_scalar(q, q, 1.0 / D, OP.mult)
            v.tensor_tensor(r, m, m, OP.mult)
            return v.tensor_tensor(q, q, r, OP.subtract)
        op("vector", f_thin, waits=[W[f"{tag}_qe"]], inc="v")
        W[f"{tag}_var"] = ev("v")

        def f_r(s):
            q = thin[0:1, TH + 2:TH + 4]
            s.activation(q, q, AF.Ln, bias=epsc[0:1, :])
            return s.activation(thin[0:1, TH + 4:TH + 6], q, AF.Exp, scale=-0.5)
        op("scalar", f_r, waits=[W[f"{tag}_var"]], inc="s")
        W[f"{tag}_r"] = ev("s")

    h_stats2("hln2")
    h_bcast(TH, 2, 40, "hb2m", deps=[W["hln2_r"]])
    h_bcast(TH + 4, 2, 42, "hb2r", deps=[W["hln2_r"]])

    def f_hx1b(v):
        v = VP(v)
        mb2 = hs_[:, 40:42]
        rb2 = hs_[:, 42:44]
        for oc in range(FC):
            yv = hs_[:, 2 * oc:2 * oc + 2]
            v.tensor_tensor(yv, yv, mb2, OP.subtract)
            v.tensor_tensor(yv, yv, rb2, OP.mult)
        for oc in range(FC):
            yv = hs_[:, 2 * oc:2 * oc + 2]
            v.tensor_tensor(yv[:, 0:1], yv[:, 0:1],
                            hs_[:, 16 + oc:16 + oc + 1], OP.add)
            v.tensor_tensor(yv[:, 1:2], yv[:, 1:2],
                            hs_[:, 16 + FC + oc:16 + FC + oc + 1], OP.add)
        return v.tensor_single_scalar(vpad[0:1, 0:1], vpad[0:1, 0:1],
                                          0.0, OP.add)
    op("vector", f_hx1b, waits=[W["hb2m_bc"], W["hb2r_bc"], W["hx2"]], inc="v")
    W["hx1"] = ev("v")

    def f_hx1sq(v):
        v = VP(v)
        v.tensor_copy(xhb[:, 0:16], hs_[:, 0:16])
        return v.tensor_tensor(xhb[:, 32:48], xhb[:, 0:16], xhb[:, 0:16],
                               OP.mult)
    op("vector", f_hx1sq, waits=[W["hx1"]], inc="v")
    W["hln3_movr"] = ev("v")
    W["hln3_sqr"] = W["hln3_movr"]
    h_stats2("hln3")
    h_bcast(TH, 2, 40, "hb3m", deps=[W["hln3_r"]])

    def f_ht3(v):
        v = VP(v)
        mb3 = hs_[:, 40:42]
        for oc in range(FC):
            yv = hs_[:, 2 * oc:2 * oc + 2]
            v.tensor_tensor(yv, yv, mb3, OP.subtract)
        return v.tensor_reduce(hs_[:, 16:18],
                               hs_[:, 0:16].rearrange("p (fc t) -> p t fc", fc=FC),
                               AX, OP.max, apply_absolute_value=True)
    op("vector", f_ht3, waits=[W["hb3m_bc"]], inc="v")
    W["ht3"] = ev("v")

    op("tensor", lambda t: t.transpose(ps_X[0:2, 0:128], hs_[:, 16:18],
                                       idnf[:, :]),
       waits=[W["ht3"], lastX[0], W["p0ld"]], inc="t")
    W["htf"] = ev("t")
    op("vector", lambda v: v.tensor_reduce(hs_[0:2, 20:21], ps_X[0:2, 0:128],
                                           AX, OP.max),
       waits=[W["htf"]], inc="v")
    W["htr"] = ev("v")
    lastX[0] = W["htr"]
    op("tensor", lambda t: t.transpose(ps_X[0:1, 0:128], hs_[:, 20:21],
                                       idnf[:, :]),
       waits=[W["htr"], lastX[0]], inc="t")
    W["htg"] = ev("t")
    op("scalar", lambda s: s.copy(thin[0:1, TH:TH + 2], ps_X[0:1, 0:2]),
       waits=[W["htg"]], inc="s")
    W["hte"] = ev("s")
    lastX[0] = W["hte"]

    def f_huc(v):
        v = VP(v)
        v.tensor_tensor(thin[0:1, TH:TH + 2], thin[0:1, TH:TH + 2],
                        thin[0:1, TH + 4:TH + 6], OP.mult)
        return v.tensor_single_scalar(thin[0:1, TH:TH + 2],
                                      thin[0:1, TH:TH + 2], 1e-5, OP.max)
    op("vector", f_huc, waits=[W["hte"]], inc="v")
    W["huc"] = ev("v")

    def f_hs3(s):
        uc = thin[0:1, TH:TH + 2]
        s3 = thin[0:1, TH + 2:TH + 4]
        s.activation(s3, uc, AF.Ln)
        return s.activation(s3, s3, AF.Exp, scale=-1.0)
    op("scalar", f_hs3, waits=[W["huc"]], inc="s")
    W["hs3"] = ev("s")

    def f_hrs(v):
        v = VP(v)
        s3 = thin[0:1, TH + 2:TH + 4]
        uc = thin[0:1, TH:TH + 2]
        v.tensor_single_scalar(s3, s3, 127.0, OP.mult)
        v.tensor_tensor(s3, s3, thin[0:1, TH + 4:TH + 6], OP.mult)
        v.tensor_single_scalar(uc, uc, 1.0 / 127.0, OP.mult)
        return v.tensor_single_scalar(vpad[0:1, 0:1], vpad[0:1, 0:1],
                                          0.0, OP.add)
    op("vector", f_hrs, waits=[W["hs3"]], inc="v")
    W["hrs"] = ev("v")
    h_bcast(TH + 2, 2, 40, "hbrs", deps=[W["hrs"]])
    h_bcast(TH, 2, 42, "hbdq", deps=[W["hrs"]])

    def f_hq(v):
        v = VP(v)
        rsb = hs_[:, 40:42]
        dqb = hs_[:, 42:44]
        for oc in range(FC):
            tcol = hs_[:, 2 * oc:2 * oc + 2]
            v.tensor_tensor(tcol, tcol, rsb, OP.mult)
            v.tensor_single_scalar(k16f[:, 0:2], tcol, 128.5, OP.add)
            v.scalar_tensor_tensor(out=hs_[:, 18:20], in0=k16f[:, 0:2],
                                   scalar=-128.5, in1=tcol,
                                   op0=OP.add, op1=OP.is_gt)
            v.tensor_tensor(hs_[:, 16:18], k16f[:, 0:2], hs_[:, 18:20],
                            OP.subtract)
            v.tensor_single_scalar(hs_[:, 16:18], hs_[:, 16:18], -128.0, OP.add)
            v.tensor_tensor(hs_[:, 16:18], hs_[:, 16:18], dqb, OP.mult)
            v.tensor_scalar(out=hs_[:, 17:18], in0=hs_[:, 17:18],
                            scalar1=maskc[:, 0:1], scalar2=None, op0=OP.mult)
            v.tensor_copy(haloq[:, 2 * oc:2 * oc + 2], hs_[:, 16:18])
        return v.tensor_single_scalar(vpad[0:1, 0:1], vpad[0:1, 0:1],
                                          0.0, OP.add)
    op("vector", f_hq, waits=[W["hbdq_bc"], W["hbrs_bc"]], inc="v")
    W["haloq"] = ev("v")

    def emit_attn(i):
        w = TILES[i]
        j0 = TO[i]
        x1t = x1[:, (i % 2) * FC * TILE:(i % 2) * FC * TILE + FC * TILE]
        winb = xqw[:, (i % 2) * FC * 458:((i % 2) + 1) * FC * 458]

        dep = [W[f"Bq3_{i-1}"]] if i > 0 else [W["hln3_movr"]]
        for fc in range(FC):
            op("sync", (lambda w_, fc_, j0_: lambda e: e.dma_start(
                xsb[:, fc_ * TILE:fc_ * TILE + w_],
                xmain_r[:, fc_, 1 + j0_:1 + j0_ + w_]))(w, fc, j0),
                waits=dep if fc == 0 else [], inc="dx")
        W[f"Bx{i}"] = ev("dx")

        cdep = [W[f"Bx{i}"]]
        if i > 0:
            cdep += [W[f"B_ln3_{i-1}_sq"], W[f"BDC{i-1}_7"]]
        op("vector", lambda v: v.tensor_copy(xb[:, :], xsb[:, :]),
           waits=cdep, inc="v")
        W[f"Bxb{i}"] = ev("v")

        pdep = [W["mainscan"]]
        pdep.append(W[f"BDC{i-1}_7"] if i > 0 else W[f"hd_mm{FC-1}"])
        def f_post(v, w_=w, j0_=j0):
            cs = costab[0:64, j0_:j0_ + w_]
            sn = costab[0:64, NCOL + j0_:NCOL + j0_ + w_]
            sp = pq[:, j0_:j0_ + w_]
            sqq = pq[:, NTOK + j0_:NTOK + j0_ + w_]
            v.tensor_tensor(scrA[0:64, 0:w_], cs, sp, OP.mult)
            v.tensor_tensor(scrB[0:64, 0:w_], sn, sqq, OP.mult)
            v.tensor_tensor(hsb[0:64, 0:w_], scrA[0:64, 0:w_],
                            scrB[0:64, 0:w_], OP.add)
            v.tensor_tensor(scrA[0:64, 0:w_], sn, sp, OP.mult)
            v.tensor_tensor(scrB[0:64, 0:w_], cs, sqq, OP.mult)
            return v.tensor_tensor(hsb[0:64, TILE:TILE + w_],
                                   scrA[0:64, 0:w_], scrB[0:64, 0:w_],
                                   OP.subtract)
        op("vector", f_post, waits=pdep, inc="v")
        W[f"Bhs{i}"] = ev("v")

        for oc in range(FC):
            bsel = oc % 2
            if oc >= 2:
                wt = [W[f"BDC{i}_{oc-2}"]]
            elif i == 0:
                wt = [W[f"hd_mm{FC - 2 + oc}"]]
            else:
                wt = [W[f"BDC{i-1}_{FC - 2 + oc}"]]
            emit_dpiece(oc, bsel, wt)
            W[f"BDld{i}_{oc}"] = ev("dw")

            def f_dc1(t, oc_=oc, b_=bsel, w_=w, j0_=j0):
                for kc in range(FC):
                    t.matmul(ps_D[:, 0:w_],
                             dwbuf[:, b_ * 1024 + kc * 128:
                                   b_ * 1024 + kc * 128 + 128],
                             fcs(xb, kc, w_), start=(kc == 0), stop=False)
                t.matmul(ps_D[:, 0:w_], negdsum[:, oc_ * 128:(oc_ + 1) * 128],
                         mall[:, j0_:j0_ + w_], start=False, stop=True)
                t.matmul(ps_C[:, 0:w_], cstre[:, oc_ * 128:(oc_ + 1) * 128],
                         hsb[0:64, 0:w_], start=True, stop=False)
                return t.matmul(ps_C[:, 0:w_],
                                cstim[:, oc_ * 128:(oc_ + 1) * 128],
                                hsb[0:64, TILE:TILE + w_], start=False,
                                stop=True)
            deps = [W[f"BDld{i}_{oc}"], W[f"Bxb{i}"], W[f"Bhs{i}"], W["negd"]]
            if oc > 0:
                deps.append(W[f"Byev{i}_{oc-1}"])
            elif i == 0:
                deps.append(W["hy"])
            elif i >= 2:
                deps.append(W[f"F_ln4_{i-2}_sq"])
            op("tensor", f_dc1, waits=deps, inc="t")
            W[f"BDC{i}_{oc}"] = ev("t")

            def f_yev(v, oc_=oc, w_=w, j0_=j0):
                v.tensor_tensor(scrA[:, 0:w_], ps_D[:, 0:w_],
                                rball[:, j0_:j0_ + w_], OP.mult)
                return v.tensor_tensor(fcs(ybf, oc_, w_), scrA[:, 0:w_],
                                       ps_C[:, 0:w_], OP.add)
            op("vector", f_yev, waits=[W[f"BDC{i}_{oc}"]], inc="v")
            W[f"Byev{i}_{oc}"] = ev("v")

        sqdep = [W[f"Byev{i}_7"]]
        if i >= 2:
            sqdep.append(W[f"F_ln4_{i-2}_sq"])
        op("vector", lambda v: v.tensor_tensor(sq[:, :], ybf[:, :], ybf[:, :],
                                               OP.mult), waits=sqdep, inc="v")
        W[f"Bysq{i}"] = ev("v")
        emit_stats(w, lambda fc: fcs(ybf, fc, w), lambda fc: fcs(sq, fc, w),
                   f"B_ln2_{i}", W[f"Byev{i}_7"], W[f"Bysq{i}"])
        emit_bcast(w, 0, lambda: bc(0, w), f"Bm2_{i}", deps=[W[f"B_ln2_{i}_r"]])
        emit_bcast(w, 2, lambda: bc(1, w), f"Br2_{i}", deps=[W[f"B_ln2_{i}_r"]])

        def f_x1(v, w_=w, x1t_=x1t):
            for fc in range(FC):
                y = fcs(ybf, fc, w_)
                v.tensor_tensor(y, y, bc(0, w_), OP.subtract)
                v.tensor_tensor(y, y, bc(1, w_), OP.mult)
                v.tensor_tensor(fcs(x1t_, fc, w_), fcs(xsb, fc, w_), y, OP.add)
            return v.tensor_single_scalar(vpad[0:1, 0:1], vpad[0:1, 0:1],
                                          0.0, OP.add)
        x1dep = [W[f"Bm2_{i}_bc"], W[f"Br2_{i}_bc"], W[f"Bx{i}"]]
        if i >= 2:
            x1dep.append(W[f"Fout{i-2}"])
        if DBG and i == 1:
            op("vector", lambda v: v.nop(), waits=x1dep, inc="v")
            dxs = nc.dram_tensor("dump_xsb1", [128, FC * TILE], F32,
                                 kind="ExternalOutput").ap()
            dyv = nc.dram_tensor("dump_ybv1", [128, FC * TILE], BF16,
                                 kind="ExternalOutput").ap()
            op("sync", lambda e: e.dma_start(dxs, xsb[:, :]),
               waits=[ev("v")], inc="dx")
            op("sync", lambda e: e.dma_start(dyv, ybf[:, :]), inc="dx")
            x1dep = x1dep + [ev("dx")]
        op("vector", f_x1, waits=x1dep, inc="v")
        W[f"Bx1_{i}"] = ev("v")

        def f_x1b(v, x1t_=x1t):
            v.tensor_copy(xb[:, :], x1t_)
            return v.tensor_tensor(sq[:, :], xb[:, :], xb[:, :], OP.mult)
        op("vector", f_x1b, waits=[W[f"Bx1_{i}"], W[f"BDC{i}_7"],
                                   W[f"B_ln2_{i}_sq"]], inc="v")
        W[f"Bx1b{i}"] = ev("v")
        emit_stats(w, lambda fc: fcs(xb, fc, w), lambda fc: fcs(sq, fc, w),
                   f"B_ln3_{i}", W[f"Bx1b{i}"], W[f"Bx1b{i}"])
        emit_bcast(w, 0, lambda: bc(2, w), f"Bm3_{i}", deps=[W[f"B_ln3_{i}_r"]])

        def f_t3(v, w_=w, x1t_=x1t):
            for fc in range(FC):
                v.tensor_tensor(fcs(xsb, fc, w_), fcs(x1t_, fc, w_),
                                bc(2, w_), OP.subtract)
            return v.tensor_single_scalar(vpad[0:1, 0:1], vpad[0:1, 0:1],
                                          0.0, OP.add)
        op("vector", f_t3, waits=[W[f"Bm3_{i}_bc"]], inc="v")
        W[f"Bt3_{i}"] = ev("v")

        emit_bcast(w, 2, lambda: bc(3, w), f"Br3b_{i}",
                   deps=[W[f"B_ln3_{i}_r"]])

        def f_mx3(v, w_=w):
            return v.tensor_reduce(rmax[:, 0:w_],
                                   xsb.ap().rearrange("p (fc t) -> p t fc", fc=FC)
                                   [:, 0:w_, :], AX, OP.max,
                                   apply_absolute_value=True)
        op("vector", f_mx3, waits=[W[f"Bt3_{i}"], W[f"Fpm{i-2}_done"]
                                   if i >= 2 else W["haloq"]], inc="v")
        W[f"Bmx3_{i}"] = ev("v")
        emit_pmax_bcast(w, f"Bpm{i}", [W[f"Bmx3_{i}"]])

        def f_scale3(v, w_=w):
            v.tensor_scalar(out=scrA[:, 0:w_], in0=rmax[:, 0:w_], scalar1=1e-5,
                            scalar2=None, op0=OP.max)
            return v.tensor_tensor(scrA[:, 0:w_], scrA[:, 0:w_], bc(3, w_),
                                   OP.mult)
        op("vector", f_scale3, waits=[W[f"Bpm{i}_done"], W[f"Br3b_{i}_bc"]],
           inc="v")
        W[f"Buc3_{i}"] = ev("v")

        def f_s3(s, w_=w):
            s.activation(scrB[:, 0:w_], scrA[:, 0:w_], AF.Ln)
            return s.activation(scrB[:, 0:w_], scrB[:, 0:w_], AF.Exp,
                                scale=-1.0)
        op("scalar", f_s3, waits=[W[f"Buc3_{i}"]], inc="s")
        W[f"Bs3_{i}"] = ev("s")

        def f_rs3(v, w_=w):
            v.tensor_single_scalar(scrB[:, 0:w_], scrB[:, 0:w_], 127.0, OP.mult)
            v.tensor_tensor(bc(3, w_), bc(3, w_), scrB[:, 0:w_], OP.mult)
            return v.tensor_scalar(out=bc(4, w_), in0=scrA[:, 0:w_],
                                   scalar1=1.0 / 127.0, scalar2=None,
                                   op0=OP.mult)
        op("vector", f_rs3, waits=[W[f"Bs3_{i}"]], inc="v")
        W[f"Brs3b_{i}_bc"] = ev("v")
        W[f"Bdq3b_{i}_bc"] = W[f"Brs3b_{i}_bc"]

        def f_q3(v, w_=w, winb_=winb):
            for fc in range(FC):
                t = fcs(xsb, fc, w_)
                v.tensor_tensor(t, t, bc(3, w_), OP.mult)
            v.tensor_single_scalar(k16d[:, 0:FC * TILE], xsb[:, :], 128.5,
                                   OP.add)
            v.scalar_tensor_tensor(out=ybf[:, :], in0=k16d[:, 0:FC * TILE],
                                   scalar=-128.5, in1=xsb[:, :],
                                   op0=OP.add, op1=OP.is_gt)
            v.tensor_tensor(xsb[:, :], k16d[:, 0:FC * TILE], ybf[:, :],
                            OP.subtract)
            for fc in range(FC):
                v.scalar_tensor_tensor(
                    out=winb_[:, fc * 458 + 1:fc * 458 + 1 + w_],
                    in0=fcs(xsb, fc, w_), scalar=-128.0, in1=bc(4, w_),
                    op0=OP.add, op1=OP.mult)
            return v.tensor_single_scalar(vpad[0:1, 0:1], vpad[0:1, 0:1],
                                          0.0, OP.add)
        q3dep = [W[f"Brs3b_{i}_bc"], W[f"Bdq3b_{i}_bc"], W[f"B_ln3_{i}_sq"]]
        if i >= 2:
            q3dep.append(W[f"Fmm1_{i-2}_{HC-1}"])
        op("vector", f_q3, waits=q3dep, inc="v")
        W[f"Bq3_{i}"] = ev("v")

        def f_edges(v, w_=w, winb_=winb, i_=i):
            if i_ == 0:
                for fc in range(FC):
                    v.tensor_copy(winb_[:, fc * 458:fc * 458 + 1],
                                  haloq[:, 2 * fc:2 * fc + 1])
            else:
                pw = TILES[i_ - 1]
                pwin = xqw[:, ((i_ - 1) % 2) * FC * 458:
                           ((i_ - 1) % 2 + 1) * FC * 458]
                for fc in range(FC):
                    v.tensor_copy(winb_[:, fc * 458:fc * 458 + 1],
                                  pwin[:, fc * 458 + pw:fc * 458 + pw + 1])
                    v.tensor_copy(pwin[:, fc * 458 + pw + 1:fc * 458 + pw + 2],
                                  winb_[:, fc * 458 + 1:fc * 458 + 2])
            if i_ == NT - 1:
                for fc in range(FC):
                    v.tensor_copy(winb_[:, fc * 458 + w_ + 1:fc * 458 + w_ + 2],
                                  haloq[:, 2 * fc + 1:2 * fc + 2])
            return v.tensor_single_scalar(vpad[0:1, 0:1], vpad[0:1, 0:1],
                                          0.0, OP.add)
        edeps = [W[f"Bq3_{i}"], W["haloq"]]
        op("vector", f_edges, waits=edeps, inc="v")
        W[f"Bwin{i}"] = ev("v")

    def emit_ffn(i):
        w = TILES[i]
        ww = w + 2
        winb = xqw[:, (i % 2) * FC * 458:((i % 2) + 1) * FC * 458]
        x1t = x1[:, (i % 2) * FC * TILE:(i % 2) * FC * TILE + FC * TILE]
        windep = W[f"Bwin{i+1}"] if i + 1 < NT else W[f"Bwin{i}"]

        for hc in range(HC):
            bsel = hc % 2
            bank = hc % 2
            if hc >= 2:
                wt = [W[f"Fmm1_{i}_{hc-2}"]]
            elif i == 0:
                wt = bg_event(("q1_codes", 0))
            else:
                wt = [W[f"Fmm1_{i-1}_{HC - 2 + hc}"]]
            for kc in range(2):
                op("sync", (lambda h_, b_, k_: lambda e: e.dma_start(
                    w1buf[:, b_ * 1024 + k_ * 512:b_ * 1024 + k_ * 512 + 512],
                    c1_f[h_, :, k_ * 512:(k_ + 1) * 512]))(hc, bsel, kc),
                    waits=wt if kc == 0 else [], inc="dx")
            W[f"Fw1ld{i}_{hc}"] = ev("dx")

            if True:
                def f_mm1(t, b_=bsel, bank_=bank, ww_=ww):
                    for kc in range(FC):
                        last = t.matmul(
                            ps_h[bank_][:, 0:ww_],
                            w1buf[:, b_ * 1024 + kc * 128:
                                  b_ * 1024 + kc * 128 + 128],
                            winb[:, kc * 458:kc * 458 + ww_],
                            start=(kc == 0), stop=(kc == FC - 1))
                    return last
                deps = [W[f"Fw1ld{i}_{hc}"], windep]
                if hc >= 2:
                    deps.append(W[f"Fconv{i}_{hc-2}"])
                elif i == 0:
                    deps.append(W["hh"])
                else:
                    deps.append(W[f"Fconv{i-1}_{HC-1}"])
                op("tensor", f_mm1, waits=deps, inc="t")
                W[f"Fmm1_{i}_{hc}"] = ev("t")

                def f_taps(s, hc_=hc, bank_=bank, w_=w):
                    pk = ps_h[bank_]
                    s.activation(cvt1[:, 0:w_], pk[:, 0:w_], AF.Copy,
                                 scale=convw[:, hc_:hc_ + 1])
                    s.activation(cvt2[:, 0:w_], pk[:, 1:1 + w_], AF.Copy,
                                 scale=convw[:, 32 + hc_:32 + hc_ + 1])
                    return s.activation(cvt3[:, 0:w_], pk[:, 2:2 + w_], AF.Copy,
                                        scale=convw[:, 64 + hc_:64 + hc_ + 1])
                tdep = [W[f"Fmm1_{i}_{hc}"]]
                if i == 0 and hc == 0:
                    tdep += bg_event(("convfold", 0))
                    tdep.append(W["haloq"])
                if hc >= 1:
                    tdep.append(W[f"Fconv{i}_{hc-1}"])
                elif i > 0:
                    tdep.append(W[f"Fq{i-1}_{HC - 1}"])
                op("scalar", f_taps, waits=tdep, inc="s")
                W[f"Ftap{i}_{hc}"] = ev("s")

                def f_adds(v, hc_=hc, w_=w):
                    v.scalar_tensor_tensor(out=cvt1[:, 0:w_], in0=cvt1[:, 0:w_],
                                           scalar=convw[:, 96 + hc_:96 + hc_ + 1],
                                           in1=cvt2[:, 0:w_],
                                           op0=OP.add, op1=OP.add)
                    return v.tensor_tensor(hsilu[:, hc_ * TILE:hc_ * TILE + w_],
                                           cvt1[:, 0:w_], cvt3[:, 0:w_], OP.add)
                adep = [W[f"Ftap{i}_{hc}"]]
                if hc == 0:
                    if i > 0:
                        adep.append(W[f"Fo{i-1}"])
                    else:
                        adep += bg_event(("q2_codes", 0))
                op("vector", f_adds, waits=adep, inc="v")
                W[f"Fconv{i}_{hc}"] = ev("v")

        hsv = hsilu.ap().rearrange("p (hc t) -> p hc t", hc=HC)[:, :, 0:w]
        op("scalar", (lambda a: lambda s: s.activation(a, a, AF.Silu))(hsv),
           waits=[W[f"Fconv{i}_{HC-1}"]], inc="s")
        W[f"Fsilu{i}"] = ev("s")

        def f_mxh(v, w_=w):
            return v.tensor_reduce(rmax[:, 0:w_],
                                   hsilu.ap().rearrange("p (hc t) -> p t hc", hc=HC)
                                   [:, 0:w_, :], AX, OP.max,
                                   apply_absolute_value=True)
        op("vector", f_mxh, waits=[W[f"Fsilu{i}"], W[f"Bpm{i+1}_done"]
                                   if i + 1 < NT else W[f"Bpm{i}_done"]],
           inc="v")
        W[f"Fmxh{i}"] = ev("v")
        emit_pmax_bcast(w, f"Fpm{i}", [W[f"Fmxh{i}"]])

        def f_scaleh(v, w_=w):
            return v.tensor_scalar(out=scrA[:, 0:w_], in0=rmax[:, 0:w_],
                                   scalar1=1e-5, scalar2=None, op0=OP.max)
        op("vector", f_scaleh, waits=[W[f"Fpm{i}_done"]], inc="v")
        W[f"Fuch{i}"] = ev("v")

        def f_sh(s, w_=w):
            s.activation(scrB[:, 0:w_], scrA[:, 0:w_], AF.Ln)
            return s.activation(scrB[:, 0:w_], scrB[:, 0:w_], AF.Exp,
                                scale=-1.0)
        op("scalar", f_sh, waits=[W[f"Fuch{i}"]], inc="s")
        W[f"Fsh{i}"] = ev("s")

        def f_al(v, w_=w):
            v.tensor_scalar(out=bc(5, w_), in0=scrB[:, 0:w_], scalar1=127.0,
                            scalar2=None, op0=OP.mult)
            return v.tensor_scalar(out=bc(2, w_), in0=scrA[:, 0:w_],
                                   scalar1=c2mc[:, 0:1], scalar2=None,
                                   op0=OP.mult)
        op("vector", f_al, waits=[W[f"Fsh{i}"]], inc="v")
        W[f"Fshb_{i}_bc"] = ev("v")
        W[f"Falb_{i}_bc"] = W[f"Fshb_{i}_bc"]

        NGRP = HC
        for gq in range(NGRP):
            def f_q(v, hc=gq, w_=w):
                hv = hsilu[:, hc * TILE:hc * TILE + w_]
                qv = qscr[:, 0:w_]
                kk = k16f[:, 0:w_]
                v.tensor_tensor(qv, hv, bc(5, w_), OP.mult)
                v.tensor_single_scalar(kk, qv, 128.5, OP.add)
                v.scalar_tensor_tensor(out=cvt1[:, 0:w_], in0=kk,
                                       scalar=-128.5, in1=qv,
                                       op0=OP.add, op1=OP.is_gt)
                return v.scalar_tensor_tensor(out=hv, in0=kk, scalar=-1.0,
                                              in1=cvt1[:, 0:w_], op0=OP.mult,
                                              op1=OP.add)
            op("vector", f_q, waits=[W[f"Fshb_{i}_bc"]], inc="v")
            W[f"Fq{i}_{gq}"] = ev("v")

        for oc in range(FC):
            for q8 in range(4):
                pidx = 4 * oc + q8
                bsel = pidx % 2
                if pidx >= 2:
                    wt = [W[f"Fmm2p{i}_{pidx - 2}"]]
                elif i == 0:
                    wt = bg_event(("q2_codes", 0))
                else:
                    wt = [W[f"Fmm2_{i-1}_7"]]
                for k8 in range(2):
                    op("sync", (lambda oc_, q_, b_, k_: lambda e: e.dma_start(
                        w2buf[:, b_ * 1024 + k_ * 512:
                              b_ * 1024 + k_ * 512 + 512],
                        c2_f[q_, oc_, :, k_ * 512:(k_ + 1) * 512]))(
                        oc, q8, bsel, k8),
                        waits=wt if k8 == 0 else [], inc="dx")
                W[f"Fw2ld{i}_{pidx}"] = ev("dx")

                def f_mm2p(t, oc_=oc, q_=q8, b_=bsel, w_=w):
                    for k8 in range(FC):
                        kc = q_ * 8 + k8
                        last = t.matmul(ps_y[oc_ % 2][:, 0:w_],
                                        w2buf[:, b_ * 1024 + k8 * 128:
                                              b_ * 1024 + k8 * 128 + 128],
                                        hsilu[:, kc * TILE:kc * TILE + w_],
                                        start=(kc == 0), stop=(kc == HC - 1))
                    return last
                deps = [W[f"Fw2ld{i}_{pidx}"]]
                if q8 == 0:
                    deps.append(W[f"Fq{i}_{NGRP-1}"])
                    if oc >= 2:
                        deps.append(W[f"Fy2ev{i}_{oc-2}"])
                    elif i == 0:
                        deps += bg_event(("c2col", 0))
                op("tensor", f_mm2p, waits=deps, inc="t")
                W[f"Fmm2p{i}_{pidx}"] = ev("t")
            W[f"Fmm2_{i}_{oc}"] = W[f"Fmm2p{i}_{4*oc+3}"]

            def f_y2ev(v, oc_=oc, w_=w):
                return v.scalar_tensor_tensor(
                    out=fcs(ybf, oc_, w_), in0=ps_y[oc_ % 2][:, 0:w_],
                    scalar=c2col[:, oc_:oc_ + 1], in1=bc(2, w_),
                    op0=OP.add, op1=OP.mult)
            ydep = [W[f"Fmm2_{i}_{oc}"], W[f"Falb_{i}_bc"]]
            if oc == 0:
                ydep.append(W[f"Bq3_{i+1}"] if i + 1 < NT else W[f"Bq3_{i}"])
            op("vector", f_y2ev, waits=ydep, inc="v")
            W[f"Fy2ev{i}_{oc}"] = ev("v")

        sqdep = [W[f"Fy2ev{i}_7"],
                 W[f"B_ln3_{i+1}_sq"] if i + 1 < NT else W[f"B_ln3_{i}_sq"]]
        op("vector", lambda v: v.tensor_tensor(sq[:, :], ybf[:, :], ybf[:, :],
                                               OP.mult), waits=sqdep, inc="v")
        W[f"Fy2sq{i}"] = ev("v")
        emit_stats(w, lambda fc: fcs(ybf, fc, w), lambda fc: fcs(sq, fc, w),
                   f"F_ln4_{i}", W[f"Fy2ev{i}_7"], W[f"Fy2sq{i}"])
        emit_bcast(w, 0, lambda: bc(0, w), f"Fm4_{i}", deps=[W[f"F_ln4_{i}_r"]])
        emit_bcast(w, 2, lambda: bc(1, w), f"Fr4_{i}", deps=[W[f"F_ln4_{i}_r"]])

        def f_out(v, w_=w, x1t_=x1t):
            for fc in range(FC):
                y = fcs(ybf, fc, w_)
                v.tensor_tensor(y, y, bc(0, w_), OP.subtract)
                v.tensor_tensor(y, y, bc(1, w_), OP.mult)
                v.tensor_tensor(outb[:, fc * TILE:fc * TILE + w_],
                                fcs(x1t_, fc, w_), y, OP.add)
            return v.tensor_single_scalar(vpad[0:1, 0:1], vpad[0:1, 0:1],
                                          0.0, OP.add)
        odeps = [W[f"Fm4_{i}_bc"], W[f"Fr4_{i}_bc"], W[f"Bx1_{i}"]]
        if i > 0:
            odeps.append(W[f"Fo{i-1}"])
        if DBG and i == 1:
            op("vector", lambda v: v.nop(), waits=odeps, inc="v")
            W["dbg_pre_out1"] = ev("v")
            dyb = nc.dram_tensor("dump_yb1", [128, FC * TILE], BF16,
                                 kind="ExternalOutput").ap()
            dbc = nc.dram_tensor("dump_bcb1", [128, 6 * TILE], BF16,
                                 kind="ExternalOutput").ap()
            dx1 = nc.dram_tensor("dump_x1s1", [128, FC * TILE], F32,
                                 kind="ExternalOutput").ap()
            op("sync", (lambda x1t_: lambda e: e.dma_start(
                dyb, ybf[:, :]))(x1t), waits=[W["dbg_pre_out1"]], inc="dx")
            op("sync", lambda e: e.dma_start(dbc, bcb[:, :]), inc="dx")
            op("sync", (lambda x1t_: lambda e: e.dma_start(dx1, x1t_))(x1t),
               inc="dx")
            odeps = odeps + [ev("dx")]
        op("vector", f_out, waits=odeps, inc="v")
        W[f"Fout{i}"] = ev("v")

        for fc in range(FC):
            op("gpsimd", (lambda i_, w_, fc_: lambda e: e.dma_start(
                out_d.rearrange("(fc p) t -> p fc t", p=128)
                [:, fc_, TO[i_]:TO[i_] + w_],
                outb[:, fc_ * TILE:fc_ * TILE + w_]))(i, w, fc),
                waits=[W[f"Fout{i}"]], inc="do")
        W[f"Fo{i}"] = ev("do")

    emit_attn(0)
    emit_attn(1)
    for i in range(NT - 1):
        emit_ffn(i)
        if i + 2 < NT:
            emit_attn(i + 2)
    emit_ffn(NT - 1)

    if DBG:
        dump_list = [("pq", pq), ("costab", costab), ("mall", mall),
                     ("rball", rball), ("th64", th64), ("sc128", sc128),
                     ("c2col", c2col), ("convw", convw), ("x1", x1),
                     ("xqw", xqw), ("hsilu", hsilu), ("ybf", ybf),
                     ("rho1", rho1), ("negdsum", negdsum), ("thin", thin),
                     ("c2mc", c2mc), ("rmax", rmax), ("bcb", bcb),
                     ("halos", halos), ("haloq", haloq), ("negbsre", negbsre),
                     ("negbsim", negbsim), ("hsb", hsb), ("xsb", xsb),
                     ("snapA", snapA), ("snapB", snapB), ("snapU", snapU),
                     ("snapC", snapC), ("snapR", snapR), ("snapP", snapP)]
        pre_do = ("do", ctr["do"])
        for nm, t_ in dump_list:
            shp = list(t_.ap().shape) if hasattr(t_, "ap") else list(t_.shape)
            dt_ = t_.ap().dtype if hasattr(t_, "ap") else t_.dtype
            dd = nc.dram_tensor("dump_" + nm, shp, dt_, kind="ExternalOutput").ap()
            op("gpsimd", (lambda d_, a_: lambda e: e.dma_start(d_, a_))(
                dd, t_.ap() if hasattr(t_, "ap") else t_), waits=[pre_do],
                inc="do")

    final_do = ("do", ctr["do"])
    op("gpsimd", lambda e: e.nop(), waits=[final_do])
    op("sync", lambda e: e.nop(), waits=[final_do])

    do_emit()
    return nc


def build_inmaps(L0_attn_u, L0_attn_log_A_real, L0_attn_log_A_imag, L0_attn_B_w,
                 L0_attn_C_w, L0_attn_D_w, L0_ffn_w1, L0_ffn_conv_w,
                 L0_ffn_conv_b, L0_ffn_w2):
    x = np.ascontiguousarray(np.asarray(L0_attn_u, dtype=np.float32))
    bw = np.asarray(L0_attn_B_w, np.float32)
    cw = np.asarray(L0_attn_C_w, np.float32)
    dw = np.asarray(L0_attn_D_w, np.float32)
    w1 = np.asarray(L0_ffn_w1, np.float32)
    w2 = np.asarray(L0_ffn_w2, np.float32)
    cvw_ = np.asarray(L0_ffn_conv_w, np.float32).reshape(H, 3)
    cvb_ = np.asarray(L0_ffn_conv_b, np.float32)

    cvpack = np.zeros((128, 128), np.float32)
    for tap in range(3):
        cvpack[:, tap * 32:(tap + 1) * 32] = cvw_[:, tap].reshape(HC, 128).T
    cvpack[:, 96:128] = cvb_.reshape(HC, 128).T

    tpos_h = np.ascontiguousarray(
        np.broadcast_to(np.arange(NTOK + 2, dtype=np.float32),
                        (NSTATE, NTOK + 2)))
    bwT_h = np.ascontiguousarray(bw.T)
    cwT_h = np.ascontiguousarray(cw.T)
    dwT_h = np.ascontiguousarray(dw.T)
    w1T_h = np.ascontiguousarray(w1.T)
    w2T_h = np.ascontiguousarray(w2.T)
    lar_h = np.asarray(L0_attn_log_A_real, np.float32).reshape(NSTATE, 1)
    lai_h = np.asarray(L0_attn_log_A_imag, np.float32).reshape(NSTATE, 1)
    eye_f = np.eye(128, dtype=np.float32)
    eye_b = np.eye(128).astype(ml_dtypes.bfloat16)

    in_maps = []
    for core in range(NCORE):
        b = core // 2
        half = core % 2
        s0 = half * NTOK
        xm = np.zeros((D, NTOK + 2), np.float32)
        lo, hi = s0 - 1, s0 + NTOK + 1
        src_lo, src_hi = max(lo, 0), min(hi, S)
        xm[:, src_lo - lo: src_hi - lo] = x[b, src_lo:src_hi, :].T
        if half == 1:
            xp = np.ascontiguousarray(x[b, 0:NTOK, :].T)
        else:
            xp = np.zeros((D, NTOK), np.float32)
        mr = np.full((128, 1), 1.0 if half == 0 else 0.0, np.float32)
        in_maps.append(dict(
            xmain=xm, xpre=xp, tpos=tpos_h, maskr=mr, lar=lar_h, lai=lai_h,
            bwT=bwT_h, cwT=cwT_h, dwT=dwT_h, w1T=w1T_h, w2T=w2T_h, cvw=cvpack,
            idn=eye_b, idnf32=eye_f,
        ))
    return in_maps


def kernel(**inputs):
    in_maps = build_inmaps(**inputs)
    nc = build_nc()
    res = run_bass_kernel_spmd(nc, in_maps, core_ids=list(range(NCORE)))
    out = np.empty((B, S, D), np.float32)
    for core in range(NCORE):
        b = core // 2
        half = core % 2
        s0 = half * NTOK
        out[b, s0:s0 + NTOK, :] = np.asarray(res.results[core]["out"]).T
    return out
